# version 03
# baseline (speedup 1.0000x reference)
import numpy as np
from contextlib import ExitStack

import concourse.bass as bass
import concourse.mybir as mybir

F32 = mybir.dt.float32
BF16 = mybir.dt.bfloat16
AF = mybir.ActivationFunctionType
ALU = mybir.AluOpType
AX = mybir.AxisListType

ENGINES = ("sync", "scalar", "vector", "tensor", "gpsimd")

S, H = 2048, 2048
HK, HV, DK, DV = 16, 32, 128, 128
NEG = -1e30


class Sem:
    __slots__ = ("h", "n", "name")

    def __init__(self, h, name):
        self.h = h
        self.n = 0
        self.name = name


class Ctx:
    def __init__(self, name="mk"):
        self.nc = bass.Bass("TRN2", target_bir_lowering=False)
        self.es = ExitStack()
        self.q = {e: [] for e in ENGINES}
        self._semcnt = 0
        self._waited = {e: {} for e in ENGINES}

    def sbuf(self, name, shape, dt=F32, scope=None):
        return (scope or self.es).enter_context(self.nc.sbuf_tensor(name, shape, dt))

    def psum(self, name, shape, dt=F32, scope=None):
        return (scope or self.es).enter_context(self.nc.psum_tensor(name, shape, dt))

    def sem(self, name=None):
        self._semcnt += 1
        h = self.es.enter_context(self.nc.semaphore(name or f"s{self._semcnt}"))
        return Sem(h, name or f"s{self._semcnt}")

    def dram(self, name, shape, dt=F32, kind="Internal"):
        return self.nc.dram_tensor(name, shape, dt, kind=kind).ap()

    def op(self, eng, fn, inc=None, k=1):
        if inc is not None:
            inc.n += k

            def g(e, fn=fn, inc=inc, k=k):
                fn(e).then_inc(inc.h, k)
        else:
            def g(e, fn=fn):
                fn(e)
        self.q[eng].append(g)
        return inc.n if inc is not None else None

    def dma(self, eng, out, in_, inc=None):
        return self.op(eng, lambda e, out=out, in_=in_: e.dma_start(out=out, in_=in_),
                       inc=inc, k=16)

    def wait(self, eng, sem, val=None):
        v = sem.n if val is None else val
        w = self._waited[eng]
        if w.get(id(sem), -1) >= v:
            return v
        w[id(sem)] = v
        self.q[eng].append(lambda e, s=sem.h, v=v: e.wait_ge(s, v))
        return v

    def finalize(self):
        nc = self.nc
        with nc.Block() as blk:
            for eng in ENGINES:
                fns = self.q[eng]
                getattr(blk, eng)(lambda e, fns=fns: [f(e) for f in fns])
        self.es.close()
        return nc



def perm_qkvz_cols():
    idx = []
    for h in range(HK):
        idx.append(np.arange(768 * h, 768 * h + 128))
    for h in range(HK):
        idx.append(np.arange(768 * h + 128, 768 * h + 256))
    for h in range(HK):
        idx.append(np.arange(768 * h + 256, 768 * h + 512))
    for h in range(HK):
        idx.append(np.arange(768 * h + 512, 768 * h + 768))
    return np.concatenate(idx)


def perm_ba_cols():
    beta_cols = [4 * h + j for h in range(HK) for j in range(2)]
    a_cols = [4 * h + 2 + j for h in range(HK) for j in range(2)]
    return np.array(beta_cols + a_cols)


def tile_km(w, nk, nm):
    return np.ascontiguousarray(
        w.reshape(nk, 128, nm, 128).transpose(2, 0, 1, 3))


def tile_km2(w, nk, nm):
    return np.ascontiguousarray(
        w.reshape(nk, 128, nm, 128).transpose(2, 1, 0, 3))


def host_prep(inputs):
    d = {}
    wq = np.asarray(inputs["L0_attn_w_qkvz"])
    d["wq"] = tile_km(np.ascontiguousarray(wq[:, perm_qkvz_cols()]), 16, 96)
    wba = np.asarray(inputs["L0_attn_w_ba"])
    d["wba"] = np.ascontiguousarray(
        wba[:, perm_ba_cols()].reshape(16, 128, 64).transpose(1, 0, 2).reshape(128, 1024))
    d["wo"] = tile_km(np.asarray(inputs["L0_attn_w_o"]), 32, 16)
    w1 = np.asarray(inputs["L0_ffn_w1"])
    d["w1t"] = tile_km2(np.ascontiguousarray(w1.T), 16, 64)
    w2 = np.asarray(inputs["L0_ffn_w2"])
    d["w2t"] = tile_km2(np.ascontiguousarray(w2.T), 64, 16)
    cw = np.asarray(inputs["L0_attn_conv_w"])
    d["cw"] = np.ascontiguousarray(
        cw.reshape(64, 128, 4).transpose(1, 0, 2).reshape(128, 256))
    d["cb"] = np.ascontiguousarray(
        np.asarray(inputs["L0_attn_conv_b"]).reshape(64, 128).T)
    d["alog"] = np.ascontiguousarray(np.asarray(inputs["L0_attn_a_log"]).reshape(32, 1))
    d["dtb"] = np.ascontiguousarray(np.asarray(inputs["L0_attn_dt_bias"]).reshape(32, 1))
    d["normw"] = np.ascontiguousarray(np.asarray(inputs["L0_attn_norm_w"]).reshape(128, 1))
    d["b1"] = np.ascontiguousarray(np.asarray(inputs["L0_ffn_b1"]).reshape(64, 128).T)
    d["b2"] = np.ascontiguousarray(np.asarray(inputs["L0_ffn_b2"]).reshape(16, 128).T)
    import ml_dtypes
    d["ident"] = np.eye(128, dtype=ml_dtypes.bfloat16)
    d["identf"] = np.eye(128, dtype=np.float32)
    d["onesf"] = np.ones((128, 128), dtype=np.float32)
    d["onesb"] = np.ones((128, 128), dtype=ml_dtypes.bfloat16)
    oh = np.zeros((32, 32, 128), dtype=np.float32)
    for h in range(32):
        oh[h, h, :] = 1.0
    d["onehot"] = np.ascontiguousarray(oh.transpose(1, 0, 2).reshape(32, 32 * 128))
    d["onehotb"] = d["onehot"].astype(ml_dtypes.bfloat16)
    co = np.zeros((32, 128, 32), dtype=ml_dtypes.bfloat16)
    for m in range(32):
        co[m, :, m] = 1.0
    d["colones"] = np.ascontiguousarray(co.transpose(1, 0, 2).reshape(128, 32 * 32))
    i, j = np.indices((128, 128))
    d["mask_strict"] = np.where(j >= i, np.float32(1e30), np.float32(0.0))
    d["mask_inclT"] = np.where(j < i, np.float32(1e30), np.float32(0.0))
    return d



def emit_p0(ctx: Ctx, w_in, w_out, ntiles, tag):
    wt = w_in.rearrange("(n p) f -> n p f", p=128)
    wo = w_out.rearrange("(n p) f -> n p f", p=128)
    NB = 4
    bufs = [ctx.sbuf(f"{tag}_in{i}", [128, 2048], F32) for i in range(NB)]
    tern = [ctx.sbuf(f"{tag}_t{i}", [128, 2048], BF16) for i in range(2)]
    ge = ctx.sbuf(f"{tag}_ge", [128, 2048], BF16)
    le = ctx.sbuf(f"{tag}_le", [128, 2048], BF16)
    acc = ctx.sbuf(f"{tag}_acc", [128, ntiles], F32)
    red = ctx.sbuf(f"{tag}_red", [128, 1], F32)
    thr = ctx.sbuf(f"{tag}_thr", [128, 1], F32)
    nthr = ctx.sbuf(f"{tag}_nthr", [128, 1], F32)

    lds = [ctx.sem(f"{tag}ld{i}") for i in range(NB)]
    sts = [ctx.sem(f"{tag}st{i}") for i in range(2)]
    rd = ctx.sem(f"{tag}rd")
    th = ctx.sem(f"{tag}th")

    for i in range(ntiles):
        s = i % NB
        b = bufs[s]
        if i >= NB:
            ctx.wait("sync", rd, i - NB + 1)
        ctx.dma("sync", b[:], wt[i], inc=lds[s])
        ctx.wait("vector", lds[s], 16 * (i // NB + 1))
        ctx.op("vector",
               lambda e, b=b, i=i: e.tensor_reduce(
                   acc[:, i:i + 1], b[:], AX.X, ALU.add,
                   apply_absolute_value=True),
               inc=rd)
    vsem = ctx.sem(f"{tag}vs")
    ctx.wait("vector", rd, ntiles)
    ctx.op("vector", lambda e: e.tensor_reduce(red[:], acc[:, 0:ntiles], AX.X, ALU.add),
           inc=vsem)
    onesf = ctx.consts["onesf"]
    pab = ctx.psum(f"{tag}_pab", [128, 1024], F32)
    pa = pab[0:1, 0:1]
    pb = pab[:, 512:513]
    s1 = ctx.sbuf(f"{tag}_s1", [1, 1], F32)
    msem = ctx.sem(f"{tag}ms")
    ctx.wait("tensor", vsem, 1)
    ctx.op("tensor", lambda e: e.matmul(pa, onesf[:, 0:1], red[:], start=True, stop=True), inc=msem)
    ctx.wait("vector", msem, 1)
    ctx.op("vector", lambda e: e.tensor_copy(s1[:], pa), inc=vsem)
    ctx.wait("tensor", vsem, 2)
    ctx.op("tensor", lambda e: e.matmul(pb, onesf[0:1, :], s1[:], start=True, stop=True), inc=msem)
    nel = float(ntiles * 128 * 2048)
    ctx.wait("vector", msem, 2)
    ctx.op("vector", lambda e: e.tensor_scalar(thr[:], pb, 0.5 / nel, 0.5e-8, ALU.mult, ALU.max), inc=th)
    ctx.wait("scalar", th, 1)
    ctx.op("scalar", lambda e: e.mul(nthr[:], thr[:], -1.0), inc=th)
    ctx.wait("gpsimd", th, 2)

    for i in range(ntiles):
        s = i % NB
        b = bufs[s]
        ctx.wait("sync", rd, ntiles + max(0, i - NB + 1))
        ctx.dma("sync", b[:], wt[i], inc=lds[s])
        t = tern[i % 2]
        ctx.wait("gpsimd", lds[s], 16 * ((ntiles + i) // NB + 1))
        if i >= 2:
            ctx.wait("gpsimd", sts[i % 2], 16 * ((i - 2) // 2 + 1))
        ctx.op("gpsimd", lambda e, b=b: e.tensor_single_scalar(ge[:], b[:], thr[:], ALU.is_ge))
        ctx.op("gpsimd", lambda e, b=b: e.tensor_single_scalar(le[:], b[:], nthr[:], ALU.is_le))
        ctx.op("gpsimd", lambda e, t=t: e.tensor_tensor(t[:], ge[:], le[:], ALU.subtract), inc=rd)
        ctx.wait("scalar", rd, ntiles + i + 1)
        ctx.dma("scalar", wo[i], t[:], inc=sts[i % 2])
    ctx._dbg_thr = thr
    ctx._dbg_acc = acc
    ctx._dbg_red = red
    ctx._dbg_s1 = s1
    ctx._dbg_ge = ge
    ctx._dbg_le = le
    return sts


def prepare_cvals(ctx: Ctx, values):
    tile = ctx.sbuf("cvals", [128, max(1, len(values))], F32)
    sem = ctx.sem("cvals")
    for i, v in enumerate(values):
        ctx.op("gpsimd", lambda e, i=i, v=v: e.memset(tile[:, i:i + 1], float(v)),
               inc=sem)
    for eng in ("vector", "scalar", "tensor", "gpsimd"):
        ctx.wait(eng, sem, len(values))
    ctx.cvals = {float(v): i for i, v in enumerate(values)}
    ctx.cvtile = tile


def cv(ctx, v, p=128):
    return ctx.cvtile[0:p, ctx.cvals[float(v)]:ctx.cvals[float(v)] + 1]


def load_consts(ctx: Ctx, dram_consts):
    ctx.consts = {}
    csem = ctx.sem("consts")
    gsem = ctx.sem("constg")
    n = g = 0
    for name, spec in dram_consts.items():
        ap, shape, dt = spec[0], spec[1], spec[2]
        cast = len(spec) > 3 and spec[3] == "cast"
        t = ctx.sbuf(f"c_{name}", list(shape), dt)
        if cast:
            ctx.dma("gpsimd", t[:], ap, inc=gsem)
            g += 16
        else:
            ctx.dma("sync", t[:], ap, inc=csem)
            n += 16
        ctx.consts[name] = t
    for eng in ("vector", "scalar", "tensor", "gpsimd", "sync"):
        if n:
            ctx.wait(eng, csem, n)
        if g:
            ctx.wait(eng, gsem, g)
    return csem



def wt(ctx, eng, rec):
    if rec is None:
        return
    if isinstance(rec, list):
        for r in rec:
            wt(ctx, eng, r)
        return
    ctx.wait(eng, rec[0], rec[1])


def emit_p1(ctx, x, wq, wba, x_fm, q_fm, k_fm, k_tm, v_tm, z_fm,
            beta_o, cumg_o, qc_scr):
    identb = ctx.consts["ident"]
    wba_s = ctx.consts["wba"]

    xt = x.rearrange("(n p) d -> n p d", p=128)
    xn_all = ctx.sbuf("p1xn", [128, 16 * 2048], BF16)
    es_a = ExitStack()
    xbufs = [ctx.sbuf(f"p1x{i}", [128, 2048], F32, scope=es_a) for i in range(2)]
    xlds = [ctx.sem(f"p1xld{i}") for i in range(2)]
    xstage = [ctx.sbuf(f"p1xs{i}", [128, 2048], BF16, scope=es_a) for i in range(2)]
    xnbf = ctx.sbuf("p1xnbf", [128, 2048], BF16, scope=es_a)
    xsq = ctx.sbuf("p1xsq", [128, 2048], F32, scope=es_a)
    xbf = ctx.sbuf("p1xbf", [128, 2048], BF16, scope=es_a)
    stats = ctx.sbuf("p1st", [128, 8], F32, scope=es_a)
    ptr = [es_a.enter_context(ctx.nc.psum_tensor(f"p1pt{i}", [128, 1024], BF16))
           for i in range(4)]
    vch = ctx.sem("p1vch")
    ach = ctx.sem("p1ach")
    tch = ctx.sem("p1tch")
    xnr = ctx.sem("p1xnr")
    xst = [ctx.sem(f"p1xst{i}") for i in range(2)]
    rel_x = [None, None]
    rel_pt = [None] * 4
    rel_xbf = None
    rel_xnbf = None
    rel_stage = [None, None]
    for tt in range(16):
        sl = tt % 2
        b = xbufs[sl]
        wt(ctx, "sync", rel_x[sl])
        ctx.dma("sync", b[:], xt[tt], inc=xlds[sl])
        ctx.wait("vector", xlds[sl], 16 * (tt // 2 + 1))
        ctx.wait("scalar", xlds[sl], 16 * (tt // 2 + 1))
        wt(ctx, "scalar", rel_xbf)
        ctx.op("scalar", lambda e, b=b: e.activation(xbf[:], b[:], AF.Identity), inc=ach)
        xbf_ready = ("scalar", ach.n)
        ctx.op("vector", lambda e, b=b: e.tensor_reduce(stats[:, 0:1], b[:], AX.X, ALU.add), inc=vch)
        ctx.op("vector", lambda e, b=b: e.tensor_tensor(xsq[:], b[:], b[:], ALU.mult), inc=vch)
        ctx.wait("vector", vch, vch.n)
        ctx.op("vector", lambda e: e.tensor_reduce(stats[:, 1:2], xsq[:], AX.X, ALU.add), inc=vch)
        ctx.wait("vector", vch, vch.n)
        ctx.op("vector", lambda e: e.tensor_scalar(stats[:, 2:3], stats[:, 0:1], 1.0 / 2048, None, ALU.mult), inc=vch)
        ctx.op("vector", lambda e: e.tensor_scalar(stats[:, 3:4], stats[:, 1:2], 1.0 / 2048, None, ALU.mult), inc=vch)
        ctx.wait("vector", vch, vch.n)
        ctx.op("vector", lambda e: e.tensor_tensor(stats[:, 4:5], stats[:, 2:3], stats[:, 2:3], ALU.mult), inc=vch)
        ctx.wait("vector", vch, vch.n)
        ctx.op("vector", lambda e: e.tensor_tensor(stats[:, 5:6], stats[:, 3:4], stats[:, 4:5], ALU.subtract), inc=vch)
        ctx.wait("scalar", vch, vch.n)
        ctx.op("scalar", lambda e: e.activation(stats[:, 6:7], stats[:, 5:6], AF.Sqrt, bias=cv(ctx, 1e-5)), inc=ach)
        ctx.wait("vector", ach, ach.n)
        ctx.op("vector", lambda e: e.reciprocal(stats[:, 7:8], stats[:, 6:7]), inc=vch)
        ctx.wait("vector", vch, vch.n)
        wt(ctx, "vector", rel_xnbf)
        ctx.op("vector", lambda e, b=b: e.tensor_scalar(
            xnbf[:], b[:], stats[:, 2:3], stats[:, 7:8], ALU.subtract, ALU.mult), inc=vch)
        rel_x[sl] = [(vch, vch.n), (ach, xbf_ready[1])]
        xn_ready = vch.n
        ctx.wait("tensor", ach, xbf_ready[1])
        wt(ctx, "vector", rel_stage[sl])
        for g in range(2):
            pt = ptr[g]
            wt(ctx, "tensor", rel_pt[g])
            for c8 in range(8):
                c = g * 8 + c8
                ctx.op("tensor", lambda e, pt=pt, c8=c8, c=c: e.transpose(
                    pt[:, c8 * 128:(c8 + 1) * 128], xbf[:, c * 128:(c + 1) * 128],
                    identb[:]), inc=(tch if c8 == 7 else None))
            ctx.wait("vector", tch, tch.n)
            ctx.op("vector", lambda e, pt=pt, g=g, sl=sl: e.tensor_copy(
                xstage[sl][:, g * 1024:(g + 1) * 1024], pt[:]), inc=vch)
            rel_pt[g] = (vch, vch.n)
        rel_xbf = (tch, tch.n)
        ctx.wait("tensor", vch, xn_ready)
        for g in range(2):
            pt = ptr[2 + g]
            wt(ctx, "tensor", rel_pt[2 + g])
            for c8 in range(8):
                c = g * 8 + c8
                ctx.op("tensor", lambda e, pt=pt, c8=c8, c=c: e.transpose(
                    pt[:, c8 * 128:(c8 + 1) * 128], xnbf[:, c * 128:(c + 1) * 128],
                    identb[:]), inc=(tch if c8 == 7 else None))
            ctx.wait("vector", tch, tch.n)
            def cpxn(e, pt=pt, g=g, tt=tt):
                dst = xn_all[:].rearrange("p (c t) -> p c t", c=16)[
                    :, g * 8:(g + 1) * 8, tt * 128:(tt + 1) * 128]
                return e.tensor_copy(dst, pt[:].rearrange("p (c t) -> p c t", c=8))
            ctx.op("vector", cpxn, inc=vch)
            ctx.op("vector", lambda e: None, inc=None)
            ctx.q["vector"].pop()
            rel_pt[2 + g] = (vch, vch.n)
            xnr.n += 0
        rel_xnbf = (tch, tch.n)
        ctx.wait("scalar", vch, vch.n)
        def st_x(e, sl=sl, tt=tt):
            dst = x_fm.rearrange("(c p) t -> p c t", p=128)[:, :, tt * 128:(tt + 1) * 128]
            return e.dma_start(out=dst, in_=xstage[sl][:].rearrange("p (c t) -> p c t", c=16))
        ctx.op("scalar", st_x, inc=xst[sl], k=16)
        rel_stage[sl] = (xst[sl], 16 * (tt // 2 + 1))
    xn_done_rec = (vch, vch.n)

    es_a.close()
    p1a_done = [xn_done_rec, (ach, ach.n), (tch, tch.n),
                (xst[0], xst[0].n), (xst[1], xst[1].n)]
    mainA = ctx.psum("p1mA", [128, 1024], F32)
    mainB = ctx.psum("p1mB", [128, 1024], F32)
    es_stats = ExitStack()
    pstat = es_stats.enter_context(ctx.nc.psum_tensor("p1ps", [32, 2048], F32))
    pmm = ctx.sem("p1pmm")
    alog = ctx.consts["alog"]
    dtb = ctx.consts["dtb"]
    es_ba = ExitStack()
    wt(ctx, "tensor", p1a_done)
    for k in range(16):
        for n in range(4):
            pb = (mainA if n < 2 else mainB)[0:64, (n % 2) * 512:(n % 2) * 512 + 512]
            ctx.op("tensor", lambda e, pb=pb, k=k, n=n: e.matmul(
                pb, wba_s[:, k * 64:(k + 1) * 64],
                xn_all[:, k * 2048 + n * 512: k * 2048 + (n + 1) * 512],
                start=(k == 0), stop=(k == 15)),
                inc=(pmm if (k == 15 and n == 3) else None))
    ba_f = ctx.sbuf("p1baf", [64, 2048], F32, scope=es_ba)
    ctx.wait("scalar", pmm, pmm.n)
    for n in range(4):
        pb = (mainA if n < 2 else mainB)[0:64, (n % 2) * 512:(n % 2) * 512 + 512]
        ctx.op("scalar", lambda e, pb=pb, n=n: e.copy(ba_f[:, n * 512:(n + 1) * 512], pb),
               inc=ach)
    rel_mainA = ("x", ach, ach.n)[1:]
    rel_mainB = (ach, ach.n)
    bsb = ctx.sbuf("p1beta", [32, 2048], F32, scope=es_ba)
    ctx.wait("scalar", ach, ach.n)
    ctx.op("scalar", lambda e: e.activation(bsb[:], ba_f[0:32, :], AF.Sigmoid), inc=ach)
    beta_rec = (ach, ach.n)
    a_sb = ctx.sbuf("p1a", [32, 2048], F32, scope=es_ba)
    t_sb = ctx.sbuf("p1t", [32, 2048], F32, scope=es_ba)
    s_sb = a_sb
    gsb = ba_f[0:32, :]
    cgsb = ba_f[32:64, :]
    ones32 = ctx.sbuf("p1o32", [32, 2048], F32, scope=es_ba)
    nea = ctx.sbuf("p1nea", [32, 1], F32, scope=es_ba)
    ones_rdy = ctx.sem("p1or")
    wt(ctx, "gpsimd", p1a_done)
    ctx.op("gpsimd", lambda e: e.memset(ones32[:], 1.0), inc=ones_rdy)
    ctx.op("scalar", lambda e: e.activation(nea[:], alog[:], AF.Exp), inc=ach)
    ctx.wait("vector", ach, ach.n)
    ctx.op("vector", lambda e: e.tensor_scalar(nea[:], nea[:], -1.0, None, ALU.mult), inc=vch)
    nea_rec = (vch, vch.n)
    ctx.op("vector", lambda e: e.tensor_copy(a_sb[:], ba_f[32:64, :]), inc=vch)
    ctx.wait("vector", vch, vch.n)
    ctx.op("vector", lambda e: e.tensor_scalar(s_sb[:], a_sb[:], dtb[:], None, ALU.add), inc=vch)
    ctx.wait("scalar", vch, vch.n)
    ctx.op("scalar", lambda e: e.activation(t_sb[:], s_sb[:], AF.Abs), inc=ach)
    ctx.wait("scalar", ach, ach.n)
    ctx.op("scalar", lambda e: e.activation(t_sb[:], t_sb[:], AF.Exp, scale=cv(ctx, -1.0, 32)), inc=ach)
    ctx.wait("scalar", ach, ach.n)
    ctx.op("scalar", lambda e: e.activation(t_sb[:], t_sb[:], AF.Ln, bias=cv(ctx, 1.0, 32)), inc=ach)
    wt(ctx, "vector", beta_rec)
    ctx.op("vector", lambda e: e.tensor_scalar(gsb[:], s_sb[:], 0.0, None, ALU.max), inc=vch)
    ctx.wait("vector", ach, ach.n)
    ctx.wait("vector", vch, vch.n)
    ctx.op("vector", lambda e: e.tensor_tensor(gsb[:], gsb[:], t_sb[:], ALU.add), inc=vch)
    ctx.wait("vector", vch, vch.n)
    wt(ctx, "vector", nea_rec)
    ctx.op("vector", lambda e: e.tensor_scalar(gsb[:], gsb[:], nea[:], None, ALU.mult), inc=vch)
    ctx.wait("vector", vch, vch.n)
    ctx.wait("vector", ones_rdy, 1)
    ctx.op("vector", lambda e: e.tensor_tensor_scan(
        cgsb[:], ones32[:], gsb[:], 0.0, ALU.mult, ALU.add), inc=vch)
    bgst = ctx.sem("p1bg")
    wt(ctx, "sync", beta_rec)
    ctx.wait("sync", vch, vch.n)
    ctx.dma("sync", beta_o, bsb[:], inc=bgst)
    ctx.dma("sync", cumg_o, cgsb[:], inc=bgst)
    if getattr(ctx, "dbg_g", None) is not None:
        ctx.dma("sync", ctx.dbg_g, gsb, inc=bgst)
        ctx.dma("sync", ctx.dbg_ones, ones32[:], inc=bgst)
    return dict(vch=vch, ach=ach, tch=tch, pmm=pmm,
                rel_mainA=rel_mainA, rel_mainB=rel_mainB,
                mainA=mainA, mainB=mainB, pstat=pstat, es_stats=es_stats,
                xn_all=xn_all, xn_done_rec=xn_done_rec, bgst=bgst, es_ba=es_ba)


def emit_p1b(ctx, h, wq, q_fm, k_fm, k_tm, v_tm, z_fm, qc_scr):
    identb = ctx.consts["ident"]
    onehotb = ctx.consts["onehotb"]
    colones = ctx.consts["colones"]
    cw = ctx.consts["cw"]
    cb = ctx.consts["cb"]
    vch, ach, tch, pmm = h["vch"], h["ach"], h["tch"], h["pmm"]
    mainA, mainB, pstat = h["mainA"], h["mainB"], h["pstat"]
    xn_all = h["xn_all"]

    h["es_ba"].close()
    wqt = wq.rearrange("(m k p) f -> m p k f", m=96, k=16)
    wbuf = [ctx.sbuf(f"p1w{i}", [128, 2048], BF16) for i in range(2)]
    wlds = [ctx.sem(f"p1wld{i}") for i in range(2)]
    qcf = ctx.sbuf("p1qcf", [128, 2048], BF16)
    accb = ctx.sbuf("p1acc", [128, 2048], BF16)
    qcb = [ctx.sbuf(f"p1qcb{i}", [128, 2048], BF16) for i in range(2)]
    sqb = [ctx.sbuf(f"p1sqb{i}", [128, 2048], BF16) for i in range(2)]
    vstage = ctx.sbuf("p1vst", [128, 2048], BF16)
    qcst = [ctx.sem(f"p1qst{i}") for i in range(2)]
    vtst = ctx.sem("p1vts")
    statdone = ctx.sem("p1sd")
    vt_evac = ctx.sem("p1vte")
    rstd_f = ctx.sbuf("p1rstd", [32, 2048], F32)
    rstd_b = ctx.sbuf("p1rstdb", [32, 2048], BF16)

    state = dict(rel_qc=[None, None], rel_sq=[None, None], rel_qcf=None,
                 rel_acc=None, rel_vtp=None, rel_vstage=None,
                 rel_w=[None, None], rel_mainA=h["rel_mainA"],
                 rel_mainB=h["rel_mainB"], rstd_rec=None, rstdb_rec=None)
    pe_post = []
    vtp_holder = {}

    def emit_rstd():
        ctx.wait("scalar", pmm, pmm.n)
        ctx.op("scalar", lambda e: e.activation(rstd_f[:], pstat[:], AF.Sqrt, bias=cv(ctx, 1e-6, 32)),
               inc=statdone)
        ctx.wait("vector", statdone, 1)
        ctx.op("vector", lambda e: e.reciprocal(rstd_f[:], rstd_f[:]), inc=vch)
        ctx.wait("vector", vch, vch.n)
        ctx.op("vector", lambda e: e.tensor_copy(rstd_b[:], rstd_f[:]), inc=vch)
        state["rstdb_rec"] = (vch, vch.n)

    for m in range(96):
        sl = m % 2
        wb = wbuf[sl]
        if m == 0:
            ctx.wait("gpsimd", h["bgst"], h["bgst"].n)
        wt(ctx, "gpsimd", state["rel_w"][sl])
        ctx.op("gpsimd", lambda e, wb=wb, m=m: e.dma_start(
            out=wb[:].rearrange("p (k f) -> p k f", k=16), in_=wqt[m]),
            inc=wlds[sl], k=16)
        ctx.wait("tensor", wlds[sl], 16 * (m // 2 + 1))
        if m == 0:
            wt(ctx, "tensor", h["xn_done_rec"])
        for half in range(2):
            pmain = mainA if half == 0 else mainB
            wt(ctx, "tensor", state["rel_mainA" if half == 0 else "rel_mainB"])
            for k in range(16):
                for n2 in range(2):
                    n = half * 2 + n2
                    ctx.op("tensor", lambda e, pmain=pmain, wb=wb, k=k, n2=n2, n=n: e.matmul(
                        pmain[:, n2 * 512:(n2 + 1) * 512],
                        wb[:, k * 128:(k + 1) * 128],
                        xn_all[:, k * 2048 + n * 512: k * 2048 + (n + 1) * 512],
                        start=(k == 0), stop=(k == 15)),
                        inc=(pmm if (k == 15 and n2 == 1) else None))
        state["rel_w"][sl] = (pmm, pmm.n)
        mm_half = [pmm.n - 1, pmm.n]
        for fn in pe_post:
            fn()
        pe_post = []
        if m == 32:
            emit_rstd()
            h["es_stats"].close()
            vtp_holder["vtp"] = ctx.psum("p1vtp", [128, 2048], BF16)

        if m < 64:
            qc = qcb[sl]
            ctx.wait("scalar", pmm, mm_half[0])
            wt(ctx, "scalar", state["rel_qcf"])
            ctx.op("scalar", lambda e: e.copy(qcf[:, 0:1024], mainA[:]), inc=ach)
            state["rel_mainA"] = (ach, ach.n)
            ctx.wait("scalar", pmm, mm_half[1])
            ctx.op("scalar", lambda e: e.copy(qcf[:, 1024:2048], mainB[:]), inc=ach)
            state["rel_mainB"] = (ach, ach.n)
            qcf_rdy = ach.n
            ctx.wait("vector", ach, qcf_rdy)
            wt(ctx, "vector", state["rel_acc"])
            cwm = [cw[:, m * 4 + j: m * 4 + j + 1] for j in (3, 2, 1, 0)]
            ctx.op("vector", lambda e, c=cwm[0]: e.tensor_scalar(accb[:], qcf[:], c, None, ALU.mult), inc=vch)
            ctx.op("vector", lambda e, c=cwm[1]: e.scalar_tensor_tensor(
                accb[:, 1:2048], qcf[:, 0:2047], c, accb[:, 1:2048], ALU.mult, ALU.add), inc=vch)
            ctx.op("vector", lambda e, c=cwm[2]: e.scalar_tensor_tensor(
                accb[:, 2:2048], qcf[:, 0:2046], c, accb[:, 2:2048], ALU.mult, ALU.add), inc=vch)
            ctx.op("vector", lambda e, c=cwm[3]: e.scalar_tensor_tensor(
                accb[:, 3:2048], qcf[:, 0:2045], c, accb[:, 3:2048], ALU.mult, ALU.add), inc=vch)
            state["rel_qcf"] = (vch, vch.n)
            ctx.wait("scalar", vch, vch.n)
            wt(ctx, "scalar", state["rel_qc"][sl])
            ctx.op("scalar", lambda e, qc=qc, m=m: e.activation(
                qc[:], accb[:], AF.Silu, bias=cb[:, m:m + 1]), inc=ach)
            state["rel_acc"] = (ach, ach.n)
            silu_rdy = ach.n
            if m < 32:
                sq = sqb[sl]
                ctx.wait("sync", ach, silu_rdy)
                ctx.dma("sync", qc_scr.rearrange("(m p) t -> m p t", p=128)[m], qc[:],
                        inc=qcst[sl])
                wt(ctx, "scalar", state["rel_sq"][sl])
                ctx.op("scalar", lambda e, qc=qc, sq=sq: e.activation(sq[:], qc[:], AF.Square),
                       inc=ach)
                sq_rdy = ach.n
                state["rel_qc"][sl] = [(qcst[sl], 16 * ((m // 2) + 1)), (ach, sq_rdy)]

                def stats_mms(m=m, sq=sq, sl=sl, sq_rdy=sq_rdy):
                    ctx.wait("tensor", ach, sq_rdy)
                    for n in range(4):
                        ctx.op("tensor", lambda e, m=m, sq=sq, n=n: e.matmul(
                            pstat[:, n * 512:(n + 1) * 512],
                            colones[:, m * 32:(m + 1) * 32],
                            sq[:, n * 512:(n + 1) * 512],
                            start=(m == 0), stop=(m == 31)),
                            inc=(pmm if n == 3 else None))
                    state["rel_sq"][sl] = (pmm, pmm.n)
                pe_post.append(stats_mms)
            else:
                vh = m - 32

                def v_post(m=m, vh=vh, silu_rdy=silu_rdy, qc=qc, sl=sl):
                    vtp = vtp_holder["vtp"]
                    ctx.wait("tensor", ach, silu_rdy)
                    ctx.wait("tensor", statdone, 1)
                    wt(ctx, "tensor", state["rel_vtp"])
                    for tc in range(16):
                        ctx.op("tensor", lambda e, tc=tc, qc=qc, vtp=vtp: e.transpose(
                            vtp[:, tc * 128:(tc + 1) * 128],
                            qc[:, tc * 128:(tc + 1) * 128], identb[:]),
                            inc=(tch if tc == 15 else None))
                    state["rel_qc"][sl] = (tch, tch.n)
                    ctx.wait("vector", tch, tch.n)
                    wt(ctx, "vector", state["rel_vstage"])
                    ctx.op("vector", lambda e, vtp=vtp: e.tensor_copy(vstage[:], vtp[:]),
                           inc=vt_evac)
                    state["rel_vtp"] = (vt_evac, vt_evac.n)
                    ctx.wait("sync", vt_evac, vt_evac.n)

                    def st_v(e, vh=vh):
                        dst = v_tm.rearrange("(tc v t) d -> v t tc d", tc=16, v=32, t=128)[vh]
                        return e.dma_start(
                            out=dst, in_=vstage[:].rearrange("p (tc d) -> p tc d", tc=16))
                    ctx.op("sync", st_v, inc=vtst, k=16)
                    state["rel_vstage"] = (vtst, vtst.n)
                pe_post.append(v_post)
        else:
            zh = m - 64
            qc = qcb[sl]
            ctx.wait("scalar", pmm, mm_half[0])
            wt(ctx, "scalar", state["rel_qc"][sl])
            ctx.op("scalar", lambda e, qc=qc: e.copy(qc[:, 0:1024], mainA[:]), inc=ach)
            state["rel_mainA"] = (ach, ach.n)
            ctx.wait("scalar", pmm, mm_half[1])
            ctx.op("scalar", lambda e, qc=qc: e.copy(qc[:, 1024:2048], mainB[:]), inc=ach)
            state["rel_mainB"] = (ach, ach.n)
            ctx.wait("sync", ach, ach.n)
            ctx.dma("sync", z_fm.rearrange("(z p) t -> z p t", p=128)[zh], qc[:],
                    inc=qcst[sl])
            state["rel_qc"][sl] = (qcst[sl], 16 * (16 + zh // 2 + 1))
    for fn in pe_post:
        fn()

    qa = [ctx.sbuf(f"p1qa{i}", [128, 2048], BF16) for i in range(2)]
    qh = [ctx.sbuf(f"p1qh{i}", [128, 2048], BF16) for i in range(2)]
    kst = ctx.sem("p1kst")
    qlds = [ctx.sem(f"p1qld{i}") for i in range(2)]
    qsts = [ctx.sem(f"p1qsts{i}") for i in range(2)]
    rel_qa = [None, None]
    rel_qh = [None, None]
    DKs = float(DK) ** -0.5
    vtp = vtp_holder["vtp"]
    for m in range(32):
        sl = m % 2
        a = qa[sl]
        qout = qh[sl]
        wt(ctx, "sync", rel_qa[sl])
        ctx.dma("sync", a[:], qc_scr.rearrange("(m p) t -> m p t", p=128)[m], inc=qlds[sl])
        ctx.wait("tensor", qlds[sl], 16 * (m // 2 + 1))
        wt(ctx, "tensor", state["rstdb_rec"])
        for n in range(4):
            pmain = mainA if n < 2 else mainB
            wt(ctx, "tensor", state["rel_mainA" if n < 2 else "rel_mainB"])
            ctx.op("tensor", lambda e, pmain=pmain, n=n, m=m: e.matmul(
                pmain[:, (n % 2) * 512:(n % 2) * 512 + 512],
                onehotb[:, m * 128:(m + 1) * 128],
                rstd_b[:, n * 512:(n + 1) * 512],
                start=True, stop=True), inc=(pmm if n == 3 else None))
        ctx.wait("vector", pmm, pmm.n)
        wt(ctx, "vector", rel_qh[sl])
        cmul = DKs if m < 16 else 1.0
        for n in range(4):
            pmain = mainA if n < 2 else mainB
            ctx.op("vector", lambda e, pmain=pmain, n=n, a=a, qout=qout, cmul=cmul:
                   e.scalar_tensor_tensor(
                       qout[:, n * 512:(n + 1) * 512], a[:, n * 512:(n + 1) * 512],
                       cmul, pmain[:, (n % 2) * 512:(n % 2) * 512 + 512],
                       ALU.mult, ALU.mult), inc=vch)
        rel_qa[sl] = (vch, vch.n)
        state["rel_mainA"] = (vch, vch.n)
        state["rel_mainB"] = (vch, vch.n)
        ctx.wait("sync", vch, vch.n)
        if m < 16:
            ctx.dma("sync", q_fm.rearrange("(h p) t -> h p t", p=128)[m], qout[:],
                    inc=qsts[sl])
            rel_qh[sl] = (qsts[sl], 16 * (m // 2 + 1))
        else:
            kh = m - 16
            ctx.dma("sync", k_fm.rearrange("(h p) t -> h p t", p=128)[kh], qout[:],
                    inc=qsts[sl])
            ctx.wait("tensor", vch, vch.n)
            wt(ctx, "tensor", state["rel_vtp"])
            for tc in range(16):
                ctx.op("tensor", lambda e, tc=tc, qout=qout: e.transpose(
                    vtp[:, tc * 128:(tc + 1) * 128], qout[:, tc * 128:(tc + 1) * 128],
                    identb[:]), inc=(tch if tc == 15 else None))
            rel_qh[sl] = [(qsts[sl], 16 * (m // 2 + 1)), (tch, tch.n)]
            ctx.wait("vector", tch, tch.n)
            wt(ctx, "vector", state["rel_vstage"])
            ctx.op("vector", lambda e: e.tensor_copy(vstage[:], vtp[:]), inc=vt_evac)
            state["rel_vtp"] = (vt_evac, vt_evac.n)
            ctx.wait("scalar", vt_evac, vt_evac.n)

            def st_k(e, kh=kh):
                dst = k_tm.rearrange("(tc h t) d -> h t tc d", tc=16, h=16, t=128)[kh]
                return e.dma_start(out=dst,
                                   in_=vstage[:].rearrange("p (tc d) -> p tc d", tc=16))
            ctx.op("scalar", st_k, inc=kst, k=16)
            state["rel_vstage"] = (kst, kst.n)
    return dict(qsts=qsts, kst=kst, vtst=vtst, qcst=qcst,
                done=[(qsts[0], qsts[0].n), (qsts[1], qsts[1].n),
                      (kst, kst.n), (vtst, vtst.n),
                      (qcst[0], qcst[0].n), (qcst[1], qcst[1].n),
                      (h["bgst"], h["bgst"].n)])



def emit_p2(ctx, q_fm, k_fm, k_tm, v_tm, z_fm, beta_i, cumg_i, o_norm, nch=16, nvh=32, stage=9):
    identb = ctx.consts["ident"]
    identf = ctx.consts["identf"]
    onesf = ctx.consts["onesf"]
    onehotf = ctx.consts["onehotf"]
    maskS = ctx.consts["maskS"]
    maskIT = ctx.consts["maskIT"]
    normw = ctx.consts["normw"]

    vch = ctx.sem("p2v")
    ach = ctx.sem("p2a")
    tch = ctx.sem("p2t")

    S_all = ctx.sbuf("p2S", [128, 32 * 128], F32)
    beta_sb = ctx.sbuf("p2beta", [32, 2048], F32)
    cumg_sb = ctx.sbuf("p2cumg", [32, 2048], F32)
    sinit = ctx.sem("p2si")
    ctx.op("gpsimd", lambda e: e.memset(S_all[:], 0.0), inc=sinit)
    ctx.wait("vector", sinit, 1)
    ost_init = ctx.sem("p2oi")
    ldrows = ctx.sem("p2ldr")
    ctx.dma("sync", beta_sb[:], beta_i, inc=ldrows)
    ctx.dma("sync", cumg_sb[:], cumg_i, inc=ldrows)

    qc_b = [ctx.sbuf(f"p2q{i}", [128, 2048], BF16) for i in range(2)]
    kc_b = [ctx.sbuf(f"p2k{i}", [128, 2048], BF16) for i in range(2)]
    kt_b = [ctx.sbuf(f"p2kt{i}", [128, 2048], BF16) for i in range(2)]
    vt_b = [ctx.sbuf(f"p2vt{i}", [128, 4096], BF16) for i in range(2)]
    zc_b = [ctx.sbuf(f"p2z{i}", [128, 4096], BF16) for i in range(2)]
    lds = [[ctx.sem(f"p2ld{i}_{j}") for j in range(5)] for i in range(2)]
    rel_in = [None, None]

    rows = ctx.sbuf("p2rows", [32, 5 * 128], F32)
    rowsT = ctx.sbuf("p2rowsT", [128, 5 * 32], F32)
    tD = ctx.sbuf("p2tD", [128, 256], F32)
    eD = ctx.sbuf("p2eD", [128, 256], F32)
    n_pp = [ctx.sbuf(f"p2n{i}", [128, 128], BF16) for i in range(2)]
    nt_pow = ctx.sbuf("p2ntp", [128, 7 * 128], BF16)
    aq_bf = ctx.sbuf("p2aq", [128, 128], BF16)
    x_bf = [ctx.sbuf(f"p2x{i}", [128, 256], BF16) for i in range(2)]
    sbf = ctx.sbuf("p2sbf", [128, 128], BF16)
    wfm = ctx.sbuf("p2wfm", [128, 128], BF16)
    dl_bf = ctx.sbuf("p2dl", [128, 128], BF16)
    dls_bf = ctx.sbuf("p2dls", [128, 128], BF16)
    t1 = ctx.sbuf("p2t1", [128, 128], F32)
    otm = ctx.sbuf("p2otm", [128, 128], BF16)
    zs = ctx.sbuf("p2zs", [128, 128], F32)
    t2 = ctx.sbuf("p2t2", [128, 128], F32)
    sq2 = ctx.sbuf("p2sq2", [128, 128], BF16)
    rrow = ctx.sbuf("p2rr", [1, 128], F32)
    ecc = ctx.sbuf("p2ec", [128, 1], F32)
    ostage = [ctx.sbuf(f"p2os{i}", [128, 4096], BF16) for i in range(2)]
    for i in range(2):
        ctx.op("gpsimd", lambda e, i=i: e.memset(ostage[i][:], 0.0), inc=ost_init)
    ctx.wait("vector", ost_init, 2)

    B0 = ctx.psum("p2B0", [128, 512], F32)
    B1 = ctx.psum("p2B1", [128, 512], F32)
    B2 = ctx.psum("p2B2", [128, 512], F32)
    B3 = ctx.psum("p2B3", [128, 512], F32)
    B4 = ctx.psum("p2B4", [128, 512], F32)
    B5 = ctx.psum("p2B5", [128, 512], F32)
    B6 = ctx.psum("p2B6", [128, 512], F32)
    B7 = ctx.psum("p2B7", [128, 1024], BF16)

    rel = {k: None for k in
           ["B0", "B1", "B2", "B3", "B4", "B5", "B6", "B7",
            "rows", "rowsT", "tD", "eD", "aq", "sbf", "wfm", "dl", "dls",
            "t1", "otm", "zs", "t2", "sq2", "rrow", "ecc", "ntpow"]}
    rel["npp"] = [None, None]
    rel["x"] = [None, None]
    rel["ost"] = [None, None]
    ost_sem = ctx.sem("p2ost")

    for tc in range(nch):
        sl = tc % 2
        wt(ctx, "sync", rel_in[sl])
        ctx.op("sync", lambda e, sl=sl, tc=tc: e.dma_start(
            out=qc_b[sl][:].rearrange("p (h t) -> p h t", h=16),
            in_=q_fm.rearrange("(h p) t -> p h t", p=128)[:, :, tc * 128:(tc + 1) * 128]),
            inc=lds[sl][0], k=16)
        ctx.op("sync", lambda e, sl=sl, tc=tc: e.dma_start(
            out=kc_b[sl][:].rearrange("p (h t) -> p h t", h=16),
            in_=k_fm.rearrange("(h p) t -> p h t", p=128)[:, :, tc * 128:(tc + 1) * 128]),
            inc=lds[sl][1], k=16)
        ctx.op("sync", lambda e, sl=sl, tc=tc: e.dma_start(
            out=kt_b[sl][:].rearrange("p (h d) -> p h d", h=16),
            in_=k_tm.rearrange("(tc h t) d -> tc t h d", tc=16, h=16)[tc]),
            inc=lds[sl][2], k=16)
        ctx.op("sync", lambda e, sl=sl, tc=tc: e.dma_start(
            out=vt_b[sl][:].rearrange("p (v d) -> p v d", v=32),
            in_=v_tm.rearrange("(tc v t) d -> tc t v d", tc=16, v=32)[tc]),
            inc=lds[sl][3], k=16)
        ctx.op("sync", lambda e, sl=sl, tc=tc: e.dma_start(
            out=zc_b[sl][:].rearrange("p (v t) -> p v t", v=32),
            in_=z_fm.rearrange("(v p) t -> p v t", p=128)[:, :, tc * 128:(tc + 1) * 128]),
            inc=lds[sl][4], k=16)
        nld = 16 * (tc // 2 + 1)
        qc, kc, kt, vt, zc = (b[sl] for b in (qc_b, kc_b, kt_b, vt_b, zc_b))
        rel_in[sl] = []

        cgl = rows[:, 0:128]
        gam = rows[:, 128:256]
        eg = rows[:, 256:384]
        bgn = rows[:, 384:512]
        bet = rows[:, 512:640]
        if tc == 0:
            ctx.wait("vector", ldrows, 32)
        wt(ctx, "vector", rel["rows"])
        if tc == 0:
            ctx.op("vector", lambda e: e.tensor_copy(cgl, cumg_sb[:, 0:128]), inc=vch)
        else:
            ctx.op("vector", lambda e, tc=tc: e.tensor_scalar(
                cgl, cumg_sb[:, tc * 128:(tc + 1) * 128],
                cumg_sb[:, tc * 128 - 1:tc * 128], None, ALU.subtract), inc=vch)
        ctx.op("vector", lambda e, tc=tc: e.tensor_copy(
            bet, beta_sb[:, tc * 128:(tc + 1) * 128]), inc=vch)
        cgl_rdy = vch.n
        ctx.wait("scalar", vch, cgl_rdy)
        ctx.op("scalar", lambda e: e.activation(gam, cgl, AF.Exp), inc=ach)
        ctx.op("scalar", lambda e: e.activation(
            eg, cgl, AF.Exp, scale=cv(ctx, -1.0, 32), bias=cgl[:, 127:128]), inc=ach)
        ctx.wait("vector", ach, ach.n)
        ctx.op("vector", lambda e: e.scalar_tensor_tensor(
            bgn, gam, -1.0, bet, ALU.mult, ALU.mult), inc=vch)
        rows_rdy_v = vch.n
        ctx.wait("tensor", vch, rows_rdy_v)
        wt(ctx, "tensor", rel["B0"])
        wt(ctx, "tensor", rel["rowsT"])
        for j in range(5):
            ctx.op("tensor", lambda e, j=j: e.transpose(
                B0[:, j * 32:(j + 1) * 32], rows[:, j * 128:(j + 1) * 128],
                identf[0:32, 0:32]), inc=(tch if j == 4 else None))
        rowsT_mm = tch.n
        ctx.wait("vector", tch, rowsT_mm)
        ctx.op("vector", lambda e: e.tensor_copy(rowsT[:], B0[:, 0:160]), inc=vch)
        rel["rows"] = (tch, rowsT_mm)
        rel["B0"] = (vch, vch.n)
        rowsT_rdy = vch.n
        cglT = rowsT[:, 0:32]
        gamT = rowsT[:, 32:64]
        egT = rowsT[:, 64:96]
        bgnT = rowsT[:, 96:128]
        betT = rowsT[:, 128:160]

        ost = ostage[sl]
        wt(ctx, "vector", rel["ost"][sl])

        for vh in range(nvh):
            kh = vh // 2
            pair = (vh // 2) % 2
            pb = B1 if pair == 0 else B2
            pbk = "B1" if pair == 0 else "B2"
            if vh % 2 == 0:
                ctx.wait("tensor", lds[sl][0], nld)
                ctx.wait("tensor", lds[sl][1], nld)
                wt(ctx, "tensor", rel[pbk])
                ctx.op("tensor", lambda e, pb=pb, kc=kc, kh=kh: e.matmul(
                    pb[:, 0:128], kc[:, kh * 128:(kh + 1) * 128],
                    kc[:, kh * 128:(kh + 1) * 128], start=True, stop=True))
                ctx.op("tensor", lambda e, pb=pb, kc=kc, qc=qc, kh=kh: e.matmul(
                    pb[:, 128:256], kc[:, kh * 128:(kh + 1) * 128],
                    qc[:, kh * 128:(kh + 1) * 128], start=True, stop=True), inc=tch)
            bcol = 256 + (vh % 2) * 128
            ctx.wait("tensor", vch, rowsT_rdy)
            wt(ctx, "tensor", rel[pbk])
            ctx.op("tensor", lambda e, pb=pb, vh=vh, bcol=bcol, cgl=cgl: e.matmul(
                pb[:, bcol:bcol + 128], onehotf[:, vh * 128:(vh + 1) * 128],
                cgl, start=True, stop=True), inc=tch)
            mm_pair = tch.n
            if stage <= 1:
                continue
            ctx.wait("vector", tch, mm_pair)
            wt(ctx, "vector", rel["tD"])
            ctx.op("vector", lambda e, pb=pb, bcol=bcol, vh=vh, cglT=cglT: e.scalar_tensor_tensor(
                tD[:, 0:128], pb[:, bcol:bcol + 128], cglT[:, vh:vh + 1], maskS[:],
                ALU.subtract, ALU.add), inc=vch)
            ctx.op("vector", lambda e, pb=pb, bcol=bcol, vh=vh, cglT=cglT: e.scalar_tensor_tensor(
                tD[:, 128:256], pb[:, bcol:bcol + 128], cglT[:, vh:vh + 1], maskIT[:],
                ALU.subtract, ALU.subtract), inc=vch)
            tD_rdy = vch.n
            ctx.wait("scalar", vch, tD_rdy)
            wt(ctx, "scalar", rel["eD"])
            ctx.op("scalar", lambda e: e.activation(eD[:, 0:128], tD[:, 0:128],
                                                    AF.Exp, scale=cv(ctx, -1.0)), inc=ach)
            ctx.op("scalar", lambda e: e.activation(eD[:, 128:256], tD[:, 128:256],
                                                    AF.Exp), inc=ach)
            rel["tD"] = (ach, ach.n)
            eD_rdy = ach.n
            if stage <= 2:
                continue
            ctx.wait("vector", ach, eD_rdy)
            wt(ctx, "vector", rel["npp"][0])
            ctx.op("vector", lambda e, pb=pb, vh=vh, betT=betT: e.scalar_tensor_tensor(
                n_pp[0][:], eD[:, 0:128], betT[:, vh:vh + 1], pb[:, 0:128],
                ALU.mult, ALU.mult), inc=vch)
            n1_rdy = vch.n
            wt(ctx, "vector", rel["aq"])
            ctx.op("vector", lambda e, pb=pb: e.tensor_tensor(
                aq_bf[:], eD[:, 128:256], pb[:, 128:256], ALU.mult), inc=vch)
            rel["eD"] = (vch, vch.n)
            rel[pbk] = (vch, vch.n)
            aq_rdy = vch.n
            if stage <= 3:
                continue
            ctx.wait("tensor", vch, n1_rdy)
            wt(ctx, "tensor", rel["B7"])
            wt(ctx, "tensor", rel["ntpow"])
            ctx.op("tensor", lambda e: e.transpose(B7[:, 0:128], n_pp[0][:], identb[:]),
                   inc=tch)
            nt1_mm = tch.n
            ctx.wait("scalar", tch, nt1_mm)
            wt(ctx, "scalar", rel["ntpow"])
            ctx.op("scalar", lambda e: e.copy(nt_pow[:, 0:128], B7[:, 0:128]), inc=ach)
            rel["B7"] = (ach, ach.n)
            nt1_rdy = ach.n
            if stage <= 4:
                continue
            wt(ctx, "scalar", rel["x"][0])
            ctx.wait("scalar", lds[sl][3], nld)
            ctx.op("scalar", lambda e, vt=vt, vh=vh, betT=betT: e.activation(
                x_bf[0][:, 0:128], vt[:, vh * 128:(vh + 1) * 128], AF.Copy,
                scale=betT[:, vh:vh + 1]), inc=ach)
            ctx.wait("scalar", lds[sl][2], nld)
            ctx.op("scalar", lambda e, kt=kt, kh=kh, vh=vh, bgnT=bgnT: e.activation(
                x_bf[0][:, 128:256], kt[:, kh * 128:(kh + 1) * 128], AF.Copy,
                scale=bgnT[:, vh:vh + 1]), inc=ach)
            x_rdy = ach.n
            if stage <= 5:
                continue

            cur = 0
            n_rdy = n1_rdy
            nt_rdy = nt1_rdy
            ntpow_reads = []
            for li in range(6):
                nxt = 1 - cur
                pq = B3 if li % 2 == 0 else B4
                pqk = "B3" if li % 2 == 0 else "B4"
                if isinstance(n_rdy, tuple):
                    ctx.wait("tensor", vch if n_rdy[0] == "v" else ach, n_rdy[1])
                else:
                    ctx.wait("tensor", vch, n_rdy)
                if isinstance(nt_rdy, tuple):
                    ctx.wait("tensor", vch if nt_rdy[0] == "v" else ach, nt_rdy[1])
                else:
                    ctx.wait("tensor", ach, nt_rdy)
                wt(ctx, "tensor", rel[pqk])
                ntsl = nt_pow[:, li * 128:(li + 1) * 128]
                ntdst = nt_pow[:, (li + 1) * 128:(li + 2) * 128]
                if li < 5:
                    ctx.op("tensor", lambda e, pq=pq, ntsl=ntsl, cur=cur: e.matmul(
                        pq[:, 0:128], ntsl, n_pp[cur][:], start=True, stop=True))
                ctx.op("tensor", lambda e, pq=pq, ntsl=ntsl, cur=cur: e.matmul(
                    pq[:, 128:256], n_pp[cur][:], ntsl, start=True, stop=True), inc=tch)
                sqmm = tch.n
                ntpow_reads.append((tch, sqmm))
                rel["npp"][cur] = (tch, sqmm)
                if li % 2 == 0:
                    ctx.wait("vector", tch, sqmm)
                    if li < 5:
                        wt(ctx, "vector", rel["npp"][nxt])
                        ctx.op("vector", lambda e, pq=pq, nxt=nxt: e.tensor_copy(
                            n_pp[nxt][:], pq[:, 0:128]), inc=vch)
                        n_rdy = vch.n
                    ctx.op("vector", lambda e, pq=pq, ntdst=ntdst: e.tensor_copy(
                        ntdst, pq[:, 128:256]), inc=vch)
                    nt_rdy = ("v", vch.n)
                    rel[pqk] = (vch, vch.n)
                else:
                    ctx.wait("scalar", tch, sqmm)
                    if li < 5:
                        wt(ctx, "scalar", rel["npp"][nxt])
                        ctx.op("scalar", lambda e, pq=pq, nxt=nxt: e.copy(
                            n_pp[nxt][:], pq[:, 0:128]), inc=ach)
                        n_rdy = ("a", ach.n)
                    ctx.op("scalar", lambda e, pq=pq, ntdst=ntdst: e.copy(
                        ntdst, pq[:, 128:256]), inc=ach)
                    nt_rdy = ("a", ach.n)
                    rel[pqk] = (ach, ach.n)
                cur = nxt
            nt_all_rdy = nt_rdy if not isinstance(nt_rdy, tuple) else None
            nt_all_rec = ((vch if nt_rdy[0] == "v" else ach, nt_rdy[1])
                          if isinstance(nt_rdy, tuple) else (ach, nt_rdy))
            if stage <= 6:
                continue

            xcur = 0
            x_cur_rdy = x_rdy
            ap_engine_flip = 0
            for ai, p in enumerate([64, 32, 16, 8, 4, 2, 1]):
                li = {1: 0, 2: 1, 4: 2, 8: 3, 16: 4, 32: 5, 64: 6}[p]
                px = B5 if ai % 2 == 0 else B6
                pxk = "B5" if ai % 2 == 0 else "B6"
                ntsl = nt_pow[:, li * 128:(li + 1) * 128]
                wt(ctx, "tensor", nt_all_rec)
                ctx.wait("tensor", ach, x_cur_rdy if ai == 0 else 0)
                ctx.wait("tensor", vch, x_cur_rdy if ai > 0 else 0)
                wt(ctx, "tensor", rel[pxk])
                ctx.op("tensor", lambda e, px=px, ntsl=ntsl, xcur=xcur: e.matmul(
                    px[:, 0:256], ntsl, x_bf[xcur][:], start=True, stop=True), inc=tch)
                apmm = tch.n
                ntpow_reads.append((tch, apmm))
                xnxt = 1 - xcur
                ctx.wait("vector", tch, apmm)
                wt(ctx, "vector", rel["x"][xnxt])
                op = ALU.add if p >= 2 else ALU.subtract
                ctx.op("vector", lambda e, px=px, xcur=xcur, xnxt=xnxt, op=op: e.tensor_tensor(
                    x_bf[xnxt][:], x_bf[xcur][:], px[:, 0:256], op), inc=vch)
                rel["x"][xcur] = (vch, vch.n)
                rel[pxk] = (vch, vch.n)
                x_cur_rdy = vch.n
                xcur = xnxt
            rel["ntpow"] = ntpow_reads[-1]
            xfin = xcur
            Ux = x_bf[xfin][:, 0:128]
            Wx = x_bf[xfin][:, 128:256]
            if stage <= 7:
                continue

            ctx.wait("vector", vch, x_cur_rdy)
            wt(ctx, "vector", rel["sbf"])
            ctx.op("vector", lambda e, vh=vh: e.tensor_copy(
                sbf[:], S_all[:, vh * 128:(vh + 1) * 128]), inc=vch)
            sbf_rdy = vch.n
            ctx.wait("tensor", vch, x_cur_rdy)
            wt(ctx, "tensor", rel["B7"])
            ctx.op("tensor", lambda e, Wx=Wx: e.transpose(B7[:, 128:256], Wx, identb[:]),
                   inc=tch)
            wmm = tch.n
            ctx.wait("scalar", tch, wmm)
            wt(ctx, "scalar", rel["wfm"])
            ctx.op("scalar", lambda e: e.copy(wfm[:], B7[:, 128:256]), inc=ach)
            rel["B7"] = (ach, ach.n)
            wfm_rdy = ach.n
            if tc == 0:
                rel["wfm"] = (ach, wfm_rdy)
            if tc > 0:
                ctx.wait("tensor", ach, wfm_rdy)
                ctx.wait("tensor", vch, sbf_rdy)
                wt(ctx, "tensor", rel["B5"])
                ctx.op("tensor", lambda e: e.matmul(
                    B5[:, 0:128], wfm[:], sbf[:], start=True, stop=True), inc=tch)
                dlmm = tch.n
                rel["wfm"] = (tch, dlmm)
                ctx.wait("vector", tch, dlmm)
                wt(ctx, "vector", rel["dl"])
                ctx.op("vector", lambda e, Ux=Ux: e.tensor_tensor(
                    dl_bf[:], Ux, B5[:, 0:128], ALU.add), inc=vch)
                rel["B5"] = (vch, vch.n)
            else:
                wt(ctx, "vector", rel["dl"])
                ctx.op("vector", lambda e, Ux=Ux: e.tensor_copy(dl_bf[:], Ux), inc=vch)
            rel["x"][xfin] = (vch, vch.n)
            dl_rdy = vch.n
            ctx.wait("scalar", vch, dl_rdy)
            wt(ctx, "scalar", rel["dls"])
            ctx.op("scalar", lambda e, vh=vh, egT=egT: e.activation(
                dls_bf[:], dl_bf[:], AF.Copy, scale=egT[:, vh:vh + 1]), inc=ach)
            dls_rdy = ach.n
            wt(ctx, "tensor", rel["B6"])
            if tc > 0:
                ctx.op("tensor", lambda e, qc=qc, kh=kh: e.matmul(
                    B6[:, 0:128], qc[:, kh * 128:(kh + 1) * 128], sbf[:],
                    start=True, stop=True))
            ctx.wait("tensor", vch, dl_rdy)
            ctx.op("tensor", lambda e, qc=qc: e.matmul(
                B6[:, 128:256], aq_bf[:], dl_bf[:], start=True, stop=True), inc=tch)
            rel["aq"] = (tch, tch.n)
            omm = tch.n
            wt(ctx, "tensor", rel["ecc"])
            ctx.op("tensor", lambda e, vh=vh, gam=gam: e.matmul(
                B0[:, 384:385], onehotf[:, vh * 128:(vh + 1) * 128],
                gam[:, 127:128], start=True, stop=True), inc=tch)
            ecmm = tch.n
            ctx.wait("vector", tch, ecmm)
            ctx.op("vector", lambda e: e.tensor_copy(ecc[:], B0[:, 384:385]), inc=vch)
            ec_rdy = vch.n
            rel["B0"] = (vch, ec_rdy)
            ctx.wait("tensor", ach, dls_rdy)
            wt(ctx, "tensor", rel["B5"])
            ctx.op("tensor", lambda e, kt=kt, kh=kh: e.matmul(
                B5[:, 128:256], kt[:, kh * 128:(kh + 1) * 128], dls_bf[:],
                start=True, stop=True), inc=tch)
            rel["dls"] = (tch, tch.n)
            psmm = tch.n
            ctx.wait("vector", tch, psmm)
            ctx.wait("vector", vch, ec_rdy)
            ctx.op("vector", lambda e, vh=vh: e.scalar_tensor_tensor(
                S_all[:, vh * 128:(vh + 1) * 128], S_all[:, vh * 128:(vh + 1) * 128],
                ecc[:], B5[:, 128:256], ALU.mult, ALU.add), inc=vch)
            rel["B5"] = (vch, vch.n)
            rel["ecc"] = (vch, vch.n)
            rel["sbf"] = (tch, omm) if tc > 0 else None
            ctx.wait("vector", tch, omm)
            wt(ctx, "vector", rel["t1"])
            wt(ctx, "vector", rel["otm"])
            if tc > 0:
                ctx.op("vector", lambda e, vh=vh, gamT=gamT: e.tensor_scalar(
                    t1[:], B6[:, 0:128], gamT[:, vh:vh + 1], None, ALU.mult), inc=vch)
                ctx.wait("vector", vch, vch.n)
                ctx.op("vector", lambda e: e.tensor_tensor(
                    otm[:], t1[:], B6[:, 128:256], ALU.add), inc=vch)
            else:
                ctx.op("vector", lambda e: e.tensor_copy(otm[:], B6[:, 128:256]), inc=vch)
            rel["B6"] = (vch, vch.n)
            rel["t1"] = (vch, vch.n)
            otm_rdy = vch.n
            rel["dl"] = (tch, omm)
            if stage <= 8:
                continue
            ctx.wait("tensor", vch, otm_rdy)
            wt(ctx, "tensor", rel["B7"])
            ctx.op("tensor", lambda e: e.transpose(B7[:, 256:384], otm[:], identb[:]),
                   inc=tch)
            otmm = tch.n
            rel["otm"] = (tch, otmm)
            ctx.wait("scalar", lds[sl][4], nld)
            wt(ctx, "scalar", rel["zs"])
            ctx.op("scalar", lambda e, zc=zc, vh=vh: e.activation(
                zs[:], zc[:, vh * 128:(vh + 1) * 128], AF.Silu), inc=ach)
            zs_rdy = ach.n
            ctx.wait("vector", tch, otmm)
            ctx.wait("vector", ach, zs_rdy)
            wt(ctx, "vector", rel["t2"])
            ctx.op("vector", lambda e: e.tensor_tensor(t2[:], B7[:, 256:384], zs[:],
                                                       ALU.mult), inc=vch)
            rel["B7"] = (vch, vch.n)
            rel["zs"] = (vch, vch.n)
            t2_rdy = vch.n
            ctx.wait("scalar", vch, t2_rdy)
            wt(ctx, "scalar", rel["sq2"])
            ctx.op("scalar", lambda e: e.activation(sq2[:], t2[:], AF.Square), inc=ach)
            sq2_rdy = ach.n
            ctx.wait("tensor", ach, sq2_rdy)
            wt(ctx, "tensor", rel["rrow"])
            wt(ctx, "tensor", rel["B5"])
            ctx.op("tensor", lambda e: e.matmul(
                B5[0:1, 384:512], ctx.consts["onesb"][:, 0:1], sq2[:],
                start=True, stop=True), inc=tch)
            prmm = tch.n
            rel["sq2"] = (tch, prmm)
            ctx.wait("scalar", tch, prmm)
            ctx.op("scalar", lambda e: e.activation(
                rrow[:], B5[0:1, 384:512], AF.Sqrt,
                scale=cv(ctx, 1.0 / 128, 1), bias=cv(ctx, 1e-6, 1)), inc=ach)
            rel["B5"] = (ach, ach.n)
            ctx.wait("vector", ach, ach.n)
            ctx.op("vector", lambda e: e.reciprocal(rrow[:], rrow[:]), inc=vch)
            rr_rdy = vch.n
            ctx.wait("tensor", vch, rr_rdy)
            wt(ctx, "tensor", rel["B6"])
            ctx.op("tensor", lambda e: e.matmul(
                B6[:, 256:384], onesf[0:1, :], rrow[:], start=True, stop=True),
                inc=tch)
            rbmm = tch.n
            rel["rrow"] = (tch, rbmm)
            ctx.wait("vector", tch, rbmm)
            ctx.op("vector", lambda e, vh=vh, ost=ost: e.scalar_tensor_tensor(
                ost[:, vh * 128:(vh + 1) * 128], t2[:], normw[:],
                B6[:, 256:384], ALU.mult, ALU.mult), inc=vch)
            rel["B6"] = (vch, vch.n)
            rel["t2"] = (vch, vch.n)
        ctx.wait("sync", vch, vch.n)
        def st_o(e, sl=sl, tc=tc):
            dst = o_norm.rearrange("(v p) t -> p v t", p=128)[:, :, tc * 128:(tc + 1) * 128]
            return e.dma_start(out=dst, in_=ostage[sl][:].rearrange("p (v t) -> p v t", v=32))
        ctx.op("sync", st_o, inc=ost_sem, k=16)
        rel["ost"][sl] = (ost_sem, ost_sem.n)
        chunk_done = [(vch, vch.n), (ach, ach.n), (tch, tch.n)]
        rel_in[sl] = chunk_done + [(ost_sem, ost_sem.n)]
        rel["rows"] = chunk_done
        rel["rowsT"] = chunk_done
    return dict(ost=ost_sem, done=[(ost_sem, ost_sem.n)])



def _row_chain(ctx, vch, ach, rows, src1, src2, n):
    ctx.op("vector", lambda e: e.tensor_scalar(
        rows[:, 2 * n:3 * n], src1, 1.0 / 2048, None, ALU.mult), inc=vch)
    ctx.op("vector", lambda e: e.tensor_scalar(
        rows[:, 3 * n:4 * n], src2, 1.0 / 2048, None, ALU.mult), inc=vch)
    ctx.wait("vector", vch, vch.n)
    ctx.op("vector", lambda e: e.tensor_tensor(
        rows[:, 0:n], rows[:, 2 * n:3 * n], rows[:, 2 * n:3 * n], ALU.mult), inc=vch)
    ctx.wait("vector", vch, vch.n)
    ctx.op("vector", lambda e: e.tensor_tensor(
        rows[:, 0:n], rows[:, 3 * n:4 * n], rows[:, 0:n], ALU.subtract), inc=vch)
    ctx.wait("scalar", vch, vch.n)
    ctx.op("scalar", lambda e: e.activation(
        rows[:, 0:n], rows[:, 0:n], AF.Sqrt, bias=cv(ctx, 1e-5, 1)), inc=ach)
    ctx.wait("vector", ach, ach.n)
    ctx.op("vector", lambda e: e.reciprocal(rows[:, 0:n], rows[:, 0:n]), inc=vch)
    ctx.wait("vector", vch, vch.n)
    ctx.op("vector", lambda e: e.scalar_tensor_tensor(
        rows[:, n:2 * n], rows[:, 2 * n:3 * n], -1.0, rows[:, 0:n],
        ALU.mult, ALU.mult), inc=vch)
    return vch.n


def emit_p2b(ctx, o_norm, x_fm, w_o, x2_fm, xn2_fm):
    onesb = ctx.consts["onesb"]
    onesf = ctx.consts["onesf"]
    vch = ctx.sem("pbv")
    ach = ctx.sem("pba")
    tch = ctx.sem("pbt")

    ost = ctx.sbuf("pbo", [128, 32 * 512], BF16)
    attn = ctx.sbuf("pbattn", [128, 16 * 512], BF16)
    x2s = ctx.sbuf("pbx2", [128, 16 * 512], BF16)
    xfs = ctx.sbuf("pbxf", [128, 16 * 512], BF16)
    sqt = ctx.sbuf("pbsq", [128, 512], BF16)
    tmp = ctx.sbuf("pbtmp", [128, 512], F32)
    rows = ctx.sbuf("pbrows", [1, 4 * 512], F32)
    wob = [ctx.sbuf(f"pbw{i}", [128, 32 * 128], BF16) for i in range(2)]
    xn2s = ost[:, 0:16 * 512]
    olds = ctx.sem("pbol")
    xlds = ctx.sem("pbxl")
    wlds = [ctx.sem(f"pbwl{i}") for i in range(2)]
    sts = ctx.sem("pbst")
    pA = ctx.psum("pbA", [128, 512], F32)
    pB = ctx.psum("pbB", [128, 512], F32)
    pS1 = ctx.psum("pbS1", [1, 512], F32)
    pS2 = ctx.psum("pbS2", [1, 512], F32)
    pR1 = ctx.psum("pbR1", [128, 512], F32)
    pR2 = ctx.psum("pbR2", [128, 512], F32)
    rel = {k: None for k in ["A", "B", "S", "R", "sq", "strip", "rows", "tmp", "ost"]}
    rel_w = [None, None]
    nw = [0]
    wot = w_o.rearrange("(m k p) f -> m p k f", m=16, k=32)

    for s in range(4):
        wt(ctx, "sync", rel["strip"])
        ctx.op("sync", lambda e, s=s: e.dma_start(
            out=ost[:].rearrange("p (k t) -> p k t", k=32),
            in_=o_norm.rearrange("(k p) t -> p k t", p=128)[:, :, s * 512:(s + 1) * 512]),
            inc=olds, k=16)
        ctx.op("sync", lambda e, s=s: e.dma_start(
            out=xfs[:].rearrange("p (k t) -> p k t", k=16),
            in_=x_fm.rearrange("(k p) t -> p k t", p=128)[:, :, s * 512:(s + 1) * 512]),
            inc=xlds, k=16)
        for m in range(16):
            sl = m % 2
            wb = wob[sl]
            wt(ctx, "gpsimd", rel_w[sl])
            ctx.op("gpsimd", lambda e, wb=wb, m=m: e.dma_start(
                out=wb[:].rearrange("p (k f) -> p k f", k=32), in_=wot[m]),
                inc=wlds[sl], k=16)
            nw[0] += 1
            ctx.wait("tensor", wlds[sl], 16 * ((nw[0] - 1) // 2 + 1))
            ctx.wait("tensor", olds, 16 * (s + 1))
            pm = pA if sl == 0 else pB
            wt(ctx, "tensor", rel["A" if sl == 0 else "B"])
            for k in range(32):
                ctx.op("tensor", lambda e, pm=pm, wb=wb, k=k: e.matmul(
                    pm[:], wb[:, k * 128:(k + 1) * 128],
                    ost[:, k * 512:(k + 1) * 512],
                    start=(k == 0), stop=(k == 31)), inc=(tch if k == 31 else None))
            rel_w[sl] = (tch, tch.n)
            mmv = tch.n
            ctx.wait("scalar", tch, mmv)
            wt(ctx, "scalar", rel["sq"])
            ctx.op("scalar", lambda e, pm=pm, m=m: e.copy(
                attn[:, m * 512:(m + 1) * 512], pm[:]), inc=ach)
            ctx.op("scalar", lambda e, pm=pm: e.activation(
                sqt[:], pm[:], AF.Square), inc=ach)
            sqv = ach.n
            rel["A" if sl == 0 else "B"] = (ach, sqv)
            ctx.wait("tensor", ach, sqv)
            if m == 0:
                wt(ctx, "tensor", rel["S"])
            ctx.op("tensor", lambda e, m=m: e.matmul(
                pS1[:], onesb[:, 0:1], attn[:, m * 512:(m + 1) * 512],
                start=(m == 0), stop=(m == 15)))
            ctx.op("tensor", lambda e, m=m: e.matmul(
                pS2[:], onesb[:, 0:1], sqt[:],
                start=(m == 0), stop=(m == 15)), inc=(tch if m == 15 else None))
            rel["sq"] = (tch, tch.n)
        stmm = tch.n
        rel["ost"] = (tch, stmm)
        ctx.wait("vector", tch, stmm)
        wt(ctx, "vector", rel["rows"])
        rc1 = _row_chain(ctx, vch, ach, rows, pS1[:], pS2[:], 512)
        rel["S"] = (vch, rc1)
        wt(ctx, "tensor", rel["R"])
        ctx.wait("tensor", vch, rc1)
        ctx.op("tensor", lambda e: e.matmul(
            pR1[:], onesf[0:1, :], rows[:, 0:512], start=True, stop=True))
        ctx.op("tensor", lambda e: e.matmul(
            pR2[:], onesf[0:1, :], rows[:, 512:1024], start=True, stop=True), inc=tch)
        rmm = tch.n
        ctx.wait("vector", tch, rmm)
        ctx.wait("vector", xlds, 16 * (s + 1))
        for m in range(16):
            off = m * 512
            wt(ctx, "vector", rel["tmp"])
            ctx.op("vector", lambda e, off=off: e.tensor_tensor(
                tmp[:], attn[:, off:off + 512], pR1[:], ALU.mult), inc=vch)
            ctx.wait("vector", vch, vch.n)
            ctx.op("vector", lambda e: e.tensor_tensor(
                tmp[:], tmp[:], pR2[:], ALU.add), inc=vch)
            ctx.wait("vector", vch, vch.n)
            ctx.op("vector", lambda e, off=off: e.tensor_tensor(
                x2s[:, off:off + 512], tmp[:], xfs[:, off:off + 512], ALU.add), inc=vch)
            rel["tmp"] = (vch, vch.n)
            x2v = vch.n
            ctx.wait("scalar", vch, x2v)
            wt(ctx, "scalar", rel["sq"])
            ctx.op("scalar", lambda e, off=off: e.activation(
                sqt[:], x2s[:, off:off + 512], AF.Square), inc=ach)
            sqv = ach.n
            ctx.wait("tensor", ach, sqv)
            ctx.wait("tensor", vch, x2v)
            if m == 0:
                wt(ctx, "tensor", rel["S"])
            ctx.op("tensor", lambda e, off=off, m=m: e.matmul(
                pS1[:], onesb[:, 0:1], x2s[:, off:off + 512],
                start=(m == 0), stop=(m == 15)))
            ctx.op("tensor", lambda e, m=m: e.matmul(
                pS2[:], onesb[:, 0:1], sqt[:],
                start=(m == 0), stop=(m == 15)), inc=(tch if m == 15 else None))
            rel["sq"] = (tch, tch.n)
        rel["R"] = (vch, vch.n)
        st2 = tch.n
        ctx.wait("vector", tch, st2)
        rc2 = _row_chain(ctx, vch, ach, rows, pS1[:], pS2[:], 512)
        rel["S"] = (vch, rc2)
        rel["rows"] = (vch, rc2)
        wt(ctx, "tensor", rel["R"])
        ctx.wait("tensor", vch, rc2)
        ctx.op("tensor", lambda e: e.matmul(
            pR1[:], onesf[0:1, :], rows[:, 0:512], start=True, stop=True))
        ctx.op("tensor", lambda e: e.matmul(
            pR2[:], onesf[0:1, :], rows[:, 512:1024], start=True, stop=True), inc=tch)
        rmm2 = tch.n
        ctx.wait("vector", tch, rmm2)
        wt(ctx, "vector", rel["ost"])
        for m in range(16):
            off = m * 512
            wt(ctx, "vector", rel["tmp"])
            ctx.op("vector", lambda e, off=off: e.tensor_tensor(
                tmp[:], x2s[:, off:off + 512], pR1[:], ALU.mult), inc=vch)
            ctx.wait("vector", vch, vch.n)
            ctx.op("vector", lambda e, off=off: e.tensor_tensor(
                xn2s[:, off:off + 512], tmp[:], pR2[:], ALU.add), inc=vch)
            rel["tmp"] = (vch, vch.n)
        rel["R"] = (vch, vch.n)
        ctx.wait("sync", vch, vch.n)
        ctx.op("sync", lambda e, s=s: e.dma_start(
            out=x2_fm.rearrange("(k p) t -> p k t", p=128)[:, :, s * 512:(s + 1) * 512],
            in_=x2s[:].rearrange("p (k t) -> p k t", k=16)), inc=sts, k=16)
        ctx.op("sync", lambda e, s=s: e.dma_start(
            out=xn2_fm.rearrange("(k p) t -> p k t", p=128)[:, :, s * 512:(s + 1) * 512],
            in_=xn2s.rearrange("p (k t) -> p k t", k=16)), inc=sts, k=16)
        rel["strip"] = [(vch, vch.n), (ach, ach.n), (tch, tch.n), (sts, sts.n)]
    return dict(sts=sts, done=[(sts, sts.n)])



def emit_p3(ctx, xn2_fm, x2_fm, w1q, w2q, out, nstrips=4):
    onesb = ctx.consts["onesb"]
    onesf = ctx.consts["onesf"]
    identf = ctx.consts["identf"]
    b1 = ctx.consts["b1"]
    b2 = ctx.consts["b2"]
    vch = ctx.sem("p3v")
    ach = ctx.sem("p3a")
    tch = ctx.sem("p3t")

    xns = ctx.sbuf("p3xn", [128, 16 * 512], BF16)
    hst = ctx.sbuf("p3h", [128, 64 * 512], BF16)
    fstg = ctx.sbuf("p3f", [128, 16 * 512], BF16)
    outT = ctx.sbuf("p3ot", [128, 4 * 2048], F32)
    sqt = ctx.sbuf("p3sq", [128, 512], BF16)
    tmp = ctx.sbuf("p3tmp", [128, 512], F32)
    outm = ctx.sbuf("p3om", [128, 512], F32)
    rows = ctx.sbuf("p3rows", [1, 4 * 512], F32)
    w1b = [ctx.sbuf(f"p3w1{i}", [128, 16 * 128], BF16) for i in range(2)]
    w2b = [ctx.sbuf(f"p3w2{i}", [128, 64 * 128], BF16) for i in range(2)]
    x2st = xns
    xl = ctx.sem("p3xl")
    x2l = ctx.sem("p3x2l")
    w1l = [ctx.sem(f"p3w1l{i}") for i in range(2)]
    w2l = [ctx.sem(f"p3w2l{i}") for i in range(2)]
    sts = ctx.sem("p3st")
    pA = ctx.psum("p3A", [128, 512], F32)
    pB = ctx.psum("p3B", [128, 512], F32)
    pS1 = ctx.psum("p3S1", [1, 512], F32)
    pS2 = ctx.psum("p3S2", [1, 512], F32)
    pR1 = ctx.psum("p3R1", [128, 512], F32)
    pR2 = ctx.psum("p3R2", [128, 512], F32)
    pT = ctx.psum("p3T", [128, 512], F32)
    rel = {k: None for k in ["A", "B", "S", "R", "T", "sq", "strip", "rows",
                             "tmp", "h", "f", "outm", "outT", "xns"]}
    rel_w1 = [None, None]
    rel_w2 = [None, None]
    nw1 = [0]
    nw2 = [0]
    w1t = w1q.rearrange("(m p) f -> m p f", m=64)
    w2t = w2q.rearrange("(m p a) b -> m p (a b)", m=16, p=128, a=4)
    outp = out.rearrange("(b p) e -> b p e", p=128)

    for s in range(nstrips):
        wt(ctx, "sync", rel["strip"])
        wt(ctx, "sync", rel["xns"])
        ctx.op("sync", lambda e, s=s: e.dma_start(
            out=xns[:].rearrange("p (k t) -> p k t", k=16),
            in_=xn2_fm.rearrange("(k p) t -> p k t", p=128)[:, :, s * 512:(s + 1) * 512]),
            inc=xl, k=16)
        for m in range(64):
            sl = m % 2
            wb = w1b[sl]
            wt(ctx, "gpsimd", rel_w1[sl])
            ctx.op("gpsimd", lambda e, wb=wb, m=m: e.dma_start(
                out=wb[:], in_=w1t[m]), inc=w1l[sl], k=16)
            nw1[0] += 1
            ctx.wait("tensor", w1l[sl], 16 * ((nw1[0] - 1) // 2 + 1))
            ctx.wait("tensor", xl, 16 * (s + 1))
            pm = pA if sl == 0 else pB
            wt(ctx, "tensor", rel["A" if sl == 0 else "B"])
            for k in range(16):
                ctx.op("tensor", lambda e, pm=pm, wb=wb, k=k: e.matmul(
                    pm[:], wb[:, k * 128:(k + 1) * 128],
                    xns[:, k * 512:(k + 1) * 512],
                    start=(k == 0), stop=(k == 15)), inc=(tch if k == 15 else None))
            rel_w1[sl] = (tch, tch.n)
            mmv = tch.n
            ctx.wait("scalar", tch, mmv)
            wt(ctx, "scalar", rel["h"])
            ctx.op("scalar", lambda e, pm=pm, m=m: e.activation(
                hst[:, m * 512:(m + 1) * 512], pm[:], AF.Gelu,
                bias=b1[:, m:m + 1]), inc=ach)
            rel["A" if sl == 0 else "B"] = (ach, ach.n)
        h_rdy = ach.n
        ffn1_mms = tch.n
        rel["xns"] = (tch, ffn1_mms)
        wt(ctx, "sync", rel["xns"])
        ctx.op("sync", lambda e, s=s: e.dma_start(
            out=x2st[:].rearrange("p (k t) -> p k t", k=16),
            in_=x2_fm.rearrange("(k p) t -> p k t", p=128)[:, :, s * 512:(s + 1) * 512]),
            inc=x2l, k=16)
        for m in range(16):
            sl = m % 2
            wb = w2b[sl]
            wt(ctx, "gpsimd", rel_w2[sl])
            ctx.op("gpsimd", lambda e, wb=wb, m=m: e.dma_start(
                out=wb[:], in_=w2t[m]), inc=w2l[sl], k=16)
            nw2[0] += 1
            ctx.wait("tensor", w2l[sl], 16 * ((nw2[0] - 1) // 2 + 1))
            ctx.wait("tensor", ach, h_rdy)
            pm = pA if sl == 0 else pB
            wt(ctx, "tensor", rel["A" if sl == 0 else "B"])
            for k in range(64):
                ctx.op("tensor", lambda e, pm=pm, wb=wb, k=k: e.matmul(
                    pm[:], wb[:, k * 128:(k + 1) * 128],
                    hst[:, k * 512:(k + 1) * 512],
                    start=(k == 0), stop=(k == 63)), inc=(tch if k == 63 else None))
            rel_w2[sl] = (tch, tch.n)
            mmv = tch.n
            ctx.wait("vector", tch, mmv)
            wt(ctx, "vector", rel["f"])
            ctx.op("vector", lambda e, pm=pm, m=m: e.tensor_scalar(
                fstg[:, m * 512:(m + 1) * 512], pm[:], b2[:, m:m + 1], None,
                ALU.add), inc=vch)
            fv = vch.n
            ctx.wait("scalar", vch, fv)
            wt(ctx, "scalar", rel["sq"])
            ctx.op("scalar", lambda e, m=m: e.activation(
                sqt[:], fstg[:, m * 512:(m + 1) * 512], AF.Square), inc=ach)
            sqv = ach.n
            rel["A" if sl == 0 else "B"] = [(vch, fv), (ach, sqv)]
            ctx.wait("tensor", ach, sqv)
            ctx.wait("tensor", vch, fv)
            if m == 0:
                wt(ctx, "tensor", rel["S"])
            ctx.op("tensor", lambda e, m=m: e.matmul(
                pS1[:], onesb[:, 0:1], fstg[:, m * 512:(m + 1) * 512],
                start=(m == 0), stop=(m == 15)))
            ctx.op("tensor", lambda e, m=m: e.matmul(
                pS2[:], onesb[:, 0:1], sqt[:],
                start=(m == 0), stop=(m == 15)), inc=(tch if m == 15 else None))
            rel["sq"] = (tch, tch.n)
        stmm = tch.n
        ctx.wait("vector", tch, stmm)
        wt(ctx, "vector", rel["rows"])
        rc = _row_chain(ctx, vch, ach, rows, pS1[:], pS2[:], 512)
        rel["S"] = (vch, rc)
        rel["rows"] = (vch, rc)
        wt(ctx, "tensor", rel["R"])
        ctx.wait("tensor", vch, rc)
        ctx.op("tensor", lambda e: e.matmul(
            pR1[:], onesf[0:1, :], rows[:, 0:512], start=True, stop=True))
        ctx.op("tensor", lambda e: e.matmul(
            pR2[:], onesf[0:1, :], rows[:, 512:1024], start=True, stop=True), inc=tch)
        rmm = tch.n
        ctx.wait("vector", tch, rmm)
        ctx.wait("vector", x2l, 16 * (s + 1))
        for m in range(16):
            off = m * 512
            wt(ctx, "vector", rel["tmp"])
            ctx.op("vector", lambda e, off=off: e.tensor_tensor(
                tmp[:], fstg[:, off:off + 512], pR1[:], ALU.mult), inc=vch)
            ctx.wait("vector", vch, vch.n)
            ctx.op("vector", lambda e: e.tensor_tensor(
                tmp[:], tmp[:], pR2[:], ALU.add), inc=vch)
            ctx.wait("vector", vch, vch.n)
            wt(ctx, "vector", rel["outm"])
            ctx.op("vector", lambda e, off=off: e.tensor_tensor(
                outm[:], tmp[:], x2st[:, off:off + 512], ALU.add), inc=vch)
            rel["tmp"] = (vch, vch.n)
            outm_rdy = vch.n
            ctx.wait("tensor", vch, outm_rdy)
            wt(ctx, "tensor", rel["T"])
            for tt4 in range(4):
                ctx.op("tensor", lambda e, tt4=tt4: e.transpose(
                    pT[:, tt4 * 128:(tt4 + 1) * 128],
                    outm[:, tt4 * 128:(tt4 + 1) * 128], identf[:]),
                    inc=(tch if tt4 == 3 else None))
            tmm = tch.n
            rel["outm"] = (tch, tmm)
            ctx.wait("vector", tch, tmm)
            wt(ctx, "vector", rel["outT"])
            def cpo(e, m=m):
                dst = outT[:].rearrange("p (b e) -> p b e", b=4)[:, :, m * 128:(m + 1) * 128]
                return e.tensor_copy(dst, pT[:].rearrange("p (b e) -> p b e", b=4))
            ctx.op("vector", cpo, inc=vch)
            rel["T"] = (vch, vch.n)
        rel["R"] = (vch, vch.n)
        rel["f"] = (vch, vch.n)
        ctx.wait("sync", vch, vch.n)
        for tt4 in range(4):
            ctx.op("sync", lambda e, s=s, tt4=tt4: e.dma_start(
                out=outp[s * 4 + tt4],
                in_=outT[:, tt4 * 2048:(tt4 + 1) * 2048]), inc=sts, k=16)
        rel["outT"] = (sts, sts.n)
        rel["h"] = (tch, tch.n)
        rel["strip"] = [(vch, vch.n), (ach, ach.n), (tch, tch.n), (sts, sts.n)]
        if s == 0 and getattr(ctx, "dbg_h", None) is not None:
            ctx.wait("sync", ach, ach.n)
            ctx.wait("sync", vch, vch.n)
            ctx.dma("sync", ctx.dbg_h, hst[:], inc=sts)
            ctx.dma("sync", ctx.dbg_f, fstg[:], inc=sts)
            rel["strip"].append((sts, sts.n))
            rel["h"] = [rel["h"], (sts, sts.n)]
            rel["f"] = [rel["f"], (sts, sts.n)]
    return dict(sts=sts, done=[(sts, sts.n)])


import numpy as _np


def _consts_decl(ctx, names):
    spec = {
        "ident": ((128, 128), BF16), "identf": ((128, 128), F32),
        "onesf": ((128, 128), F32), "onesb": ((128, 128), BF16),
        "onehotb": ((32, 4096), BF16), "onehotf": ((32, 4096), F32),
        "colones": ((128, 1024), BF16), "cw": ((128, 256), F32),
        "cb": ((128, 64), F32), "alog": ((32, 1), F32), "dtb": ((32, 1), F32),
        "maskS": ((128, 128), F32), "maskIT": ((128, 128), F32),
        "normw": ((128, 1), F32), "b1": ((128, 64), F32), "b2": ((128, 16), F32),
    }
    cns = {}
    for nm in names:
        sh, dt = spec[nm]
        cns[nm] = (ctx.dram("d_" + nm, list(sh), dt, kind="ExternalInput"), sh, dt)
    if "wba" in names:
        pass
    return cns


_CONST_SRC = {
    "ident": "ident", "identf": "identf", "onesf": "onesf", "onesb": "onesb",
    "onehotb": "onehotb", "onehotf": "onehot", "colones": "colones",
    "cw": "cw", "cb": "cb", "alog": "alog", "dtb": "dtb",
    "maskS": "mask_strict", "maskIT": "mask_inclT", "normw": "normw",
    "b1": "b1", "b2": "b2",
}


def _const_inputs(hp, names):
    return {"d_" + nm: _np.ascontiguousarray(hp[_CONST_SRC[nm]]) for nm in names}


def _build_p0():
    ctx = Ctx("p0")
    w1 = ctx.dram("w1in", [8192, 2048], F32, kind="ExternalInput")
    w2 = ctx.dram("w2in", [8192, 2048], F32, kind="ExternalInput")
    w1q = ctx.dram("w1q", [8192, 2048], BF16, kind="ExternalOutput")
    w2q = ctx.dram("w2q", [8192, 2048], BF16, kind="ExternalOutput")
    cns = _consts_decl(ctx, ["onesf"])
    load_consts(ctx, cns)
    s1 = emit_p0(ctx, w1, w1q, 64, "wa")
    s2 = emit_p0(ctx, w2, w2q, 64, "wb")
    ctx.wait("sync", s1[0], 16 * 32)
    ctx.wait("sync", s1[1], 16 * 32)
    ctx.wait("sync", s2[0], 16 * 32)
    ctx.wait("sync", s2[1], 16 * 32)
    return ctx.finalize()


def _build_p1():
    ctx = Ctx("p1")
    x = ctx.dram("x", [2048, 2048], F32, kind="ExternalInput")
    wq = ctx.dram("wq", [96 * 16 * 128, 128], F32, kind="ExternalInput")
    cns = _consts_decl(ctx, ["ident", "onesf", "onehotb", "colones", "cw", "cb",
                             "alog", "dtb"])
    cns["wba"] = (ctx.dram("d_wba", [128, 1024], F32, kind="ExternalInput"),
                  (128, 1024), BF16, "cast")
    x_fm = ctx.dram("x_fm", [16 * 128, 2048], BF16, kind="ExternalOutput")
    q_fm = ctx.dram("q_fm", [16 * 128, 2048], BF16, kind="ExternalOutput")
    k_fm = ctx.dram("k_fm", [16 * 128, 2048], BF16, kind="ExternalOutput")
    k_tm = ctx.dram("k_tm", [16 * 16 * 128, 128], BF16, kind="ExternalOutput")
    v_tm = ctx.dram("v_tm", [16 * 32 * 128, 128], BF16, kind="ExternalOutput")
    z_fm = ctx.dram("z_fm", [32 * 128, 2048], BF16, kind="ExternalOutput")
    beta = ctx.dram("beta", [32, 2048], F32, kind="ExternalOutput")
    cumg = ctx.dram("cumg", [32, 2048], F32, kind="ExternalOutput")
    qc_scr = ctx.dram("qc_scr", [32 * 128, 2048], BF16, kind="Internal")
    load_consts(ctx, cns)
    prepare_cvals(ctx, [1e-5, 1e-6, 1.0, -1.0, 0.5, 1.0 / 128])
    h = emit_p1(ctx, x, wq, None, x_fm, q_fm, k_fm, k_tm, v_tm, z_fm,
                beta, cumg, qc_scr)
    d = emit_p1b(ctx, h, wq, q_fm, k_fm, k_tm, v_tm, z_fm, qc_scr)
    wt(ctx, "sync", d["done"])
    return ctx.finalize()


def _build_p2():
    ctx = Ctx("p2")
    q_fm = ctx.dram("q_fm", [16 * 128, 2048], BF16, kind="ExternalInput")
    k_fm = ctx.dram("k_fm", [16 * 128, 2048], BF16, kind="ExternalInput")
    k_tm = ctx.dram("k_tm", [16 * 16 * 128, 128], BF16, kind="ExternalInput")
    v_tm = ctx.dram("v_tm", [16 * 32 * 128, 128], BF16, kind="ExternalInput")
    z_fm = ctx.dram("z_fm", [32 * 128, 2048], BF16, kind="ExternalInput")
    beta = ctx.dram("beta", [32, 2048], F32, kind="ExternalInput")
    cumg = ctx.dram("cumg", [32, 2048], F32, kind="ExternalInput")
    o_norm = ctx.dram("o_norm", [32 * 128, 2048], BF16, kind="ExternalOutput")
    cns = _consts_decl(ctx, ["ident", "identf", "onesf", "onesb", "onehotf",
                             "maskS", "maskIT", "normw"])
    load_consts(ctx, cns)
    prepare_cvals(ctx, [1e-5, 1e-6, 1.0, -1.0, 0.5, 1.0 / 128])
    d = emit_p2(ctx, q_fm, k_fm, k_tm, v_tm, z_fm, beta, cumg, o_norm)
    wt(ctx, "sync", d["done"])
    return ctx.finalize()


def _build_p2b():
    ctx = Ctx("p2b")
    o_norm = ctx.dram("o_norm", [32 * 128, 2048], BF16, kind="ExternalInput")
    x_fm = ctx.dram("x_fm", [16 * 128, 2048], BF16, kind="ExternalInput")
    w_o = ctx.dram("w_o", [16 * 32 * 128, 128], F32, kind="ExternalInput")
    x2_fm = ctx.dram("x2_fm", [16 * 128, 2048], BF16, kind="ExternalOutput")
    xn2_fm = ctx.dram("xn2_fm", [16 * 128, 2048], BF16, kind="ExternalOutput")
    cns = _consts_decl(ctx, ["onesb", "onesf"])
    load_consts(ctx, cns)
    prepare_cvals(ctx, [1e-5, 1e-6, 1.0, -1.0, 0.5, 1.0 / 128])
    d = emit_p2b(ctx, o_norm, x_fm, w_o, x2_fm, xn2_fm)
    wt(ctx, "sync", d["done"])
    return ctx.finalize()


def _build_p3():
    ctx = Ctx("p3")
    xn2_fm = ctx.dram("xn2_fm", [16 * 128, 2048], BF16, kind="ExternalInput")
    x2_fm = ctx.dram("x2_fm", [16 * 128, 2048], BF16, kind="ExternalInput")
    w1q = ctx.dram("w1q", [8192, 2048], BF16, kind="ExternalInput")
    w2q = ctx.dram("w2q", [8192, 2048], BF16, kind="ExternalInput")
    out = ctx.dram("out", [2048, 2048], F32, kind="ExternalOutput")
    cns = _consts_decl(ctx, ["onesb", "onesf", "identf", "b1", "b2"])
    load_consts(ctx, cns)
    prepare_cvals(ctx, [1e-5, 1e-6, 1.0, -1.0, 0.5, 1.0 / 128])
    d = emit_p3(ctx, xn2_fm, x2_fm, w1q, w2q, out)
    wt(ctx, "sync", d["done"])
    return ctx.finalize()


def kernel(**inputs):
    from concourse.bass_utils import run_bass_kernel_spmd
    N = 8
    hp = host_prep(inputs)
    x = _np.ascontiguousarray(_np.asarray(inputs["L0_attn_x"], dtype=_np.float32))

    nc = _build_p0()
    im = {"w1in": _np.ascontiguousarray(hp["w1t"].reshape(8192, 2048)),
          "w2in": _np.ascontiguousarray(hp["w2t"].reshape(8192, 2048)),
          **_const_inputs(hp, ["onesf"])}
    r0 = run_bass_kernel_spmd(nc, [im] * N, core_ids=list(range(N))).results

    nc = _build_p1()
    base = {"wq": _np.ascontiguousarray(hp["wq"].reshape(96 * 16 * 128, 128)),
            "d_wba": hp["wba"],
            **_const_inputs(hp, ["ident", "onesf", "onehotb", "colones", "cw",
                                 "cb", "alog", "dtb"])}
    ims = [dict(base, x=_np.ascontiguousarray(x[i])) for i in range(N)]
    r1 = run_bass_kernel_spmd(nc, ims, core_ids=list(range(N))).results

    nc = _build_p2()
    cns2 = _const_inputs(hp, ["ident", "identf", "onesf", "onesb", "onehotf",
                              "maskS", "maskIT", "normw"])
    ims = [dict(cns2, q_fm=r1[i]["q_fm"], k_fm=r1[i]["k_fm"], k_tm=r1[i]["k_tm"],
                v_tm=r1[i]["v_tm"], z_fm=r1[i]["z_fm"], beta=r1[i]["beta"],
                cumg=r1[i]["cumg"]) for i in range(N)]
    r2 = run_bass_kernel_spmd(nc, ims, core_ids=list(range(N))).results

    nc = _build_p2b()
    cns3 = _const_inputs(hp, ["onesb", "onesf"])
    wof = _np.ascontiguousarray(hp["wo"].reshape(16 * 32 * 128, 128))
    ims = [dict(cns3, o_norm=r2[i]["o_norm"], x_fm=r1[i]["x_fm"], w_o=wof)
           for i in range(N)]
    r3 = run_bass_kernel_spmd(nc, ims, core_ids=list(range(N))).results

    nc = _build_p3()
    cns4 = _const_inputs(hp, ["onesb", "onesf", "identf", "b1", "b2"])
    ims = [dict(cns4, xn2_fm=r3[i]["xn2_fm"], x2_fm=r3[i]["x2_fm"],
                w1q=r0[i]["w1q"], w2q=r0[i]["w2q"]) for i in range(N)]
    r4 = run_bass_kernel_spmd(nc, ims, core_ids=list(range(N))).results

    out = _np.stack([_np.asarray(r4[i]["out"], dtype=_np.float32) for i in range(N)])
    return out


# version 06
# speedup vs baseline: 1.0085x; 1.0085x over previous
import numpy as np
from contextlib import ExitStack

import concourse.bass as bass
import concourse.mybir as mybir

F32 = mybir.dt.float32
BF16 = mybir.dt.bfloat16
AF = mybir.ActivationFunctionType
ALU = mybir.AluOpType
AX = mybir.AxisListType

ENGINES = ("sync", "scalar", "vector", "tensor", "gpsimd")

S, H = 2048, 2048
HK, HV, DK, DV = 16, 32, 128, 128
NEG = -1e30


class Sem:
    __slots__ = ("h", "n", "name")

    def __init__(self, h, name):
        self.h = h
        self.n = 0
        self.name = name


class Ctx:
    def __init__(self, name="mk"):
        self.nc = bass.Bass("TRN2", target_bir_lowering=False)
        self.es = ExitStack()
        self.q = {e: [] for e in ENGINES}
        self._semcnt = 0
        self._waited = {e: {} for e in ENGINES}
        self._sems = []
        self.phase_scope = None

    def sbuf(self, name, shape, dt=F32, scope=None):
        return (scope or self.phase_scope or self.es).enter_context(
            self.nc.sbuf_tensor(name, shape, dt))

    def psum(self, name, shape, dt=F32, scope=None):
        return (scope or self.phase_scope or self.es).enter_context(
            self.nc.psum_tensor(name, shape, dt))

    def begin_phase(self):
        self.phase_scope = ExitStack()

    def end_phase(self):
        if self.phase_scope is not None:
            self.phase_scope.close()
            self.phase_scope = None
        for eng in ENGINES:
            for sm in self._sems:
                if sm.n > 0:
                    self.wait(eng, sm)

    def sem(self, name=None):
        self._semcnt += 1
        h = self.es.enter_context(self.nc.semaphore(name or f"s{self._semcnt}"))
        sm = Sem(h, name or f"s{self._semcnt}")
        self._sems.append(sm)
        return sm

    def dram(self, name, shape, dt=F32, kind="Internal"):
        return self.nc.dram_tensor(name, shape, dt, kind=kind).ap()

    def op(self, eng, fn, inc=None, k=1):
        if inc is not None:
            inc.n += k

            def g(e, fn=fn, inc=inc, k=k):
                fn(e).then_inc(inc.h, k)
        else:
            def g(e, fn=fn):
                fn(e)
        self.q[eng].append(g)
        return inc.n if inc is not None else None

    def dma(self, eng, out, in_, inc=None):
        return self.op(eng, lambda e, out=out, in_=in_: e.dma_start(out=out, in_=in_),
                       inc=inc, k=16)

    def wait(self, eng, sem, val=None):
        v = sem.n if val is None else val
        w = self._waited[eng]
        if w.get(id(sem), -1) >= v:
            return v
        w[id(sem)] = v
        self.q[eng].append(lambda e, s=sem.h, v=v: e.wait_ge(s, v))
        return v

    def finalize(self):
        nc = self.nc
        with nc.Block() as blk:
            for eng in ENGINES:
                fns = self.q[eng]
                getattr(blk, eng)(lambda e, fns=fns: [f(e) for f in fns])
        self.es.close()
        return nc



def perm_qkvz_cols():
    idx = []
    for h in range(HK):
        idx.append(np.arange(768 * h, 768 * h + 128))
    for h in range(HK):
        idx.append(np.arange(768 * h + 128, 768 * h + 256))
    for h in range(HK):
        idx.append(np.arange(768 * h + 256, 768 * h + 512))
    for h in range(HK):
        idx.append(np.arange(768 * h + 512, 768 * h + 768))
    return np.concatenate(idx)


def perm_ba_cols():
    beta_cols = [4 * h + j for h in range(HK) for j in range(2)]
    a_cols = [4 * h + 2 + j for h in range(HK) for j in range(2)]
    return np.array(beta_cols + a_cols)


def tile_km(w, nk, nm):
    return np.ascontiguousarray(
        w.reshape(nk, 128, nm, 128).transpose(2, 0, 1, 3))


def tile_km2(w, nk, nm):
    return np.ascontiguousarray(
        w.reshape(nk, 128, nm, 128).transpose(2, 1, 0, 3))


def host_prep(inputs):
    d = {}
    wq = np.asarray(inputs["L0_attn_w_qkvz"])
    d["wq"] = tile_km(np.ascontiguousarray(wq[:, perm_qkvz_cols()]), 16, 96)
    wba = np.asarray(inputs["L0_attn_w_ba"])
    d["wba"] = np.ascontiguousarray(
        wba[:, perm_ba_cols()].reshape(16, 128, 64).transpose(1, 0, 2).reshape(128, 1024))
    d["wo"] = tile_km(np.asarray(inputs["L0_attn_w_o"]), 32, 16)
    w1 = np.asarray(inputs["L0_ffn_w1"])
    d["w1t"] = tile_km2(np.ascontiguousarray(w1.T), 16, 64)
    w2 = np.asarray(inputs["L0_ffn_w2"])
    d["w2t"] = tile_km2(np.ascontiguousarray(w2.T), 64, 16)
    cw = np.asarray(inputs["L0_attn_conv_w"])
    d["cw"] = np.ascontiguousarray(
        cw.reshape(64, 128, 4).transpose(1, 0, 2).reshape(128, 256))
    d["cb"] = np.ascontiguousarray(
        np.asarray(inputs["L0_attn_conv_b"]).reshape(64, 128).T)
    d["alog"] = np.ascontiguousarray(np.asarray(inputs["L0_attn_a_log"]).reshape(32, 1))
    d["dtb"] = np.ascontiguousarray(np.asarray(inputs["L0_attn_dt_bias"]).reshape(32, 1))
    d["normw"] = np.ascontiguousarray(np.asarray(inputs["L0_attn_norm_w"]).reshape(128, 1))
    d["b1"] = np.ascontiguousarray(np.asarray(inputs["L0_ffn_b1"]).reshape(64, 128).T)
    d["b2"] = np.ascontiguousarray(np.asarray(inputs["L0_ffn_b2"]).reshape(16, 128).T)
    import ml_dtypes
    d["ident"] = np.eye(128, dtype=ml_dtypes.bfloat16)
    d["identf"] = np.eye(128, dtype=np.float32)
    d["onesf"] = np.ones((128, 128), dtype=np.float32)
    d["onesb"] = np.ones((128, 128), dtype=ml_dtypes.bfloat16)
    oh = np.zeros((32, 32, 128), dtype=np.float32)
    for h in range(32):
        oh[h, h, :] = 1.0
    d["onehot"] = np.ascontiguousarray(oh.transpose(1, 0, 2).reshape(32, 32 * 128))
    d["onehotb"] = d["onehot"].astype(ml_dtypes.bfloat16)
    co = np.zeros((32, 128, 32), dtype=ml_dtypes.bfloat16)
    for m in range(32):
        co[m, :, m] = 1.0
    d["colones"] = np.ascontiguousarray(co.transpose(1, 0, 2).reshape(128, 32 * 32))
    i, j = np.indices((128, 128))
    d["mask_strict"] = np.where(j >= i, np.float32(1e30), np.float32(0.0))
    d["mask_inclT"] = np.where(j < i, np.float32(1e30), np.float32(0.0))
    return d



def emit_p0(ctx: Ctx, w_in, w_out, ntiles, tag):
    wt = w_in.rearrange("(n p) f -> n p f", p=128)
    wo = w_out.rearrange("(n p) f -> n p f", p=128)
    NB = 4
    bufs = [ctx.sbuf(f"{tag}_in{i}", [128, 2048], F32) for i in range(NB)]
    tern = [ctx.sbuf(f"{tag}_t{i}", [128, 2048], BF16) for i in range(2)]
    ge = ctx.sbuf(f"{tag}_ge", [128, 2048], BF16)
    le = ctx.sbuf(f"{tag}_le", [128, 2048], BF16)
    acc = ctx.sbuf(f"{tag}_acc", [128, ntiles], F32)
    red = ctx.sbuf(f"{tag}_red", [128, 1], F32)
    thr = ctx.sbuf(f"{tag}_thr", [128, 1], F32)
    nthr = ctx.sbuf(f"{tag}_nthr", [128, 1], F32)

    lds = [ctx.sem(f"{tag}ld{i}") for i in range(NB)]
    sts = [ctx.sem(f"{tag}st{i}") for i in range(2)]
    rd = ctx.sem(f"{tag}rd")
    th = ctx.sem(f"{tag}th")

    for i in range(ntiles):
        s = i % NB
        b = bufs[s]
        if i >= NB:
            ctx.wait("sync", rd, i - NB + 1)
        ctx.dma("sync", b[:], wt[i], inc=lds[s])
        ctx.wait("vector", lds[s], 16 * (i // NB + 1))
        ctx.op("vector",
               lambda e, b=b, i=i: e.tensor_reduce(
                   acc[:, i:i + 1], b[:], AX.X, ALU.add,
                   apply_absolute_value=True),
               inc=rd)
    vsem = ctx.sem(f"{tag}vs")
    ctx.wait("vector", rd, ntiles)
    ctx.op("vector", lambda e: e.tensor_reduce(red[:], acc[:, 0:ntiles], AX.X, ALU.add),
           inc=vsem)
    onesf = ctx.consts["onesf"]
    pab = ctx.psum(f"{tag}_pab", [128, 1024], F32)
    pa = pab[0:1, 0:1]
    pb = pab[:, 512:513]
    s1 = ctx.sbuf(f"{tag}_s1", [1, 1], F32)
    msem = ctx.sem(f"{tag}ms")
    ctx.wait("tensor", vsem, 1)
    ctx.op("tensor", lambda e: e.matmul(pa, onesf[:, 0:1], red[:], start=True, stop=True), inc=msem)
    ctx.wait("vector", msem, 1)
    ctx.op("vector", lambda e: e.tensor_copy(s1[:], pa), inc=vsem)
    ctx.wait("tensor", vsem, 2)
    ctx.op("tensor", lambda e: e.matmul(pb, onesf[0:1, :], s1[:], start=True, stop=True), inc=msem)
    nel = float(ntiles * 128 * 2048)
    ctx.wait("vector", msem, 2)
    ctx.op("vector", lambda e: e.tensor_scalar(thr[:], pb, 0.5 / nel, 0.5e-8, ALU.mult, ALU.max), inc=th)
    ctx.wait("scalar", th, 1)
    ctx.op("scalar", lambda e: e.mul(nthr[:], thr[:], -1.0), inc=th)
    ctx.wait("gpsimd", th, 2)

    for i in range(ntiles):
        s = i % NB
        b = bufs[s]
        ctx.wait("sync", rd, ntiles + max(0, i - NB + 1))
        ctx.dma("sync", b[:], wt[i], inc=lds[s])
        t = tern[i % 2]
        ctx.wait("gpsimd", lds[s], 16 * ((ntiles + i) // NB + 1))
        if i >= 2:
            ctx.wait("gpsimd", sts[i % 2], 16 * ((i - 2) // 2 + 1))
        ctx.op("gpsimd", lambda e, b=b: e.tensor_single_scalar(ge[:], b[:], thr[:], ALU.is_ge))
        ctx.op("gpsimd", lambda e, b=b: e.tensor_single_scalar(le[:], b[:], nthr[:], ALU.is_le))
        ctx.op("gpsimd", lambda e, t=t: e.tensor_tensor(t[:], ge[:], le[:], ALU.subtract), inc=rd)
        ctx.wait("scalar", rd, ntiles + i + 1)
        ctx.dma("scalar", wo[i], t[:], inc=sts[i % 2])
    ctx._dbg_thr = thr
    ctx._dbg_acc = acc
    ctx._dbg_red = red
    ctx._dbg_s1 = s1
    ctx._dbg_ge = ge
    ctx._dbg_le = le
    return sts


def prepare_cvals(ctx: Ctx, values):
    tile = ctx.sbuf("cvals", [128, max(1, len(values))], F32)
    sem = ctx.sem("cvals")
    for i, v in enumerate(values):
        ctx.op("gpsimd", lambda e, i=i, v=v: e.memset(tile[:, i:i + 1], float(v)),
               inc=sem)
    for eng in ("vector", "scalar", "tensor", "gpsimd"):
        ctx.wait(eng, sem, len(values))
    ctx.cvals = {float(v): i for i, v in enumerate(values)}
    ctx.cvtile = tile


def cv(ctx, v, p=128):
    return ctx.cvtile[0:p, ctx.cvals[float(v)]:ctx.cvals[float(v)] + 1]


def load_consts(ctx: Ctx, dram_consts):
    if not hasattr(ctx, "consts") or ctx.consts is None:
        ctx.consts = {}
    else:
        ctx.consts = dict(ctx.consts)
    csem = ctx.sem()
    gsem = ctx.sem()
    n = g = 0
    for name, spec in dram_consts.items():
        ap, shape, dt = spec[0], spec[1], spec[2]
        cast = len(spec) > 3 and spec[3] == "cast"
        t = ctx.sbuf(f"c_{name}", list(shape), dt)
        if cast:
            ctx.dma("gpsimd", t[:], ap, inc=gsem)
            g += 16
        else:
            ctx.dma("sync", t[:], ap, inc=csem)
            n += 16
        ctx.consts[name] = t
    for eng in ("vector", "scalar", "tensor", "gpsimd", "sync"):
        if n:
            ctx.wait(eng, csem, n)
        if g:
            ctx.wait(eng, gsem, g)
    return csem



def wt(ctx, eng, rec):
    if rec is None:
        return
    if isinstance(rec, list):
        for r in rec:
            wt(ctx, eng, r)
        return
    ctx.wait(eng, rec[0], rec[1])


def emit_p1(ctx, x, wq, wba, x_fm, q_fm, k_fm, k_tm, v_tm, z_fm,
            beta_o, cumg_o, qc_scr):
    identb = ctx.consts["ident"]
    wba_s = ctx.consts["wba"]

    xt = x.rearrange("(n p) d -> n p d", p=128)
    xn_all = ctx.sbuf("p1xn", [128, 16 * 2048], BF16)
    es_a = ExitStack()
    xbufs = [ctx.sbuf(f"p1x{i}", [128, 2048], F32, scope=es_a) for i in range(2)]
    xlds = [ctx.sem(f"p1xld{i}") for i in range(2)]
    xstage = [ctx.sbuf(f"p1xs{i}", [128, 2048], BF16, scope=es_a) for i in range(2)]
    xnbf = ctx.sbuf("p1xnbf", [128, 2048], BF16, scope=es_a)
    xsq = ctx.sbuf("p1xsq", [128, 2048], F32, scope=es_a)
    xbf = ctx.sbuf("p1xbf", [128, 2048], BF16, scope=es_a)
    stats = ctx.sbuf("p1st", [128, 8], F32, scope=es_a)
    ptr = [es_a.enter_context(ctx.nc.psum_tensor(f"p1pt{i}", [128, 1024], BF16))
           for i in range(4)]
    vch = ctx.sem("p1vch")
    ach = ctx.sem("p1ach")
    tch = ctx.sem("p1tch")
    xnr = ctx.sem("p1xnr")
    xst = [ctx.sem(f"p1xst{i}") for i in range(2)]
    rel_x = [None, None]
    rel_pt = [None] * 4
    rel_xbf = None
    rel_xnbf = None
    rel_stage = [None, None]
    for tt in range(16):
        sl = tt % 2
        b = xbufs[sl]
        wt(ctx, "sync", rel_x[sl])
        ctx.dma("sync", b[:], xt[tt], inc=xlds[sl])
        ctx.wait("vector", xlds[sl], 16 * (tt // 2 + 1))
        ctx.wait("scalar", xlds[sl], 16 * (tt // 2 + 1))
        wt(ctx, "scalar", rel_xbf)
        ctx.op("scalar", lambda e, b=b: e.activation(xbf[:], b[:], AF.Identity), inc=ach)
        xbf_ready = ("scalar", ach.n)
        ctx.op("vector", lambda e, b=b: e.tensor_reduce(stats[:, 0:1], b[:], AX.X, ALU.add), inc=vch)
        ctx.op("vector", lambda e, b=b: e.tensor_tensor(xsq[:], b[:], b[:], ALU.mult), inc=vch)
        ctx.wait("vector", vch, vch.n)
        ctx.op("vector", lambda e: e.tensor_reduce(stats[:, 1:2], xsq[:], AX.X, ALU.add), inc=vch)
        ctx.wait("vector", vch, vch.n)
        ctx.op("vector", lambda e: e.tensor_scalar(stats[:, 2:3], stats[:, 0:1], 1.0 / 2048, None, ALU.mult), inc=vch)
        ctx.op("vector", lambda e: e.tensor_scalar(stats[:, 3:4], stats[:, 1:2], 1.0 / 2048, None, ALU.mult), inc=vch)
        ctx.wait("vector", vch, vch.n)
        ctx.op("vector", lambda e: e.tensor_tensor(stats[:, 4:5], stats[:, 2:3], stats[:, 2:3], ALU.mult), inc=vch)
        ctx.wait("vector", vch, vch.n)
        ctx.op("vector", lambda e: e.tensor_tensor(stats[:, 5:6], stats[:, 3:4], stats[:, 4:5], ALU.subtract), inc=vch)
        ctx.wait("scalar", vch, vch.n)
        ctx.op("scalar", lambda e: e.activation(stats[:, 6:7], stats[:, 5:6], AF.Sqrt, bias=cv(ctx, 1e-5)), inc=ach)
        ctx.wait("vector", ach, ach.n)
        ctx.op("vector", lambda e: e.reciprocal(stats[:, 7:8], stats[:, 6:7]), inc=vch)
        ctx.wait("vector", vch, vch.n)
        wt(ctx, "vector", rel_xnbf)
        ctx.op("vector", lambda e, b=b: e.tensor_scalar(
            xnbf[:], b[:], stats[:, 2:3], stats[:, 7:8], ALU.subtract, ALU.mult), inc=vch)
        rel_x[sl] = [(vch, vch.n), (ach, xbf_ready[1])]
        xn_ready = vch.n
        ctx.wait("tensor", ach, xbf_ready[1])
        wt(ctx, "vector", rel_stage[sl])
        for g in range(2):
            pt = ptr[g]
            wt(ctx, "tensor", rel_pt[g])
            for c8 in range(8):
                c = g * 8 + c8
                ctx.op("tensor", lambda e, pt=pt, c8=c8, c=c: e.transpose(
                    pt[:, c8 * 128:(c8 + 1) * 128], xbf[:, c * 128:(c + 1) * 128],
                    identb[:]), inc=(tch if c8 == 7 else None))
            ctx.wait("vector", tch, tch.n)
            ctx.op("vector", lambda e, pt=pt, g=g, sl=sl: e.tensor_copy(
                xstage[sl][:, g * 1024:(g + 1) * 1024], pt[:]), inc=vch)
            rel_pt[g] = (vch, vch.n)
        rel_xbf = (tch, tch.n)
        ctx.wait("tensor", vch, xn_ready)
        for g in range(2):
            pt = ptr[2 + g]
            wt(ctx, "tensor", rel_pt[2 + g])
            for c8 in range(8):
                c = g * 8 + c8
                ctx.op("tensor", lambda e, pt=pt, c8=c8, c=c: e.transpose(
                    pt[:, c8 * 128:(c8 + 1) * 128], xnbf[:, c * 128:(c + 1) * 128],
                    identb[:]), inc=(tch if c8 == 7 else None))
            ctx.wait("vector", tch, tch.n)
            def cpxn(e, pt=pt, g=g, tt=tt):
                dst = xn_all[:].rearrange("p (c t) -> p c t", c=16)[
                    :, g * 8:(g + 1) * 8, tt * 128:(tt + 1) * 128]
                return e.tensor_copy(dst, pt[:].rearrange("p (c t) -> p c t", c=8))
            ctx.op("vector", cpxn, inc=vch)
            ctx.op("vector", lambda e: None, inc=None)
            ctx.q["vector"].pop()
            rel_pt[2 + g] = (vch, vch.n)
            xnr.n += 0
        rel_xnbf = (tch, tch.n)
        ctx.wait("scalar", vch, vch.n)
        def st_x(e, sl=sl, tt=tt):
            dst = x_fm.rearrange("(c p) t -> p c t", p=128)[:, :, tt * 128:(tt + 1) * 128]
            return e.dma_start(out=dst, in_=xstage[sl][:].rearrange("p (c t) -> p c t", c=16))
        ctx.op("scalar", st_x, inc=xst[sl], k=16)
        rel_stage[sl] = (xst[sl], 16 * (tt // 2 + 1))
    xn_done_rec = (vch, vch.n)

    es_a.close()
    p1a_done = [xn_done_rec, (ach, ach.n), (tch, tch.n),
                (xst[0], xst[0].n), (xst[1], xst[1].n)]
    mainA = ctx.psum("p1mA", [128, 1024], F32)
    mainB = ctx.psum("p1mB", [128, 1024], F32)
    es_stats = ExitStack()
    pstat = es_stats.enter_context(ctx.nc.psum_tensor("p1ps", [32, 2048], F32))
    pmm = ctx.sem("p1pmm")
    alog = ctx.consts["alog"]
    dtb = ctx.consts["dtb"]
    es_ba = ExitStack()
    wt(ctx, "tensor", p1a_done)
    for k in range(16):
        for n in range(4):
            pb = (mainA if n < 2 else mainB)[0:64, (n % 2) * 512:(n % 2) * 512 + 512]
            ctx.op("tensor", lambda e, pb=pb, k=k, n=n: e.matmul(
                pb, wba_s[:, k * 64:(k + 1) * 64],
                xn_all[:, k * 2048 + n * 512: k * 2048 + (n + 1) * 512],
                start=(k == 0), stop=(k == 15)),
                inc=(pmm if (k == 15 and n == 3) else None))
    ba_f = ctx.sbuf("p1baf", [64, 2048], F32, scope=es_ba)
    ctx.wait("scalar", pmm, pmm.n)
    for n in range(4):
        pb = (mainA if n < 2 else mainB)[0:64, (n % 2) * 512:(n % 2) * 512 + 512]
        ctx.op("scalar", lambda e, pb=pb, n=n: e.copy(ba_f[:, n * 512:(n + 1) * 512], pb),
               inc=ach)
    rel_mainA = ("x", ach, ach.n)[1:]
    rel_mainB = (ach, ach.n)
    bsb = ctx.sbuf("p1beta", [32, 2048], F32, scope=es_ba)
    ctx.wait("scalar", ach, ach.n)
    ctx.op("scalar", lambda e: e.activation(bsb[:], ba_f[0:32, :], AF.Sigmoid), inc=ach)
    beta_rec = (ach, ach.n)
    a_sb = ctx.sbuf("p1a", [32, 2048], F32, scope=es_ba)
    t_sb = ctx.sbuf("p1t", [32, 2048], F32, scope=es_ba)
    s_sb = a_sb
    gsb = ba_f[0:32, :]
    cgsb = ba_f[32:64, :]
    ones32 = ctx.sbuf("p1o32", [32, 2048], F32, scope=es_ba)
    nea = ctx.sbuf("p1nea", [32, 1], F32, scope=es_ba)
    ones_rdy = ctx.sem("p1or")
    wt(ctx, "gpsimd", p1a_done)
    ctx.op("gpsimd", lambda e: e.memset(ones32[:], 1.0), inc=ones_rdy)
    ctx.op("scalar", lambda e: e.activation(nea[:], alog[:], AF.Exp), inc=ach)
    ctx.wait("vector", ach, ach.n)
    ctx.op("vector", lambda e: e.tensor_scalar(nea[:], nea[:], -1.0, None, ALU.mult), inc=vch)
    nea_rec = (vch, vch.n)
    ctx.op("vector", lambda e: e.tensor_copy(a_sb[:], ba_f[32:64, :]), inc=vch)
    ctx.wait("vector", vch, vch.n)
    ctx.op("vector", lambda e: e.tensor_scalar(s_sb[:], a_sb[:], dtb[:], None, ALU.add), inc=vch)
    ctx.wait("scalar", vch, vch.n)
    ctx.op("scalar", lambda e: e.activation(t_sb[:], s_sb[:], AF.Abs), inc=ach)
    ctx.wait("scalar", ach, ach.n)
    ctx.op("scalar", lambda e: e.activation(t_sb[:], t_sb[:], AF.Exp, scale=cv(ctx, -1.0, 32)), inc=ach)
    ctx.wait("scalar", ach, ach.n)
    ctx.op("scalar", lambda e: e.activation(t_sb[:], t_sb[:], AF.Ln, bias=cv(ctx, 1.0, 32)), inc=ach)
    wt(ctx, "vector", beta_rec)
    ctx.op("vector", lambda e: e.tensor_scalar(gsb[:], s_sb[:], 0.0, None, ALU.max), inc=vch)
    ctx.wait("vector", ach, ach.n)
    ctx.wait("vector", vch, vch.n)
    ctx.op("vector", lambda e: e.tensor_tensor(gsb[:], gsb[:], t_sb[:], ALU.add), inc=vch)
    ctx.wait("vector", vch, vch.n)
    wt(ctx, "vector", nea_rec)
    ctx.op("vector", lambda e: e.tensor_scalar(gsb[:], gsb[:], nea[:], None, ALU.mult), inc=vch)
    ctx.wait("vector", vch, vch.n)
    ctx.wait("vector", ones_rdy, 1)
    ctx.op("vector", lambda e: e.tensor_tensor_scan(
        cgsb[:], ones32[:], gsb[:], 0.0, ALU.mult, ALU.add), inc=vch)
    bgst = ctx.sem("p1bg")
    wt(ctx, "sync", beta_rec)
    ctx.wait("sync", vch, vch.n)
    ctx.dma("sync", beta_o, bsb[:], inc=bgst)
    ctx.dma("sync", cumg_o, cgsb[:], inc=bgst)
    if getattr(ctx, "dbg_g", None) is not None:
        ctx.dma("sync", ctx.dbg_g, gsb, inc=bgst)
        ctx.dma("sync", ctx.dbg_ones, ones32[:], inc=bgst)
    return dict(vch=vch, ach=ach, tch=tch, pmm=pmm,
                rel_mainA=rel_mainA, rel_mainB=rel_mainB,
                mainA=mainA, mainB=mainB, pstat=pstat, es_stats=es_stats,
                xn_all=xn_all, xn_done_rec=xn_done_rec, bgst=bgst, es_ba=es_ba)


def emit_p1b(ctx, h, wq, q_fm, k_fm, k_tm, v_tm, z_fm, qc_scr):
    identb = ctx.consts["ident"]
    onehotb = ctx.consts["onehotb"]
    colones = ctx.consts["colones"]
    cw = ctx.consts["cw"]
    cb = ctx.consts["cb"]
    vch, ach, tch, pmm = h["vch"], h["ach"], h["tch"], h["pmm"]
    mainA, mainB, pstat = h["mainA"], h["mainB"], h["pstat"]
    xn_all = h["xn_all"]

    h["es_ba"].close()
    wqt = wq.rearrange("(m k p) f -> m p k f", m=96, k=16)
    wbuf = [ctx.sbuf(f"p1w{i}", [128, 2048], BF16) for i in range(2)]
    wlds = [ctx.sem(f"p1wld{i}") for i in range(2)]
    qcf = ctx.sbuf("p1qcf", [128, 2048], BF16)
    accb = ctx.sbuf("p1acc", [128, 2048], BF16)
    qcb = [ctx.sbuf(f"p1qcb{i}", [128, 2048], BF16) for i in range(2)]
    sqb = [ctx.sbuf(f"p1sqb{i}", [128, 2048], BF16) for i in range(2)]
    vstage = ctx.sbuf("p1vst", [128, 2048], BF16)
    qcst = [ctx.sem(f"p1qst{i}") for i in range(2)]
    vtst = ctx.sem("p1vts")
    statdone = ctx.sem("p1sd")
    vt_evac = ctx.sem("p1vte")
    rstd_f = ctx.sbuf("p1rstd", [32, 2048], F32)
    rstd_b = ctx.sbuf("p1rstdb", [32, 2048], BF16)

    state = dict(rel_qc=[None, None], rel_sq=[None, None], rel_qcf=None,
                 rel_acc=None, rel_vtp=None, rel_vstage=None,
                 rel_w=[None, None], rel_mainA=h["rel_mainA"],
                 rel_mainB=h["rel_mainB"], rstd_rec=None, rstdb_rec=None)
    pe_post = []
    vtp_holder = {}

    def emit_rstd():
        ctx.wait("scalar", pmm, pmm.n)
        ctx.op("scalar", lambda e: e.activation(rstd_f[:], pstat[:], AF.Sqrt, bias=cv(ctx, 1e-6, 32)),
               inc=statdone)
        ctx.wait("vector", statdone, 1)
        ctx.op("vector", lambda e: e.reciprocal(rstd_f[:], rstd_f[:]), inc=vch)
        ctx.wait("vector", vch, vch.n)
        ctx.op("vector", lambda e: e.tensor_copy(rstd_b[:], rstd_f[:]), inc=vch)
        state["rstdb_rec"] = (vch, vch.n)

    for m in range(96):
        sl = m % 2
        wb = wbuf[sl]
        if m == 0:
            ctx.wait("gpsimd", h["bgst"], h["bgst"].n)
        wt(ctx, "gpsimd", state["rel_w"][sl])
        ctx.op("gpsimd", lambda e, wb=wb, m=m: e.dma_start(
            out=wb[:].rearrange("p (k f) -> p k f", k=16), in_=wqt[m]),
            inc=wlds[sl], k=16)
        ctx.wait("tensor", wlds[sl], 16 * (m // 2 + 1))
        if m == 0:
            wt(ctx, "tensor", h["xn_done_rec"])
        for half in range(2):
            pmain = mainA if half == 0 else mainB
            wt(ctx, "tensor", state["rel_mainA" if half == 0 else "rel_mainB"])
            for k in range(16):
                for n2 in range(2):
                    n = half * 2 + n2
                    ctx.op("tensor", lambda e, pmain=pmain, wb=wb, k=k, n2=n2, n=n: e.matmul(
                        pmain[:, n2 * 512:(n2 + 1) * 512],
                        wb[:, k * 128:(k + 1) * 128],
                        xn_all[:, k * 2048 + n * 512: k * 2048 + (n + 1) * 512],
                        start=(k == 0), stop=(k == 15)),
                        inc=(pmm if (k == 15 and n2 == 1) else None))
        state["rel_w"][sl] = (pmm, pmm.n)
        mm_half = [pmm.n - 1, pmm.n]
        for fn in pe_post:
            fn()
        pe_post = []
        if m == 32:
            emit_rstd()
            h["es_stats"].close()
            vtp_holder["vtp"] = ctx.psum("p1vtp", [128, 2048], BF16)

        if m < 64:
            qc = qcb[sl]
            ctx.wait("scalar", pmm, mm_half[0])
            wt(ctx, "scalar", state["rel_qcf"])
            ctx.op("scalar", lambda e: e.copy(qcf[:, 0:1024], mainA[:]), inc=ach)
            state["rel_mainA"] = (ach, ach.n)
            ctx.wait("scalar", pmm, mm_half[1])
            ctx.op("scalar", lambda e: e.copy(qcf[:, 1024:2048], mainB[:]), inc=ach)
            state["rel_mainB"] = (ach, ach.n)
            qcf_rdy = ach.n
            ctx.wait("vector", ach, qcf_rdy)
            wt(ctx, "vector", state["rel_acc"])
            cwm = [cw[:, m * 4 + j: m * 4 + j + 1] for j in (3, 2, 1, 0)]
            ctx.op("vector", lambda e, c=cwm[0]: e.tensor_scalar(accb[:], qcf[:], c, None, ALU.mult), inc=vch)
            ctx.op("vector", lambda e, c=cwm[1]: e.scalar_tensor_tensor(
                accb[:, 1:2048], qcf[:, 0:2047], c, accb[:, 1:2048], ALU.mult, ALU.add), inc=vch)
            ctx.op("vector", lambda e, c=cwm[2]: e.scalar_tensor_tensor(
                accb[:, 2:2048], qcf[:, 0:2046], c, accb[:, 2:2048], ALU.mult, ALU.add), inc=vch)
            ctx.op("vector", lambda e, c=cwm[3]: e.scalar_tensor_tensor(
                accb[:, 3:2048], qcf[:, 0:2045], c, accb[:, 3:2048], ALU.mult, ALU.add), inc=vch)
            state["rel_qcf"] = (vch, vch.n)
            ctx.wait("scalar", vch, vch.n)
            wt(ctx, "scalar", state["rel_qc"][sl])
            ctx.op("scalar", lambda e, qc=qc, m=m: e.activation(
                qc[:], accb[:], AF.Silu, bias=cb[:, m:m + 1]), inc=ach)
            state["rel_acc"] = (ach, ach.n)
            silu_rdy = ach.n
            if m < 32:
                sq = sqb[sl]
                ctx.wait("sync", ach, silu_rdy)
                ctx.dma("sync", qc_scr.rearrange("(m p) t -> m p t", p=128)[m], qc[:],
                        inc=qcst[sl])
                wt(ctx, "scalar", state["rel_sq"][sl])
                ctx.op("scalar", lambda e, qc=qc, sq=sq: e.activation(sq[:], qc[:], AF.Square),
                       inc=ach)
                sq_rdy = ach.n
                state["rel_qc"][sl] = [(qcst[sl], 16 * ((m // 2) + 1)), (ach, sq_rdy)]

                def stats_mms(m=m, sq=sq, sl=sl, sq_rdy=sq_rdy):
                    ctx.wait("tensor", ach, sq_rdy)
                    for n in range(4):
                        ctx.op("tensor", lambda e, m=m, sq=sq, n=n: e.matmul(
                            pstat[:, n * 512:(n + 1) * 512],
                            colones[:, m * 32:(m + 1) * 32],
                            sq[:, n * 512:(n + 1) * 512],
                            start=(m == 0), stop=(m == 31)),
                            inc=(pmm if n == 3 else None))
                    state["rel_sq"][sl] = (pmm, pmm.n)
                pe_post.append(stats_mms)
            else:
                vh = m - 32

                def v_post(m=m, vh=vh, silu_rdy=silu_rdy, qc=qc, sl=sl):
                    vtp = vtp_holder["vtp"]
                    ctx.wait("tensor", ach, silu_rdy)
                    ctx.wait("tensor", statdone, 1)
                    wt(ctx, "tensor", state["rel_vtp"])
                    for tc in range(16):
                        ctx.op("tensor", lambda e, tc=tc, qc=qc, vtp=vtp: e.transpose(
                            vtp[:, tc * 128:(tc + 1) * 128],
                            qc[:, tc * 128:(tc + 1) * 128], identb[:]),
                            inc=(tch if tc == 15 else None))
                    state["rel_qc"][sl] = (tch, tch.n)
                    ctx.wait("vector", tch, tch.n)
                    wt(ctx, "vector", state["rel_vstage"])
                    ctx.op("vector", lambda e, vtp=vtp: e.tensor_copy(vstage[:], vtp[:]),
                           inc=vt_evac)
                    state["rel_vtp"] = (vt_evac, vt_evac.n)
                    ctx.wait("sync", vt_evac, vt_evac.n)

                    def st_v(e, vh=vh):
                        dst = v_tm.rearrange("(tc v t) d -> v t tc d", tc=16, v=32, t=128)[vh]
                        return e.dma_start(
                            out=dst, in_=vstage[:].rearrange("p (tc d) -> p tc d", tc=16))
                    ctx.op("sync", st_v, inc=vtst, k=16)
                    state["rel_vstage"] = (vtst, vtst.n)
                pe_post.append(v_post)
        else:
            zh = m - 64
            qc = qcb[sl]
            ctx.wait("scalar", pmm, mm_half[0])
            wt(ctx, "scalar", state["rel_qc"][sl])
            ctx.op("scalar", lambda e, qc=qc: e.copy(qc[:, 0:1024], mainA[:]), inc=ach)
            state["rel_mainA"] = (ach, ach.n)
            ctx.wait("scalar", pmm, mm_half[1])
            ctx.op("scalar", lambda e, qc=qc: e.copy(qc[:, 1024:2048], mainB[:]), inc=ach)
            state["rel_mainB"] = (ach, ach.n)
            ctx.wait("sync", ach, ach.n)
            ctx.dma("sync", z_fm.rearrange("(z p) t -> z p t", p=128)[zh], qc[:],
                    inc=qcst[sl])
            state["rel_qc"][sl] = (qcst[sl], 16 * (16 + zh // 2 + 1))
    for fn in pe_post:
        fn()

    qa = [ctx.sbuf(f"p1qa{i}", [128, 2048], BF16) for i in range(2)]
    qh = [ctx.sbuf(f"p1qh{i}", [128, 2048], BF16) for i in range(2)]
    kst = ctx.sem("p1kst")
    qlds = [ctx.sem(f"p1qld{i}") for i in range(2)]
    qsts = [ctx.sem(f"p1qsts{i}") for i in range(2)]
    rel_qa = [None, None]
    rel_qh = [None, None]
    DKs = float(DK) ** -0.5
    vtp = vtp_holder["vtp"]
    for m in range(32):
        sl = m % 2
        a = qa[sl]
        qout = qh[sl]
        wt(ctx, "sync", rel_qa[sl])
        ctx.dma("sync", a[:], qc_scr.rearrange("(m p) t -> m p t", p=128)[m], inc=qlds[sl])
        ctx.wait("tensor", qlds[sl], 16 * (m // 2 + 1))
        wt(ctx, "tensor", state["rstdb_rec"])
        for n in range(4):
            pmain = mainA if n < 2 else mainB
            wt(ctx, "tensor", state["rel_mainA" if n < 2 else "rel_mainB"])
            ctx.op("tensor", lambda e, pmain=pmain, n=n, m=m: e.matmul(
                pmain[:, (n % 2) * 512:(n % 2) * 512 + 512],
                onehotb[:, m * 128:(m + 1) * 128],
                rstd_b[:, n * 512:(n + 1) * 512],
                start=True, stop=True), inc=(pmm if n == 3 else None))
        ctx.wait("vector", pmm, pmm.n)
        wt(ctx, "vector", rel_qh[sl])
        cmul = DKs if m < 16 else 1.0
        for n in range(4):
            pmain = mainA if n < 2 else mainB
            ctx.op("vector", lambda e, pmain=pmain, n=n, a=a, qout=qout, cmul=cmul:
                   e.scalar_tensor_tensor(
                       qout[:, n * 512:(n + 1) * 512], a[:, n * 512:(n + 1) * 512],
                       cmul, pmain[:, (n % 2) * 512:(n % 2) * 512 + 512],
                       ALU.mult, ALU.mult), inc=vch)
        rel_qa[sl] = (vch, vch.n)
        state["rel_mainA"] = (vch, vch.n)
        state["rel_mainB"] = (vch, vch.n)
        ctx.wait("sync", vch, vch.n)
        if m < 16:
            ctx.dma("sync", q_fm.rearrange("(h p) t -> h p t", p=128)[m], qout[:],
                    inc=qsts[sl])
            rel_qh[sl] = (qsts[sl], 16 * (m // 2 + 1))
        else:
            kh = m - 16
            ctx.dma("sync", k_fm.rearrange("(h p) t -> h p t", p=128)[kh], qout[:],
                    inc=qsts[sl])
            ctx.wait("tensor", vch, vch.n)
            wt(ctx, "tensor", state["rel_vtp"])
            for tc in range(16):
                ctx.op("tensor", lambda e, tc=tc, qout=qout: e.transpose(
                    vtp[:, tc * 128:(tc + 1) * 128], qout[:, tc * 128:(tc + 1) * 128],
                    identb[:]), inc=(tch if tc == 15 else None))
            rel_qh[sl] = [(qsts[sl], 16 * (m // 2 + 1)), (tch, tch.n)]
            ctx.wait("vector", tch, tch.n)
            wt(ctx, "vector", state["rel_vstage"])
            ctx.op("vector", lambda e: e.tensor_copy(vstage[:], vtp[:]), inc=vt_evac)
            state["rel_vtp"] = (vt_evac, vt_evac.n)
            ctx.wait("scalar", vt_evac, vt_evac.n)

            def st_k(e, kh=kh):
                dst = k_tm.rearrange("(tc h t) d -> h t tc d", tc=16, h=16, t=128)[kh]
                return e.dma_start(out=dst,
                                   in_=vstage[:].rearrange("p (tc d) -> p tc d", tc=16))
            ctx.op("scalar", st_k, inc=kst, k=16)
            state["rel_vstage"] = (kst, kst.n)
    return dict(qsts=qsts, kst=kst, vtst=vtst, qcst=qcst,
                done=[(qsts[0], qsts[0].n), (qsts[1], qsts[1].n),
                      (kst, kst.n), (vtst, vtst.n),
                      (qcst[0], qcst[0].n), (qcst[1], qcst[1].n),
                      (h["bgst"], h["bgst"].n)])



def emit_p2(ctx, q_fm, k_fm, k_tm, v_tm, z_fm, beta_i, cumg_i, o_norm, nch=16, nvh=32, stage=9):
    identb = ctx.consts["ident"]
    identf = ctx.consts["identf"]
    onesf = ctx.consts["onesf"]
    onehotf = ctx.consts["onehotf"]
    maskS = ctx.consts["maskS"]
    maskIT = ctx.consts["maskIT"]
    normw = ctx.consts["normw"]

    vch = ctx.sem("p2v")
    ach = ctx.sem("p2a")
    tch = ctx.sem("p2t")

    S_all = ctx.sbuf("p2S", [128, 32 * 128], F32)
    beta_sb = ctx.sbuf("p2beta", [32, 2048], F32)
    cumg_sb = ctx.sbuf("p2cumg", [32, 2048], F32)
    sinit = ctx.sem("p2si")
    ctx.op("gpsimd", lambda e: e.memset(S_all[:], 0.0), inc=sinit)
    ctx.wait("vector", sinit, 1)
    ost_init = ctx.sem("p2oi")
    ldrows = ctx.sem("p2ldr")
    ctx.dma("sync", beta_sb[:], beta_i, inc=ldrows)
    ctx.dma("sync", cumg_sb[:], cumg_i, inc=ldrows)

    qc_b = [ctx.sbuf(f"p2q{i}", [128, 2048], BF16) for i in range(2)]
    kc_b = [ctx.sbuf(f"p2k{i}", [128, 2048], BF16) for i in range(2)]
    kt_b = [ctx.sbuf(f"p2kt{i}", [128, 2048], BF16) for i in range(2)]
    vt_b = [ctx.sbuf(f"p2vt{i}", [128, 4096], BF16) for i in range(2)]
    zc_b = [ctx.sbuf(f"p2z{i}", [128, 4096], BF16) for i in range(2)]
    lds = [[ctx.sem(f"p2ld{i}_{j}") for j in range(5)] for i in range(2)]
    rel_in = [None, None]

    rows = ctx.sbuf("p2rows", [32, 5 * 128], F32)
    rowsT = ctx.sbuf("p2rowsT", [128, 5 * 32], F32)
    tD = ctx.sbuf("p2tD", [128, 256], F32)
    eD = ctx.sbuf("p2eD", [128, 256], F32)
    n_pp = [ctx.sbuf(f"p2n{i}", [128, 128], BF16) for i in range(2)]
    nt_pow = ctx.sbuf("p2ntp", [128, 7 * 128], BF16)
    aq_bf = ctx.sbuf("p2aq", [128, 128], BF16)
    x_bf = [ctx.sbuf(f"p2x{i}", [128, 256], BF16) for i in range(2)]
    sbf = ctx.sbuf("p2sbf", [128, 128], BF16)
    wfm = ctx.sbuf("p2wfm", [128, 128], BF16)
    dl_bf = ctx.sbuf("p2dl", [128, 128], BF16)
    dls_bf = ctx.sbuf("p2dls", [128, 128], BF16)
    t1 = ctx.sbuf("p2t1", [128, 128], F32)
    otm = ctx.sbuf("p2otm", [128, 128], BF16)
    zs = ctx.sbuf("p2zs", [128, 128], F32)
    t2 = ctx.sbuf("p2t2", [128, 128], F32)
    sq2 = ctx.sbuf("p2sq2", [128, 128], BF16)
    rrow = ctx.sbuf("p2rr", [1, 128], F32)
    ecc = ctx.sbuf("p2ec", [128, 1], F32)
    ostage = [ctx.sbuf(f"p2os{i}", [128, 4096], BF16) for i in range(2)]
    for i in range(2):
        ctx.op("gpsimd", lambda e, i=i: e.memset(ostage[i][:], 0.0), inc=ost_init)
    ctx.wait("vector", ost_init, 2)

    B0 = ctx.psum("p2B0", [128, 512], F32)
    B1 = ctx.psum("p2B1", [128, 512], F32)
    B2 = ctx.psum("p2B2", [128, 512], F32)
    B3 = ctx.psum("p2B3", [128, 512], F32)
    B4 = ctx.psum("p2B4", [128, 512], F32)
    B5 = ctx.psum("p2B5", [128, 512], F32)
    B6 = ctx.psum("p2B6", [128, 512], F32)
    B7 = ctx.psum("p2B7", [128, 1024], BF16)

    rel = {k: None for k in
           ["B0", "B1", "B2", "B3", "B4", "B5", "B6", "B7",
            "rows", "rowsT", "tD", "eD", "aq", "sbf", "wfm", "dl", "dls",
            "t1", "otm", "zs", "t2", "sq2", "rrow", "ecc", "ntpow"]}
    rel["npp"] = [None, None]
    rel["x"] = [None, None]
    rel["ost"] = [None, None]
    ost_sem = ctx.sem("p2ost")

    for tc in range(nch):
        sl = tc % 2
        wt(ctx, "sync", rel_in[sl])
        ctx.op("sync", lambda e, sl=sl, tc=tc: e.dma_start(
            out=qc_b[sl][:].rearrange("p (h t) -> p h t", h=16),
            in_=q_fm.rearrange("(h p) t -> p h t", p=128)[:, :, tc * 128:(tc + 1) * 128]),
            inc=lds[sl][0], k=16)
        ctx.op("sync", lambda e, sl=sl, tc=tc: e.dma_start(
            out=kc_b[sl][:].rearrange("p (h t) -> p h t", h=16),
            in_=k_fm.rearrange("(h p) t -> p h t", p=128)[:, :, tc * 128:(tc + 1) * 128]),
            inc=lds[sl][1], k=16)
        ctx.op("sync", lambda e, sl=sl, tc=tc: e.dma_start(
            out=kt_b[sl][:].rearrange("p (h d) -> p h d", h=16),
            in_=k_tm.rearrange("(tc h t) d -> tc t h d", tc=16, h=16)[tc]),
            inc=lds[sl][2], k=16)
        ctx.op("sync", lambda e, sl=sl, tc=tc: e.dma_start(
            out=vt_b[sl][:].rearrange("p (v d) -> p v d", v=32),
            in_=v_tm.rearrange("(tc v t) d -> tc t v d", tc=16, v=32)[tc]),
            inc=lds[sl][3], k=16)
        ctx.op("sync", lambda e, sl=sl, tc=tc: e.dma_start(
            out=zc_b[sl][:].rearrange("p (v t) -> p v t", v=32),
            in_=z_fm.rearrange("(v p) t -> p v t", p=128)[:, :, tc * 128:(tc + 1) * 128]),
            inc=lds[sl][4], k=16)
        nld = 16 * (tc // 2 + 1)
        qc, kc, kt, vt, zc = (b[sl] for b in (qc_b, kc_b, kt_b, vt_b, zc_b))
        rel_in[sl] = []

        cgl = rows[:, 0:128]
        gam = rows[:, 128:256]
        eg = rows[:, 256:384]
        bgn = rows[:, 384:512]
        bet = rows[:, 512:640]
        if tc == 0:
            ctx.wait("vector", ldrows, 32)
        wt(ctx, "vector", rel["rows"])
        if tc == 0:
            ctx.op("vector", lambda e: e.tensor_copy(cgl, cumg_sb[:, 0:128]), inc=vch)
        else:
            ctx.op("vector", lambda e, tc=tc: e.tensor_scalar(
                cgl, cumg_sb[:, tc * 128:(tc + 1) * 128],
                cumg_sb[:, tc * 128 - 1:tc * 128], None, ALU.subtract), inc=vch)
        ctx.op("vector", lambda e, tc=tc: e.tensor_copy(
            bet, beta_sb[:, tc * 128:(tc + 1) * 128]), inc=vch)
        cgl_rdy = vch.n
        ctx.wait("scalar", vch, cgl_rdy)
        ctx.op("scalar", lambda e: e.activation(gam, cgl, AF.Exp), inc=ach)
        ctx.op("scalar", lambda e: e.activation(
            eg, cgl, AF.Exp, scale=cv(ctx, -1.0, 32), bias=cgl[:, 127:128]), inc=ach)
        ctx.wait("vector", ach, ach.n)
        ctx.op("vector", lambda e: e.scalar_tensor_tensor(
            bgn, gam, -1.0, bet, ALU.mult, ALU.mult), inc=vch)
        rows_rdy_v = vch.n
        ctx.wait("tensor", vch, rows_rdy_v)
        wt(ctx, "tensor", rel["B0"])
        wt(ctx, "tensor", rel["rowsT"])
        for j in range(5):
            ctx.op("tensor", lambda e, j=j: e.transpose(
                B0[:, j * 32:(j + 1) * 32], rows[:, j * 128:(j + 1) * 128],
                identf[0:32, 0:32]), inc=(tch if j == 4 else None))
        rowsT_mm = tch.n
        ctx.wait("vector", tch, rowsT_mm)
        ctx.op("vector", lambda e: e.tensor_copy(rowsT[:], B0[:, 0:160]), inc=vch)
        rel["rows"] = (tch, rowsT_mm)
        rel["B0"] = (vch, vch.n)
        rowsT_rdy = vch.n
        cglT = rowsT[:, 0:32]
        gamT = rowsT[:, 32:64]
        egT = rowsT[:, 64:96]
        bgnT = rowsT[:, 96:128]
        betT = rowsT[:, 128:160]

        ost = ostage[sl]
        wt(ctx, "vector", rel["ost"][sl])

        for vh in range(nvh):
            kh = vh // 2
            pair = (vh // 2) % 2
            pb = B1 if pair == 0 else B2
            pbk = "B1" if pair == 0 else "B2"
            if vh % 2 == 0:
                ctx.wait("tensor", lds[sl][0], nld)
                ctx.wait("tensor", lds[sl][1], nld)
                wt(ctx, "tensor", rel[pbk])
                ctx.op("tensor", lambda e, pb=pb, kc=kc, kh=kh: e.matmul(
                    pb[:, 0:128], kc[:, kh * 128:(kh + 1) * 128],
                    kc[:, kh * 128:(kh + 1) * 128], start=True, stop=True))
                ctx.op("tensor", lambda e, pb=pb, kc=kc, qc=qc, kh=kh: e.matmul(
                    pb[:, 128:256], kc[:, kh * 128:(kh + 1) * 128],
                    qc[:, kh * 128:(kh + 1) * 128], start=True, stop=True), inc=tch)
            bcol = 256 + (vh % 2) * 128
            ctx.wait("tensor", vch, rowsT_rdy)
            wt(ctx, "tensor", rel[pbk])
            ctx.op("tensor", lambda e, pb=pb, vh=vh, bcol=bcol, cgl=cgl: e.matmul(
                pb[:, bcol:bcol + 128], onehotf[:, vh * 128:(vh + 1) * 128],
                cgl, start=True, stop=True), inc=tch)
            mm_pair = tch.n
            if stage <= 1:
                continue
            ctx.wait("vector", tch, mm_pair)
            wt(ctx, "vector", rel["tD"])
            ctx.op("vector", lambda e, pb=pb, bcol=bcol, vh=vh, cglT=cglT: e.scalar_tensor_tensor(
                tD[:, 0:128], pb[:, bcol:bcol + 128], cglT[:, vh:vh + 1], maskS[:],
                ALU.subtract, ALU.add), inc=vch)
            ctx.op("vector", lambda e, pb=pb, bcol=bcol, vh=vh, cglT=cglT: e.scalar_tensor_tensor(
                tD[:, 128:256], pb[:, bcol:bcol + 128], cglT[:, vh:vh + 1], maskIT[:],
                ALU.subtract, ALU.subtract), inc=vch)
            tD_rdy = vch.n
            ctx.wait("scalar", vch, tD_rdy)
            wt(ctx, "scalar", rel["eD"])
            ctx.op("scalar", lambda e: e.activation(eD[:, 0:128], tD[:, 0:128],
                                                    AF.Exp, scale=cv(ctx, -1.0)), inc=ach)
            ctx.op("scalar", lambda e: e.activation(eD[:, 128:256], tD[:, 128:256],
                                                    AF.Exp), inc=ach)
            rel["tD"] = (ach, ach.n)
            eD_rdy = ach.n
            if stage <= 2:
                continue
            ctx.wait("vector", ach, eD_rdy)
            wt(ctx, "vector", rel["npp"][0])
            ctx.op("vector", lambda e, pb=pb, vh=vh, betT=betT: e.scalar_tensor_tensor(
                n_pp[0][:], eD[:, 0:128], betT[:, vh:vh + 1], pb[:, 0:128],
                ALU.mult, ALU.mult), inc=vch)
            n1_rdy = vch.n
            wt(ctx, "vector", rel["aq"])
            ctx.op("vector", lambda e, pb=pb: e.tensor_tensor(
                aq_bf[:], eD[:, 128:256], pb[:, 128:256], ALU.mult), inc=vch)
            rel["eD"] = (vch, vch.n)
            rel[pbk] = (vch, vch.n)
            aq_rdy = vch.n
            if stage <= 3:
                continue
            ctx.wait("tensor", vch, n1_rdy)
            wt(ctx, "tensor", rel["B7"])
            wt(ctx, "tensor", rel["ntpow"])
            ctx.op("tensor", lambda e: e.transpose(B7[:, 0:128], n_pp[0][:], identb[:]),
                   inc=tch)
            nt1_mm = tch.n
            ctx.wait("scalar", tch, nt1_mm)
            wt(ctx, "scalar", rel["ntpow"])
            ctx.op("scalar", lambda e: e.copy(nt_pow[:, 0:128], B7[:, 0:128]), inc=ach)
            rel["B7"] = (ach, ach.n)
            nt1_rdy = ach.n
            if stage <= 4:
                continue
            wt(ctx, "scalar", rel["x"][0])
            ctx.wait("scalar", lds[sl][3], nld)
            ctx.op("scalar", lambda e, vt=vt, vh=vh, betT=betT: e.activation(
                x_bf[0][:, 0:128], vt[:, vh * 128:(vh + 1) * 128], AF.Copy,
                scale=betT[:, vh:vh + 1]), inc=ach)
            ctx.wait("scalar", lds[sl][2], nld)
            ctx.op("scalar", lambda e, kt=kt, kh=kh, vh=vh, bgnT=bgnT: e.activation(
                x_bf[0][:, 128:256], kt[:, kh * 128:(kh + 1) * 128], AF.Copy,
                scale=bgnT[:, vh:vh + 1]), inc=ach)
            x_rdy = ach.n
            if stage <= 5:
                continue

            cur = 0
            n_rdy = n1_rdy
            nt_rdy = nt1_rdy
            ntpow_reads = []
            for li in range(6):
                nxt = 1 - cur
                pq = B3 if li % 2 == 0 else B4
                pqk = "B3" if li % 2 == 0 else "B4"
                if isinstance(n_rdy, tuple):
                    ctx.wait("tensor", vch if n_rdy[0] == "v" else ach, n_rdy[1])
                else:
                    ctx.wait("tensor", vch, n_rdy)
                if isinstance(nt_rdy, tuple):
                    ctx.wait("tensor", vch if nt_rdy[0] == "v" else ach, nt_rdy[1])
                else:
                    ctx.wait("tensor", ach, nt_rdy)
                wt(ctx, "tensor", rel[pqk])
                ntsl = nt_pow[:, li * 128:(li + 1) * 128]
                ntdst = nt_pow[:, (li + 1) * 128:(li + 2) * 128]
                if li < 5:
                    ctx.op("tensor", lambda e, pq=pq, ntsl=ntsl, cur=cur: e.matmul(
                        pq[:, 0:128], ntsl, n_pp[cur][:], start=True, stop=True))
                ctx.op("tensor", lambda e, pq=pq, ntsl=ntsl, cur=cur: e.matmul(
                    pq[:, 128:256], n_pp[cur][:], ntsl, start=True, stop=True), inc=tch)
                sqmm = tch.n
                ntpow_reads.append((tch, sqmm))
                rel["npp"][cur] = (tch, sqmm)
                if li % 2 == 0:
                    ctx.wait("vector", tch, sqmm)
                    if li < 5:
                        wt(ctx, "vector", rel["npp"][nxt])
                        ctx.op("vector", lambda e, pq=pq, nxt=nxt: e.tensor_copy(
                            n_pp[nxt][:], pq[:, 0:128]), inc=vch)
                        n_rdy = vch.n
                    ctx.op("vector", lambda e, pq=pq, ntdst=ntdst: e.tensor_copy(
                        ntdst, pq[:, 128:256]), inc=vch)
                    nt_rdy = ("v", vch.n)
                    rel[pqk] = (vch, vch.n)
                else:
                    ctx.wait("scalar", tch, sqmm)
                    if li < 5:
                        wt(ctx, "scalar", rel["npp"][nxt])
                        ctx.op("scalar", lambda e, pq=pq, nxt=nxt: e.copy(
                            n_pp[nxt][:], pq[:, 0:128]), inc=ach)
                        n_rdy = ("a", ach.n)
                    ctx.op("scalar", lambda e, pq=pq, ntdst=ntdst: e.copy(
                        ntdst, pq[:, 128:256]), inc=ach)
                    nt_rdy = ("a", ach.n)
                    rel[pqk] = (ach, ach.n)
                cur = nxt
            nt_all_rdy = nt_rdy if not isinstance(nt_rdy, tuple) else None
            nt_all_rec = ((vch if nt_rdy[0] == "v" else ach, nt_rdy[1])
                          if isinstance(nt_rdy, tuple) else (ach, nt_rdy))
            if stage <= 6:
                continue

            xcur = 0
            x_cur_rdy = x_rdy
            ap_engine_flip = 0
            for ai, p in enumerate([64, 32, 16, 8, 4, 2, 1]):
                li = {1: 0, 2: 1, 4: 2, 8: 3, 16: 4, 32: 5, 64: 6}[p]
                px = B5 if ai % 2 == 0 else B6
                pxk = "B5" if ai % 2 == 0 else "B6"
                ntsl = nt_pow[:, li * 128:(li + 1) * 128]
                wt(ctx, "tensor", nt_all_rec)
                ctx.wait("tensor", ach, x_cur_rdy if ai == 0 else 0)
                ctx.wait("tensor", vch, x_cur_rdy if ai > 0 else 0)
                wt(ctx, "tensor", rel[pxk])
                ctx.op("tensor", lambda e, px=px, ntsl=ntsl, xcur=xcur: e.matmul(
                    px[:, 0:256], ntsl, x_bf[xcur][:], start=True, stop=True), inc=tch)
                apmm = tch.n
                ntpow_reads.append((tch, apmm))
                xnxt = 1 - xcur
                ctx.wait("vector", tch, apmm)
                wt(ctx, "vector", rel["x"][xnxt])
                op = ALU.add if p >= 2 else ALU.subtract
                ctx.op("vector", lambda e, px=px, xcur=xcur, xnxt=xnxt, op=op: e.tensor_tensor(
                    x_bf[xnxt][:], x_bf[xcur][:], px[:, 0:256], op), inc=vch)
                rel["x"][xcur] = (vch, vch.n)
                rel[pxk] = (vch, vch.n)
                x_cur_rdy = vch.n
                xcur = xnxt
            rel["ntpow"] = ntpow_reads[-1]
            xfin = xcur
            Ux = x_bf[xfin][:, 0:128]
            Wx = x_bf[xfin][:, 128:256]
            if stage <= 7:
                continue

            ctx.wait("vector", vch, x_cur_rdy)
            wt(ctx, "vector", rel["sbf"])
            ctx.op("vector", lambda e, vh=vh: e.tensor_copy(
                sbf[:], S_all[:, vh * 128:(vh + 1) * 128]), inc=vch)
            sbf_rdy = vch.n
            ctx.wait("tensor", vch, x_cur_rdy)
            wt(ctx, "tensor", rel["B7"])
            ctx.op("tensor", lambda e, Wx=Wx: e.transpose(B7[:, 128:256], Wx, identb[:]),
                   inc=tch)
            wmm = tch.n
            ctx.wait("scalar", tch, wmm)
            wt(ctx, "scalar", rel["wfm"])
            ctx.op("scalar", lambda e: e.copy(wfm[:], B7[:, 128:256]), inc=ach)
            rel["B7"] = (ach, ach.n)
            wfm_rdy = ach.n
            if tc == 0:
                rel["wfm"] = (ach, wfm_rdy)
            if tc > 0:
                ctx.wait("tensor", ach, wfm_rdy)
                ctx.wait("tensor", vch, sbf_rdy)
                wt(ctx, "tensor", rel["B5"])
                ctx.op("tensor", lambda e: e.matmul(
                    B5[:, 0:128], wfm[:], sbf[:], start=True, stop=True), inc=tch)
                dlmm = tch.n
                rel["wfm"] = (tch, dlmm)
                ctx.wait("vector", tch, dlmm)
                wt(ctx, "vector", rel["dl"])
                ctx.op("vector", lambda e, Ux=Ux: e.tensor_tensor(
                    dl_bf[:], Ux, B5[:, 0:128], ALU.add), inc=vch)
                rel["B5"] = (vch, vch.n)
            else:
                wt(ctx, "vector", rel["dl"])
                ctx.op("vector", lambda e, Ux=Ux: e.tensor_copy(dl_bf[:], Ux), inc=vch)
            rel["x"][xfin] = (vch, vch.n)
            dl_rdy = vch.n
            ctx.wait("scalar", vch, dl_rdy)
            wt(ctx, "scalar", rel["dls"])
            ctx.op("scalar", lambda e, vh=vh, egT=egT: e.activation(
                dls_bf[:], dl_bf[:], AF.Copy, scale=egT[:, vh:vh + 1]), inc=ach)
            dls_rdy = ach.n
            wt(ctx, "tensor", rel["B6"])
            if tc > 0:
                ctx.op("tensor", lambda e, qc=qc, kh=kh: e.matmul(
                    B6[:, 0:128], qc[:, kh * 128:(kh + 1) * 128], sbf[:],
                    start=True, stop=True))
            ctx.wait("tensor", vch, dl_rdy)
            ctx.op("tensor", lambda e, qc=qc: e.matmul(
                B6[:, 128:256], aq_bf[:], dl_bf[:], start=True, stop=True), inc=tch)
            rel["aq"] = (tch, tch.n)
            omm = tch.n
            wt(ctx, "tensor", rel["ecc"])
            ctx.op("tensor", lambda e, vh=vh, gam=gam: e.matmul(
                B0[:, 384:385], onehotf[:, vh * 128:(vh + 1) * 128],
                gam[:, 127:128], start=True, stop=True), inc=tch)
            ecmm = tch.n
            ctx.wait("vector", tch, ecmm)
            ctx.op("vector", lambda e: e.tensor_copy(ecc[:], B0[:, 384:385]), inc=vch)
            ec_rdy = vch.n
            rel["B0"] = (vch, ec_rdy)
            ctx.wait("tensor", ach, dls_rdy)
            wt(ctx, "tensor", rel["B5"])
            ctx.op("tensor", lambda e, kt=kt, kh=kh: e.matmul(
                B5[:, 128:256], kt[:, kh * 128:(kh + 1) * 128], dls_bf[:],
                start=True, stop=True), inc=tch)
            rel["dls"] = (tch, tch.n)
            psmm = tch.n
            ctx.wait("vector", tch, psmm)
            ctx.wait("vector", vch, ec_rdy)
            ctx.op("vector", lambda e, vh=vh: e.scalar_tensor_tensor(
                S_all[:, vh * 128:(vh + 1) * 128], S_all[:, vh * 128:(vh + 1) * 128],
                ecc[:], B5[:, 128:256], ALU.mult, ALU.add), inc=vch)
            rel["B5"] = (vch, vch.n)
            rel["ecc"] = (vch, vch.n)
            rel["sbf"] = (tch, omm) if tc > 0 else None
            ctx.wait("vector", tch, omm)
            wt(ctx, "vector", rel["t1"])
            wt(ctx, "vector", rel["otm"])
            if tc > 0:
                ctx.op("vector", lambda e, vh=vh, gamT=gamT: e.tensor_scalar(
                    t1[:], B6[:, 0:128], gamT[:, vh:vh + 1], None, ALU.mult), inc=vch)
                ctx.wait("vector", vch, vch.n)
                ctx.op("vector", lambda e: e.tensor_tensor(
                    otm[:], t1[:], B6[:, 128:256], ALU.add), inc=vch)
            else:
                ctx.op("vector", lambda e: e.tensor_copy(otm[:], B6[:, 128:256]), inc=vch)
            rel["B6"] = (vch, vch.n)
            rel["t1"] = (vch, vch.n)
            otm_rdy = vch.n
            rel["dl"] = (tch, omm)
            if stage <= 8:
                continue
            ctx.wait("tensor", vch, otm_rdy)
            wt(ctx, "tensor", rel["B7"])
            ctx.op("tensor", lambda e: e.transpose(B7[:, 256:384], otm[:], identb[:]),
                   inc=tch)
            otmm = tch.n
            rel["otm"] = (tch, otmm)
            ctx.wait("scalar", lds[sl][4], nld)
            wt(ctx, "scalar", rel["zs"])
            ctx.op("scalar", lambda e, zc=zc, vh=vh: e.activation(
                zs[:], zc[:, vh * 128:(vh + 1) * 128], AF.Silu), inc=ach)
            zs_rdy = ach.n
            ctx.wait("vector", tch, otmm)
            ctx.wait("vector", ach, zs_rdy)
            wt(ctx, "vector", rel["t2"])
            ctx.op("vector", lambda e: e.tensor_tensor(t2[:], B7[:, 256:384], zs[:],
                                                       ALU.mult), inc=vch)
            rel["B7"] = (vch, vch.n)
            rel["zs"] = (vch, vch.n)
            t2_rdy = vch.n
            ctx.wait("scalar", vch, t2_rdy)
            wt(ctx, "scalar", rel["sq2"])
            ctx.op("scalar", lambda e: e.activation(sq2[:], t2[:], AF.Square), inc=ach)
            sq2_rdy = ach.n
            ctx.wait("tensor", ach, sq2_rdy)
            wt(ctx, "tensor", rel["rrow"])
            wt(ctx, "tensor", rel["B5"])
            ctx.op("tensor", lambda e: e.matmul(
                B5[0:1, 384:512], ctx.consts["onesb"][:, 0:1], sq2[:],
                start=True, stop=True), inc=tch)
            prmm = tch.n
            rel["sq2"] = (tch, prmm)
            ctx.wait("scalar", tch, prmm)
            ctx.op("scalar", lambda e: e.activation(
                rrow[:], B5[0:1, 384:512], AF.Sqrt,
                scale=cv(ctx, 1.0 / 128, 1), bias=cv(ctx, 1e-6, 1)), inc=ach)
            rel["B5"] = (ach, ach.n)
            ctx.wait("vector", ach, ach.n)
            ctx.op("vector", lambda e: e.reciprocal(rrow[:], rrow[:]), inc=vch)
            rr_rdy = vch.n
            ctx.wait("tensor", vch, rr_rdy)
            wt(ctx, "tensor", rel["B6"])
            ctx.op("tensor", lambda e: e.matmul(
                B6[:, 256:384], onesf[0:1, :], rrow[:], start=True, stop=True),
                inc=tch)
            rbmm = tch.n
            rel["rrow"] = (tch, rbmm)
            ctx.wait("vector", tch, rbmm)
            ctx.op("vector", lambda e, vh=vh, ost=ost: e.scalar_tensor_tensor(
                ost[:, vh * 128:(vh + 1) * 128], t2[:], normw[:],
                B6[:, 256:384], ALU.mult, ALU.mult), inc=vch)
            rel["B6"] = (vch, vch.n)
            rel["t2"] = (vch, vch.n)
        ctx.wait("sync", vch, vch.n)
        def st_o(e, sl=sl, tc=tc):
            dst = o_norm.rearrange("(v p) t -> p v t", p=128)[:, :, tc * 128:(tc + 1) * 128]
            return e.dma_start(out=dst, in_=ostage[sl][:].rearrange("p (v t) -> p v t", v=32))
        ctx.op("sync", st_o, inc=ost_sem, k=16)
        rel["ost"][sl] = (ost_sem, ost_sem.n)
        chunk_done = [(vch, vch.n), (ach, ach.n), (tch, tch.n)]
        rel_in[sl] = chunk_done + [(ost_sem, ost_sem.n)]
        rel["rows"] = chunk_done
        rel["rowsT"] = chunk_done
    return dict(ost=ost_sem, done=[(ost_sem, ost_sem.n)])



def _row_chain(ctx, vch, ach, rows, src1, src2, n):
    ctx.op("vector", lambda e: e.tensor_scalar(
        rows[:, 2 * n:3 * n], src1, 1.0 / 2048, None, ALU.mult), inc=vch)
    ctx.op("vector", lambda e: e.tensor_scalar(
        rows[:, 3 * n:4 * n], src2, 1.0 / 2048, None, ALU.mult), inc=vch)
    ctx.wait("vector", vch, vch.n)
    ctx.op("vector", lambda e: e.tensor_tensor(
        rows[:, 0:n], rows[:, 2 * n:3 * n], rows[:, 2 * n:3 * n], ALU.mult), inc=vch)
    ctx.wait("vector", vch, vch.n)
    ctx.op("vector", lambda e: e.tensor_tensor(
        rows[:, 0:n], rows[:, 3 * n:4 * n], rows[:, 0:n], ALU.subtract), inc=vch)
    ctx.wait("scalar", vch, vch.n)
    ctx.op("scalar", lambda e: e.activation(
        rows[:, 0:n], rows[:, 0:n], AF.Sqrt, bias=cv(ctx, 1e-5, 1)), inc=ach)
    ctx.wait("vector", ach, ach.n)
    ctx.op("vector", lambda e: e.reciprocal(rows[:, 0:n], rows[:, 0:n]), inc=vch)
    ctx.wait("vector", vch, vch.n)
    ctx.op("vector", lambda e: e.scalar_tensor_tensor(
        rows[:, n:2 * n], rows[:, 2 * n:3 * n], -1.0, rows[:, 0:n],
        ALU.mult, ALU.mult), inc=vch)
    return vch.n


def emit_p2b(ctx, o_norm, x_fm, w_o, x2_fm, xn2_fm):
    onesb = ctx.consts["onesb"]
    onesf = ctx.consts["onesf"]
    vch = ctx.sem("pbv")
    ach = ctx.sem("pba")
    tch = ctx.sem("pbt")

    ost = ctx.sbuf("pbo", [128, 32 * 512], BF16)
    attn = ctx.sbuf("pbattn", [128, 16 * 512], BF16)
    x2s = ctx.sbuf("pbx2", [128, 16 * 512], BF16)
    xfs = ctx.sbuf("pbxf", [128, 16 * 512], BF16)
    sqt = ctx.sbuf("pbsq", [128, 512], BF16)
    tmp = ctx.sbuf("pbtmp", [128, 512], F32)
    rows = ctx.sbuf("pbrows", [1, 4 * 512], F32)
    wob = [ctx.sbuf(f"pbw{i}", [128, 32 * 128], BF16) for i in range(2)]
    xn2s = ost[:, 0:16 * 512]
    olds = ctx.sem("pbol")
    xlds = ctx.sem("pbxl")
    wlds = [ctx.sem(f"pbwl{i}") for i in range(2)]
    sts = ctx.sem("pbst")
    pA = ctx.psum("pbA", [128, 512], F32)
    pB = ctx.psum("pbB", [128, 512], F32)
    pS1 = ctx.psum("pbS1", [1, 512], F32)
    pS2 = ctx.psum("pbS2", [1, 512], F32)
    pR1 = ctx.psum("pbR1", [128, 512], F32)
    pR2 = ctx.psum("pbR2", [128, 512], F32)
    rel = {k: None for k in ["A", "B", "S", "R", "sq", "strip", "rows", "tmp", "ost"]}
    rel_w = [None, None]
    nw = [0]
    wot = w_o.rearrange("(m k p) f -> m p k f", m=16, k=32)

    for s in range(4):
        wt(ctx, "sync", rel["strip"])
        ctx.op("sync", lambda e, s=s: e.dma_start(
            out=ost[:].rearrange("p (k t) -> p k t", k=32),
            in_=o_norm.rearrange("(k p) t -> p k t", p=128)[:, :, s * 512:(s + 1) * 512]),
            inc=olds, k=16)
        ctx.op("sync", lambda e, s=s: e.dma_start(
            out=xfs[:].rearrange("p (k t) -> p k t", k=16),
            in_=x_fm.rearrange("(k p) t -> p k t", p=128)[:, :, s * 512:(s + 1) * 512]),
            inc=xlds, k=16)
        for m in range(16):
            sl = m % 2
            wb = wob[sl]
            wt(ctx, "gpsimd", rel_w[sl])
            ctx.op("gpsimd", lambda e, wb=wb, m=m: e.dma_start(
                out=wb[:].rearrange("p (k f) -> p k f", k=32), in_=wot[m]),
                inc=wlds[sl], k=16)
            nw[0] += 1
            ctx.wait("tensor", wlds[sl], 16 * ((nw[0] - 1) // 2 + 1))
            ctx.wait("tensor", olds, 16 * (s + 1))
            pm = pA if sl == 0 else pB
            wt(ctx, "tensor", rel["A" if sl == 0 else "B"])
            for k in range(32):
                ctx.op("tensor", lambda e, pm=pm, wb=wb, k=k: e.matmul(
                    pm[:], wb[:, k * 128:(k + 1) * 128],
                    ost[:, k * 512:(k + 1) * 512],
                    start=(k == 0), stop=(k == 31)), inc=(tch if k == 31 else None))
            rel_w[sl] = (tch, tch.n)
            mmv = tch.n
            ctx.wait("scalar", tch, mmv)
            wt(ctx, "scalar", rel["sq"])
            ctx.op("scalar", lambda e, pm=pm, m=m: e.copy(
                attn[:, m * 512:(m + 1) * 512], pm[:]), inc=ach)
            ctx.op("scalar", lambda e, pm=pm: e.activation(
                sqt[:], pm[:], AF.Square), inc=ach)
            sqv = ach.n
            rel["A" if sl == 0 else "B"] = (ach, sqv)
            ctx.wait("tensor", ach, sqv)
            if m == 0:
                wt(ctx, "tensor", rel["S"])
            ctx.op("tensor", lambda e, m=m: e.matmul(
                pS1[:], onesb[:, 0:1], attn[:, m * 512:(m + 1) * 512],
                start=(m == 0), stop=(m == 15)))
            ctx.op("tensor", lambda e, m=m: e.matmul(
                pS2[:], onesb[:, 0:1], sqt[:],
                start=(m == 0), stop=(m == 15)), inc=(tch if m == 15 else None))
            rel["sq"] = (tch, tch.n)
        stmm = tch.n
        rel["ost"] = (tch, stmm)
        ctx.wait("vector", tch, stmm)
        wt(ctx, "vector", rel["rows"])
        rc1 = _row_chain(ctx, vch, ach, rows, pS1[:], pS2[:], 512)
        rel["S"] = (vch, rc1)
        wt(ctx, "tensor", rel["R"])
        ctx.wait("tensor", vch, rc1)
        ctx.op("tensor", lambda e: e.matmul(
            pR1[:], onesf[0:1, :], rows[:, 0:512], start=True, stop=True))
        ctx.op("tensor", lambda e: e.matmul(
            pR2[:], onesf[0:1, :], rows[:, 512:1024], start=True, stop=True), inc=tch)
        rmm = tch.n
        ctx.wait("vector", tch, rmm)
        ctx.wait("vector", xlds, 16 * (s + 1))
        for m in range(16):
            off = m * 512
            wt(ctx, "vector", rel["tmp"])
            ctx.op("vector", lambda e, off=off: e.tensor_tensor(
                tmp[:], attn[:, off:off + 512], pR1[:], ALU.mult), inc=vch)
            ctx.wait("vector", vch, vch.n)
            ctx.op("vector", lambda e: e.tensor_tensor(
                tmp[:], tmp[:], pR2[:], ALU.add), inc=vch)
            ctx.wait("vector", vch, vch.n)
            ctx.op("vector", lambda e, off=off: e.tensor_tensor(
                x2s[:, off:off + 512], tmp[:], xfs[:, off:off + 512], ALU.add), inc=vch)
            rel["tmp"] = (vch, vch.n)
            x2v = vch.n
            ctx.wait("scalar", vch, x2v)
            wt(ctx, "scalar", rel["sq"])
            ctx.op("scalar", lambda e, off=off: e.activation(
                sqt[:], x2s[:, off:off + 512], AF.Square), inc=ach)
            sqv = ach.n
            ctx.wait("tensor", ach, sqv)
            ctx.wait("tensor", vch, x2v)
            if m == 0:
                wt(ctx, "tensor", rel["S"])
            ctx.op("tensor", lambda e, off=off, m=m: e.matmul(
                pS1[:], onesb[:, 0:1], x2s[:, off:off + 512],
                start=(m == 0), stop=(m == 15)))
            ctx.op("tensor", lambda e, m=m: e.matmul(
                pS2[:], onesb[:, 0:1], sqt[:],
                start=(m == 0), stop=(m == 15)), inc=(tch if m == 15 else None))
            rel["sq"] = (tch, tch.n)
        rel["R"] = (vch, vch.n)
        st2 = tch.n
        ctx.wait("vector", tch, st2)
        rc2 = _row_chain(ctx, vch, ach, rows, pS1[:], pS2[:], 512)
        rel["S"] = (vch, rc2)
        rel["rows"] = (vch, rc2)
        wt(ctx, "tensor", rel["R"])
        ctx.wait("tensor", vch, rc2)
        ctx.op("tensor", lambda e: e.matmul(
            pR1[:], onesf[0:1, :], rows[:, 0:512], start=True, stop=True))
        ctx.op("tensor", lambda e: e.matmul(
            pR2[:], onesf[0:1, :], rows[:, 512:1024], start=True, stop=True), inc=tch)
        rmm2 = tch.n
        ctx.wait("vector", tch, rmm2)
        wt(ctx, "vector", rel["ost"])
        for m in range(16):
            off = m * 512
            wt(ctx, "vector", rel["tmp"])
            ctx.op("vector", lambda e, off=off: e.tensor_tensor(
                tmp[:], x2s[:, off:off + 512], pR1[:], ALU.mult), inc=vch)
            ctx.wait("vector", vch, vch.n)
            ctx.op("vector", lambda e, off=off: e.tensor_tensor(
                xn2s[:, off:off + 512], tmp[:], pR2[:], ALU.add), inc=vch)
            rel["tmp"] = (vch, vch.n)
        rel["R"] = (vch, vch.n)
        ctx.wait("sync", vch, vch.n)
        ctx.op("sync", lambda e, s=s: e.dma_start(
            out=x2_fm.rearrange("(k p) t -> p k t", p=128)[:, :, s * 512:(s + 1) * 512],
            in_=x2s[:].rearrange("p (k t) -> p k t", k=16)), inc=sts, k=16)
        ctx.op("sync", lambda e, s=s: e.dma_start(
            out=xn2_fm.rearrange("(k p) t -> p k t", p=128)[:, :, s * 512:(s + 1) * 512],
            in_=xn2s.rearrange("p (k t) -> p k t", k=16)), inc=sts, k=16)
        rel["strip"] = [(vch, vch.n), (ach, ach.n), (tch, tch.n), (sts, sts.n)]
    return dict(sts=sts, done=[(sts, sts.n)])



def emit_p3(ctx, xn2_fm, x2_fm, w1q, w2q, out, nstrips=4):
    onesb = ctx.consts["onesb"]
    onesf = ctx.consts["onesf"]
    identf = ctx.consts["identf"]
    b1 = ctx.consts["b1"]
    b2 = ctx.consts["b2"]
    vch = ctx.sem("p3v")
    ach = ctx.sem("p3a")
    tch = ctx.sem("p3t")

    xns = ctx.sbuf("p3xn", [128, 16 * 512], BF16)
    hst = ctx.sbuf("p3h", [128, 64 * 512], BF16)
    fstg = ctx.sbuf("p3f", [128, 16 * 512], BF16)
    outT = ctx.sbuf("p3ot", [128, 4 * 2048], F32)
    sqt = ctx.sbuf("p3sq", [128, 512], BF16)
    tmp = ctx.sbuf("p3tmp", [128, 512], F32)
    outm = ctx.sbuf("p3om", [128, 512], F32)
    rows = ctx.sbuf("p3rows", [1, 4 * 512], F32)
    w1b = [ctx.sbuf(f"p3w1{i}", [128, 16 * 128], BF16) for i in range(2)]
    w2b = [ctx.sbuf(f"p3w2{i}", [128, 64 * 128], BF16) for i in range(2)]
    x2st = xns
    xl = ctx.sem("p3xl")
    x2l = ctx.sem("p3x2l")
    w1l = [ctx.sem(f"p3w1l{i}") for i in range(2)]
    w2l = [ctx.sem(f"p3w2l{i}") for i in range(2)]
    sts = ctx.sem("p3st")
    pA = ctx.psum("p3A", [128, 512], F32)
    pB = ctx.psum("p3B", [128, 512], F32)
    pS1 = ctx.psum("p3S1", [1, 512], F32)
    pS2 = ctx.psum("p3S2", [1, 512], F32)
    pR1 = ctx.psum("p3R1", [128, 512], F32)
    pR2 = ctx.psum("p3R2", [128, 512], F32)
    pT = ctx.psum("p3T", [128, 512], F32)
    rel = {k: None for k in ["A", "B", "S", "R", "T", "sq", "strip", "rows",
                             "tmp", "h", "f", "outm", "outT", "xns"]}
    rel_w1 = [None, None]
    rel_w2 = [None, None]
    nw1 = [0]
    nw2 = [0]
    w1t = w1q.rearrange("(m p) f -> m p f", m=64)
    w2t = w2q.rearrange("(m p a) b -> m p (a b)", m=16, p=128, a=4)
    outp = out.rearrange("(b p) e -> b p e", p=128)

    for s in range(nstrips):
        wt(ctx, "sync", rel["strip"])
        wt(ctx, "sync", rel["xns"])
        ctx.op("sync", lambda e, s=s: e.dma_start(
            out=xns[:].rearrange("p (k t) -> p k t", k=16),
            in_=xn2_fm.rearrange("(k p) t -> p k t", p=128)[:, :, s * 512:(s + 1) * 512]),
            inc=xl, k=16)
        for m in range(64):
            sl = m % 2
            wb = w1b[sl]
            wt(ctx, "gpsimd", rel_w1[sl])
            ctx.op("gpsimd", lambda e, wb=wb, m=m: e.dma_start(
                out=wb[:], in_=w1t[m]), inc=w1l[sl], k=16)
            nw1[0] += 1
            ctx.wait("tensor", w1l[sl], 16 * ((nw1[0] - 1) // 2 + 1))
            ctx.wait("tensor", xl, 16 * (s + 1))
            pm = pA if sl == 0 else pB
            wt(ctx, "tensor", rel["A" if sl == 0 else "B"])
            for k in range(16):
                ctx.op("tensor", lambda e, pm=pm, wb=wb, k=k: e.matmul(
                    pm[:], wb[:, k * 128:(k + 1) * 128],
                    xns[:, k * 512:(k + 1) * 512],
                    start=(k == 0), stop=(k == 15)), inc=(tch if k == 15 else None))
            rel_w1[sl] = (tch, tch.n)
            mmv = tch.n
            ctx.wait("scalar", tch, mmv)
            wt(ctx, "scalar", rel["h"])
            ctx.op("scalar", lambda e, pm=pm, m=m: e.activation(
                hst[:, m * 512:(m + 1) * 512], pm[:], AF.Gelu,
                bias=b1[:, m:m + 1]), inc=ach)
            rel["A" if sl == 0 else "B"] = (ach, ach.n)
        h_rdy = ach.n
        ffn1_mms = tch.n
        rel["xns"] = (tch, ffn1_mms)
        wt(ctx, "sync", rel["xns"])
        ctx.op("sync", lambda e, s=s: e.dma_start(
            out=x2st[:].rearrange("p (k t) -> p k t", k=16),
            in_=x2_fm.rearrange("(k p) t -> p k t", p=128)[:, :, s * 512:(s + 1) * 512]),
            inc=x2l, k=16)
        for m in range(16):
            sl = m % 2
            wb = w2b[sl]
            wt(ctx, "gpsimd", rel_w2[sl])
            ctx.op("gpsimd", lambda e, wb=wb, m=m: e.dma_start(
                out=wb[:], in_=w2t[m]), inc=w2l[sl], k=16)
            nw2[0] += 1
            ctx.wait("tensor", w2l[sl], 16 * ((nw2[0] - 1) // 2 + 1))
            ctx.wait("tensor", ach, h_rdy)
            pm = pA if sl == 0 else pB
            wt(ctx, "tensor", rel["A" if sl == 0 else "B"])
            for k in range(64):
                ctx.op("tensor", lambda e, pm=pm, wb=wb, k=k: e.matmul(
                    pm[:], wb[:, k * 128:(k + 1) * 128],
                    hst[:, k * 512:(k + 1) * 512],
                    start=(k == 0), stop=(k == 63)), inc=(tch if k == 63 else None))
            rel_w2[sl] = (tch, tch.n)
            mmv = tch.n
            ctx.wait("vector", tch, mmv)
            wt(ctx, "vector", rel["f"])
            ctx.op("vector", lambda e, pm=pm, m=m: e.tensor_scalar(
                fstg[:, m * 512:(m + 1) * 512], pm[:], b2[:, m:m + 1], None,
                ALU.add), inc=vch)
            fv = vch.n
            ctx.wait("scalar", vch, fv)
            wt(ctx, "scalar", rel["sq"])
            ctx.op("scalar", lambda e, m=m: e.activation(
                sqt[:], fstg[:, m * 512:(m + 1) * 512], AF.Square), inc=ach)
            sqv = ach.n
            rel["A" if sl == 0 else "B"] = [(vch, fv), (ach, sqv)]
            ctx.wait("tensor", ach, sqv)
            ctx.wait("tensor", vch, fv)
            if m == 0:
                wt(ctx, "tensor", rel["S"])
            ctx.op("tensor", lambda e, m=m: e.matmul(
                pS1[:], onesb[:, 0:1], fstg[:, m * 512:(m + 1) * 512],
                start=(m == 0), stop=(m == 15)))
            ctx.op("tensor", lambda e, m=m: e.matmul(
                pS2[:], onesb[:, 0:1], sqt[:],
                start=(m == 0), stop=(m == 15)), inc=(tch if m == 15 else None))
            rel["sq"] = (tch, tch.n)
        stmm = tch.n
        ctx.wait("vector", tch, stmm)
        wt(ctx, "vector", rel["rows"])
        rc = _row_chain(ctx, vch, ach, rows, pS1[:], pS2[:], 512)
        rel["S"] = (vch, rc)
        rel["rows"] = (vch, rc)
        wt(ctx, "tensor", rel["R"])
        ctx.wait("tensor", vch, rc)
        ctx.op("tensor", lambda e: e.matmul(
            pR1[:], onesf[0:1, :], rows[:, 0:512], start=True, stop=True))
        ctx.op("tensor", lambda e: e.matmul(
            pR2[:], onesf[0:1, :], rows[:, 512:1024], start=True, stop=True), inc=tch)
        rmm = tch.n
        ctx.wait("vector", tch, rmm)
        ctx.wait("vector", x2l, 16 * (s + 1))
        for m in range(16):
            off = m * 512
            wt(ctx, "vector", rel["tmp"])
            ctx.op("vector", lambda e, off=off: e.tensor_tensor(
                tmp[:], fstg[:, off:off + 512], pR1[:], ALU.mult), inc=vch)
            ctx.wait("vector", vch, vch.n)
            ctx.op("vector", lambda e: e.tensor_tensor(
                tmp[:], tmp[:], pR2[:], ALU.add), inc=vch)
            ctx.wait("vector", vch, vch.n)
            wt(ctx, "vector", rel["outm"])
            ctx.op("vector", lambda e, off=off: e.tensor_tensor(
                outm[:], tmp[:], x2st[:, off:off + 512], ALU.add), inc=vch)
            rel["tmp"] = (vch, vch.n)
            outm_rdy = vch.n
            ctx.wait("tensor", vch, outm_rdy)
            wt(ctx, "tensor", rel["T"])
            for tt4 in range(4):
                ctx.op("tensor", lambda e, tt4=tt4: e.transpose(
                    pT[:, tt4 * 128:(tt4 + 1) * 128],
                    outm[:, tt4 * 128:(tt4 + 1) * 128], identf[:]),
                    inc=(tch if tt4 == 3 else None))
            tmm = tch.n
            rel["outm"] = (tch, tmm)
            ctx.wait("vector", tch, tmm)
            wt(ctx, "vector", rel["outT"])
            def cpo(e, m=m):
                dst = outT[:].rearrange("p (b e) -> p b e", b=4)[:, :, m * 128:(m + 1) * 128]
                return e.tensor_copy(dst, pT[:].rearrange("p (b e) -> p b e", b=4))
            ctx.op("vector", cpo, inc=vch)
            rel["T"] = (vch, vch.n)
        rel["R"] = (vch, vch.n)
        rel["f"] = (vch, vch.n)
        ctx.wait("sync", vch, vch.n)
        for tt4 in range(4):
            ctx.op("sync", lambda e, s=s, tt4=tt4: e.dma_start(
                out=outp[s * 4 + tt4],
                in_=outT[:, tt4 * 2048:(tt4 + 1) * 2048]), inc=sts, k=16)
        rel["outT"] = (sts, sts.n)
        rel["h"] = (tch, tch.n)
        rel["strip"] = [(vch, vch.n), (ach, ach.n), (tch, tch.n), (sts, sts.n)]
        if s == 0 and getattr(ctx, "dbg_h", None) is not None:
            ctx.wait("sync", ach, ach.n)
            ctx.wait("sync", vch, vch.n)
            ctx.dma("sync", ctx.dbg_h, hst[:], inc=sts)
            ctx.dma("sync", ctx.dbg_f, fstg[:], inc=sts)
            rel["strip"].append((sts, sts.n))
            rel["h"] = [rel["h"], (sts, sts.n)]
            rel["f"] = [rel["f"], (sts, sts.n)]
    return dict(sts=sts, done=[(sts, sts.n)])


import numpy as _np


def _consts_decl(ctx, names):
    spec = {
        "ident": ((128, 128), BF16), "identf": ((128, 128), F32),
        "onesf": ((128, 128), F32), "onesb": ((128, 128), BF16),
        "onehotb": ((32, 4096), BF16), "onehotf": ((32, 4096), F32),
        "colones": ((128, 1024), BF16), "cw": ((128, 256), F32),
        "cb": ((128, 64), F32), "alog": ((32, 1), F32), "dtb": ((32, 1), F32),
        "maskS": ((128, 128), F32), "maskIT": ((128, 128), F32),
        "normw": ((128, 1), F32), "b1": ((128, 64), F32), "b2": ((128, 16), F32),
    }
    cns = {}
    for nm in names:
        sh, dt = spec[nm]
        cns[nm] = (ctx.dram("d_" + nm, list(sh), dt, kind="ExternalInput"), sh, dt)
    if "wba" in names:
        pass
    return cns


_CONST_SRC = {
    "ident": "ident", "identf": "identf", "onesf": "onesf", "onesb": "onesb",
    "onehotb": "onehotb", "onehotf": "onehot", "colones": "colones",
    "cw": "cw", "cb": "cb", "alog": "alog", "dtb": "dtb",
    "maskS": "mask_strict", "maskIT": "mask_inclT", "normw": "normw",
    "b1": "b1", "b2": "b2",
}


def _const_inputs(hp, names):
    return {"d_" + nm: _np.ascontiguousarray(hp[_CONST_SRC[nm]]) for nm in names}


def _build_p0():
    ctx = Ctx("p0")
    w1 = ctx.dram("w1in", [8192, 2048], F32, kind="ExternalInput")
    w2 = ctx.dram("w2in", [8192, 2048], F32, kind="ExternalInput")
    w1q = ctx.dram("w1q", [8192, 2048], BF16, kind="ExternalOutput")
    w2q = ctx.dram("w2q", [8192, 2048], BF16, kind="ExternalOutput")
    cns = _consts_decl(ctx, ["onesf"])
    load_consts(ctx, cns)
    s1 = emit_p0(ctx, w1, w1q, 64, "wa")
    s2 = emit_p0(ctx, w2, w2q, 64, "wb")
    ctx.wait("sync", s1[0], 16 * 32)
    ctx.wait("sync", s1[1], 16 * 32)
    ctx.wait("sync", s2[0], 16 * 32)
    ctx.wait("sync", s2[1], 16 * 32)
    return ctx.finalize()


def _build_p1():
    ctx = Ctx("p1")
    x = ctx.dram("x", [2048, 2048], F32, kind="ExternalInput")
    wq = ctx.dram("wq", [96 * 16 * 128, 128], F32, kind="ExternalInput")
    cns = _consts_decl(ctx, ["ident", "onesf", "onehotb", "colones", "cw", "cb",
                             "alog", "dtb"])
    cns["wba"] = (ctx.dram("d_wba", [128, 1024], F32, kind="ExternalInput"),
                  (128, 1024), BF16, "cast")
    x_fm = ctx.dram("x_fm", [16 * 128, 2048], BF16, kind="ExternalOutput")
    q_fm = ctx.dram("q_fm", [16 * 128, 2048], BF16, kind="ExternalOutput")
    k_fm = ctx.dram("k_fm", [16 * 128, 2048], BF16, kind="ExternalOutput")
    k_tm = ctx.dram("k_tm", [16 * 16 * 128, 128], BF16, kind="ExternalOutput")
    v_tm = ctx.dram("v_tm", [16 * 32 * 128, 128], BF16, kind="ExternalOutput")
    z_fm = ctx.dram("z_fm", [32 * 128, 2048], BF16, kind="ExternalOutput")
    beta = ctx.dram("beta", [32, 2048], F32, kind="ExternalOutput")
    cumg = ctx.dram("cumg", [32, 2048], F32, kind="ExternalOutput")
    qc_scr = ctx.dram("qc_scr", [32 * 128, 2048], BF16, kind="Internal")
    load_consts(ctx, cns)
    prepare_cvals(ctx, [1e-5, 1e-6, 1.0, -1.0, 0.5, 1.0 / 128])
    h = emit_p1(ctx, x, wq, None, x_fm, q_fm, k_fm, k_tm, v_tm, z_fm,
                beta, cumg, qc_scr)
    d = emit_p1b(ctx, h, wq, q_fm, k_fm, k_tm, v_tm, z_fm, qc_scr)
    wt(ctx, "sync", d["done"])
    return ctx.finalize()


def _build_p2():
    ctx = Ctx("p2")
    q_fm = ctx.dram("q_fm", [16 * 128, 2048], BF16, kind="ExternalInput")
    k_fm = ctx.dram("k_fm", [16 * 128, 2048], BF16, kind="ExternalInput")
    k_tm = ctx.dram("k_tm", [16 * 16 * 128, 128], BF16, kind="ExternalInput")
    v_tm = ctx.dram("v_tm", [16 * 32 * 128, 128], BF16, kind="ExternalInput")
    z_fm = ctx.dram("z_fm", [32 * 128, 2048], BF16, kind="ExternalInput")
    beta = ctx.dram("beta", [32, 2048], F32, kind="ExternalInput")
    cumg = ctx.dram("cumg", [32, 2048], F32, kind="ExternalInput")
    o_norm = ctx.dram("o_norm", [32 * 128, 2048], BF16, kind="ExternalOutput")
    cns = _consts_decl(ctx, ["ident", "identf", "onesf", "onesb", "onehotf",
                             "maskS", "maskIT", "normw"])
    load_consts(ctx, cns)
    prepare_cvals(ctx, [1e-5, 1e-6, 1.0, -1.0, 0.5, 1.0 / 128])
    d = emit_p2(ctx, q_fm, k_fm, k_tm, v_tm, z_fm, beta, cumg, o_norm)
    wt(ctx, "sync", d["done"])
    return ctx.finalize()


def _build_p2b():
    ctx = Ctx("p2b")
    o_norm = ctx.dram("o_norm", [32 * 128, 2048], BF16, kind="ExternalInput")
    x_fm = ctx.dram("x_fm", [16 * 128, 2048], BF16, kind="ExternalInput")
    w_o = ctx.dram("w_o", [16 * 32 * 128, 128], F32, kind="ExternalInput")
    x2_fm = ctx.dram("x2_fm", [16 * 128, 2048], BF16, kind="ExternalOutput")
    xn2_fm = ctx.dram("xn2_fm", [16 * 128, 2048], BF16, kind="ExternalOutput")
    cns = _consts_decl(ctx, ["onesb", "onesf"])
    load_consts(ctx, cns)
    prepare_cvals(ctx, [1e-5, 1e-6, 1.0, -1.0, 0.5, 1.0 / 128])
    d = emit_p2b(ctx, o_norm, x_fm, w_o, x2_fm, xn2_fm)
    wt(ctx, "sync", d["done"])
    return ctx.finalize()


def _build_p3():
    ctx = Ctx("p3")
    xn2_fm = ctx.dram("xn2_fm", [16 * 128, 2048], BF16, kind="ExternalInput")
    x2_fm = ctx.dram("x2_fm", [16 * 128, 2048], BF16, kind="ExternalInput")
    w1q = ctx.dram("w1q", [8192, 2048], BF16, kind="ExternalInput")
    w2q = ctx.dram("w2q", [8192, 2048], BF16, kind="ExternalInput")
    out = ctx.dram("out", [2048, 2048], F32, kind="ExternalOutput")
    cns = _consts_decl(ctx, ["onesb", "onesf", "identf", "b1", "b2"])
    load_consts(ctx, cns)
    prepare_cvals(ctx, [1e-5, 1e-6, 1.0, -1.0, 0.5, 1.0 / 128])
    d = emit_p3(ctx, xn2_fm, x2_fm, w1q, w2q, out)
    wt(ctx, "sync", d["done"])
    return ctx.finalize()


def kernel_multi(**inputs):
    from concourse.bass_utils import run_bass_kernel_spmd
    N = 8
    hp = host_prep(inputs)
    x = _np.ascontiguousarray(_np.asarray(inputs["L0_attn_x"], dtype=_np.float32))

    nc = _build_p0()
    im = {"w1in": _np.ascontiguousarray(hp["w1t"].reshape(8192, 2048)),
          "w2in": _np.ascontiguousarray(hp["w2t"].reshape(8192, 2048)),
          **_const_inputs(hp, ["onesf"])}
    r0 = run_bass_kernel_spmd(nc, [im] * N, core_ids=list(range(N))).results

    nc = _build_p1()
    base = {"wq": _np.ascontiguousarray(hp["wq"].reshape(96 * 16 * 128, 128)),
            "d_wba": hp["wba"],
            **_const_inputs(hp, ["ident", "onesf", "onehotb", "colones", "cw",
                                 "cb", "alog", "dtb"])}
    ims = [dict(base, x=_np.ascontiguousarray(x[i])) for i in range(N)]
    r1 = run_bass_kernel_spmd(nc, ims, core_ids=list(range(N))).results

    nc = _build_p2()
    cns2 = _const_inputs(hp, ["ident", "identf", "onesf", "onesb", "onehotf",
                              "maskS", "maskIT", "normw"])
    ims = [dict(cns2, q_fm=r1[i]["q_fm"], k_fm=r1[i]["k_fm"], k_tm=r1[i]["k_tm"],
                v_tm=r1[i]["v_tm"], z_fm=r1[i]["z_fm"], beta=r1[i]["beta"],
                cumg=r1[i]["cumg"]) for i in range(N)]
    r2 = run_bass_kernel_spmd(nc, ims, core_ids=list(range(N))).results

    nc = _build_p2b()
    cns3 = _const_inputs(hp, ["onesb", "onesf"])
    wof = _np.ascontiguousarray(hp["wo"].reshape(16 * 32 * 128, 128))
    ims = [dict(cns3, o_norm=r2[i]["o_norm"], x_fm=r1[i]["x_fm"], w_o=wof)
           for i in range(N)]
    r3 = run_bass_kernel_spmd(nc, ims, core_ids=list(range(N))).results

    nc = _build_p3()
    cns4 = _const_inputs(hp, ["onesb", "onesf", "identf", "b1", "b2"])
    ims = [dict(cns4, xn2_fm=r3[i]["xn2_fm"], x2_fm=r3[i]["x2_fm"],
                w1q=r0[i]["w1q"], w2q=r0[i]["w2q"]) for i in range(N)]
    r4 = run_bass_kernel_spmd(nc, ims, core_ids=list(range(N))).results

    out = _np.stack([_np.asarray(r4[i]["out"], dtype=_np.float32) for i in range(N)])
    return out



def _build_fused():
    ctx = Ctx("mk")
    x = ctx.dram("x", [2048, 2048], F32, kind="ExternalInput")
    wq = ctx.dram("wq", [96 * 16 * 128, 128], F32, kind="ExternalInput")
    w_o = ctx.dram("w_o", [16 * 32 * 128, 128], F32, kind="ExternalInput")
    w1 = ctx.dram("w1in", [8192, 2048], F32, kind="ExternalInput")
    w2 = ctx.dram("w2in", [8192, 2048], F32, kind="ExternalInput")
    out = ctx.dram("out", [2048, 2048], F32, kind="ExternalOutput")
    w1q = ctx.dram("w1q", [8192, 2048], BF16, kind="Internal")
    w2q = ctx.dram("w2q", [8192, 2048], BF16, kind="Internal")
    x_fm = ctx.dram("x_fm", [16 * 128, 2048], BF16, kind="Internal")
    q_fm = ctx.dram("q_fm", [16 * 128, 2048], BF16, kind="Internal")
    k_fm = ctx.dram("k_fm", [16 * 128, 2048], BF16, kind="Internal")
    k_tm = ctx.dram("k_tm", [16 * 16 * 128, 128], BF16, kind="Internal")
    v_tm = ctx.dram("v_tm", [16 * 32 * 128, 128], BF16, kind="Internal")
    z_fm = ctx.dram("z_fm", [32 * 128, 2048], BF16, kind="Internal")
    beta = ctx.dram("beta", [32, 2048], F32, kind="Internal")
    cumg = ctx.dram("cumg", [32, 2048], F32, kind="Internal")
    qc_scr = ctx.dram("qc_scr", [32 * 128, 2048], BF16, kind="Internal")
    o_norm = ctx.dram("o_norm", [32 * 128, 2048], BF16, kind="Internal")
    x2_fm = ctx.dram("x2_fm", [16 * 128, 2048], BF16, kind="Internal")
    xn2_fm = ctx.dram("xn2_fm", [16 * 128, 2048], BF16, kind="Internal")
    gcns = _consts_decl(ctx, ["ident", "identf", "onesf", "onesb", "b1", "b2"])
    p1cns = _consts_decl(ctx, ["onehotb", "colones", "cw", "cb", "alog", "dtb"])
    p1cns["wba"] = (ctx.dram("d_wba", [128, 1024], F32, kind="ExternalInput"),
                    (128, 1024), BF16, "cast")
    p2cns = _consts_decl(ctx, ["onehotf", "maskS", "maskIT", "normw"])
    load_consts(ctx, gcns)
    gconsts = dict(ctx.consts)
    prepare_cvals(ctx, [1e-5, 1e-6, 1.0, -1.0, 0.5, 1.0 / 128])
    ctx.begin_phase()
    emit_p0(ctx, w1, w1q, 64, "wa")
    emit_p0(ctx, w2, w2q, 64, "wb")
    ctx.end_phase()
    ctx.begin_phase()
    load_consts(ctx, p1cns)
    ctx.consts.update(gconsts)
    h = emit_p1(ctx, x, wq, None, x_fm, q_fm, k_fm, k_tm, v_tm, z_fm,
                beta, cumg, qc_scr)
    emit_p1b(ctx, h, wq, q_fm, k_fm, k_tm, v_tm, z_fm, qc_scr)
    ctx.end_phase()
    ctx.begin_phase()
    load_consts(ctx, p2cns)
    ctx.consts.update(gconsts)
    emit_p2(ctx, q_fm, k_fm, k_tm, v_tm, z_fm, beta, cumg, o_norm)
    ctx.end_phase()
    ctx.begin_phase()
    ctx.consts = dict(gconsts)
    emit_p2b(ctx, o_norm, x_fm, w_o, x2_fm, xn2_fm)
    ctx.end_phase()
    ctx.begin_phase()
    d = emit_p3(ctx, xn2_fm, x2_fm, w1q, w2q, out)
    ctx.end_phase()
    wt(ctx, "sync", d["done"])
    return ctx.finalize()


def kernel(**inputs):
    from concourse.bass_utils import run_bass_kernel_spmd
    N = 8
    hp = host_prep(inputs)
    x = _np.ascontiguousarray(_np.asarray(inputs["L0_attn_x"], dtype=_np.float32))
    nc = _build_fused()
    base = {"wq": _np.ascontiguousarray(hp["wq"].reshape(96 * 16 * 128, 128)),
            "w_o": _np.ascontiguousarray(hp["wo"].reshape(16 * 32 * 128, 128)),
            "w1in": _np.ascontiguousarray(hp["w1t"].reshape(8192, 2048)),
            "w2in": _np.ascontiguousarray(hp["w2t"].reshape(8192, 2048)),
            "d_wba": hp["wba"],
            **_const_inputs(hp, ["ident", "identf", "onesf", "onesb", "onehotb",
                                 "onehotf", "colones", "cw", "cb", "alog", "dtb",
                                 "maskS", "maskIT", "normw", "b1", "b2"])}
    ims = [dict(base, x=_np.ascontiguousarray(x[i])) for i in range(N)]
    r = run_bass_kernel_spmd(nc, ims, core_ids=list(range(N))).results
    return _np.stack([_np.asarray(r[i]["out"], dtype=_np.float32) for i in range(N)])


# version 08 (final)
# speedup vs baseline: 1.1620x; 1.1523x over previous
import numpy as np
from contextlib import ExitStack

import concourse.bass as bass
import concourse.mybir as mybir

F32 = mybir.dt.float32
BF16 = mybir.dt.bfloat16
AF = mybir.ActivationFunctionType
ALU = mybir.AluOpType
AX = mybir.AxisListType

ENGINES = ("sync", "scalar", "vector", "tensor", "gpsimd")

S, H = 2048, 2048
HK, HV, DK, DV = 16, 32, 128, 128
NEG = -1e30


class Sem:
    __slots__ = ("h", "n", "name")

    def __init__(self, h, name):
        self.h = h
        self.n = 0
        self.name = name


class Ctx:
    def __init__(self, name="mk"):
        self.nc = bass.Bass("TRN2", target_bir_lowering=False)
        self.es = ExitStack()
        self.q = {e: [] for e in ENGINES}
        self._semcnt = 0
        self._waited = {e: {} for e in ENGINES}
        self._sems = []
        self.phase_scope = None

    def sbuf(self, name, shape, dt=F32, scope=None):
        return (scope or self.phase_scope or self.es).enter_context(
            self.nc.sbuf_tensor(name, shape, dt))

    def psum(self, name, shape, dt=F32, scope=None):
        return (scope or self.phase_scope or self.es).enter_context(
            self.nc.psum_tensor(name, shape, dt))

    def begin_phase(self):
        self.phase_scope = ExitStack()

    def end_phase(self):
        if self.phase_scope is not None:
            self.phase_scope.close()
            self.phase_scope = None
        for eng in ENGINES:
            for sm in self._sems:
                if sm.n > 0:
                    self.wait(eng, sm)

    def sem(self, name=None):
        self._semcnt += 1
        h = self.es.enter_context(self.nc.semaphore(name or f"s{self._semcnt}"))
        sm = Sem(h, name or f"s{self._semcnt}")
        self._sems.append(sm)
        return sm

    def dram(self, name, shape, dt=F32, kind="Internal"):
        return self.nc.dram_tensor(name, shape, dt, kind=kind).ap()

    def op(self, eng, fn, inc=None, k=1):
        if inc is not None:
            inc.n += k

            def g(e, fn=fn, inc=inc, k=k):
                fn(e).then_inc(inc.h, k)
        else:
            def g(e, fn=fn):
                fn(e)
        self.q[eng].append(g)
        return inc.n if inc is not None else None

    def dma(self, eng, out, in_, inc=None):
        return self.op(eng, lambda e, out=out, in_=in_: e.dma_start(out=out, in_=in_),
                       inc=inc, k=16)

    def wait(self, eng, sem, val=None):
        v = sem.n if val is None else val
        w = self._waited[eng]
        if w.get(id(sem), -1) >= v:
            return v
        w[id(sem)] = v
        self.q[eng].append(lambda e, s=sem.h, v=v: e.wait_ge(s, v))
        return v

    def finalize(self):
        nc = self.nc
        with nc.Block() as blk:
            for eng in ENGINES:
                fns = self.q[eng]
                getattr(blk, eng)(lambda e, fns=fns: [f(e) for f in fns])
        self.es.close()
        return nc



def perm_qkvz_cols():
    idx = []
    for h in range(HK):
        idx.append(np.arange(768 * h, 768 * h + 128))
    for h in range(HK):
        idx.append(np.arange(768 * h + 128, 768 * h + 256))
    for h in range(HK):
        idx.append(np.arange(768 * h + 256, 768 * h + 512))
    for h in range(HK):
        idx.append(np.arange(768 * h + 512, 768 * h + 768))
    return np.concatenate(idx)


def perm_ba_cols():
    beta_cols = [4 * h + j for h in range(HK) for j in range(2)]
    a_cols = [4 * h + 2 + j for h in range(HK) for j in range(2)]
    return np.array(beta_cols + a_cols)


def tile_km(w, nk, nm):
    return np.ascontiguousarray(
        w.reshape(nk, 128, nm, 128).transpose(2, 0, 1, 3))


def tile_km2(w, nk, nm):
    return np.ascontiguousarray(
        w.reshape(nk, 128, nm, 128).transpose(2, 1, 0, 3))


def host_prep(inputs):
    d = {}
    wq = np.asarray(inputs["L0_attn_w_qkvz"])
    d["wq"] = tile_km(np.ascontiguousarray(wq[:, perm_qkvz_cols()]), 16, 96)
    wba = np.asarray(inputs["L0_attn_w_ba"])
    d["wba"] = np.ascontiguousarray(
        wba[:, perm_ba_cols()].reshape(16, 128, 64).transpose(1, 0, 2).reshape(128, 1024))
    d["wo"] = tile_km(np.asarray(inputs["L0_attn_w_o"]), 32, 16)
    w1 = np.asarray(inputs["L0_ffn_w1"])
    d["w1t"] = tile_km2(np.ascontiguousarray(w1.T), 16, 64)
    w2 = np.asarray(inputs["L0_ffn_w2"])
    d["w2t"] = tile_km2(np.ascontiguousarray(w2.T), 64, 16)
    cw = np.asarray(inputs["L0_attn_conv_w"])
    d["cw"] = np.ascontiguousarray(
        cw.reshape(64, 128, 4).transpose(1, 0, 2).reshape(128, 256))
    d["cb"] = np.ascontiguousarray(
        np.asarray(inputs["L0_attn_conv_b"]).reshape(64, 128).T)
    d["alog"] = np.ascontiguousarray(np.asarray(inputs["L0_attn_a_log"]).reshape(32, 1))
    d["dtb"] = np.ascontiguousarray(np.asarray(inputs["L0_attn_dt_bias"]).reshape(32, 1))
    d["normw"] = np.ascontiguousarray(np.asarray(inputs["L0_attn_norm_w"]).reshape(128, 1))
    d["b1"] = np.ascontiguousarray(np.asarray(inputs["L0_ffn_b1"]).reshape(64, 128).T)
    d["b2"] = np.ascontiguousarray(np.asarray(inputs["L0_ffn_b2"]).reshape(16, 128).T)
    import ml_dtypes
    d["ident"] = np.eye(128, dtype=ml_dtypes.bfloat16)
    d["identf"] = np.eye(128, dtype=np.float32)
    d["onesf"] = np.ones((128, 128), dtype=np.float32)
    d["onesb"] = np.ones((128, 128), dtype=ml_dtypes.bfloat16)
    oh = np.zeros((32, 32, 128), dtype=np.float32)
    for h in range(32):
        oh[h, h, :] = 1.0
    d["onehot"] = np.ascontiguousarray(oh.transpose(1, 0, 2).reshape(32, 32 * 128))
    d["onehotb"] = d["onehot"].astype(ml_dtypes.bfloat16)
    co = np.zeros((32, 128, 32), dtype=ml_dtypes.bfloat16)
    for m in range(32):
        co[m, :, m] = 1.0
    d["colones"] = np.ascontiguousarray(co.transpose(1, 0, 2).reshape(128, 32 * 32))
    i, j = np.indices((128, 128))
    d["mask_strict"] = np.where(j >= i, np.float32(1e30), np.float32(0.0))
    d["mask_inclT"] = np.where(j < i, np.float32(1e30), np.float32(0.0))
    return d



def emit_p0(ctx: Ctx, w_in, w_out, ntiles, tag):
    wt_ = w_in.rearrange("(n p) f -> n p f", p=128)
    wo = w_out.rearrange("(n p) f -> n p f", p=128)
    NB = 4
    bufs = [ctx.sbuf(f"{tag}_in{i}", [128, 2048], F32) for i in range(NB)]
    tern = [ctx.sbuf(f"{tag}_t{i}", [128, 2048], BF16) for i in range(2)]
    ge = ctx.sbuf(f"{tag}_ge", [128, 2048], BF16)
    le = [ctx.sbuf(f"{tag}_le{i}", [128, 2048], BF16) for i in range(2)]
    acc = ctx.sbuf(f"{tag}_acc", [128, ntiles], F32)
    red = ctx.sbuf(f"{tag}_red", [128, 1], F32)
    thr = ctx.sbuf(f"{tag}_thr", [128, 1], F32)
    nthr = ctx.sbuf(f"{tag}_nthr", [128, 1], F32)

    lds = [ctx.sem() for i in range(NB)]
    sts = [ctx.sem() for i in range(2)]
    rd = ctx.sem()
    pch = ctx.sem()
    th = ctx.sem()

    onesf = ctx.consts["onesf"]
    onesb = ctx.consts["onesb"]
    ab = [ctx.sbuf(f"{tag}_ab{i}", [128, 2048], F32) for i in range(2)]
    pab = ctx.psum(f"{tag}_pab", [128, 1024], F32)
    psum_row = pab[0:1, 0:512]
    rowsum = ctx.sbuf(f"{tag}_rs", [1, 4], F32)
    msem = ctx.sem()
    vsem = ctx.sem()
    for i in range(ntiles):
        s_ = i % NB
        b = bufs[s_]
        if i >= NB:
            ctx.wait("sync", pch, max(0, pch.n - 3))
        ctx.dma("sync", b[:], wt_[i], inc=lds[s_])
        a2 = ab[i % 2]
        ctx.wait("scalar", lds[s_], 16 * (i // NB + 1))
        if i >= 2:
            ctx.wait("scalar", msem, msem.n - 4)
        ctx.op("scalar", lambda e, b=b, a2=a2: e.activation(a2[:], b[:], AF.Abs), inc=pch)
        ctx.wait("tensor", pch, pch.n)
        for n in range(4):
            ctx.op("tensor", lambda e, a2=a2, n=n, i=i: e.matmul(
                pab[0:1, 0:512], onesf[:, 0:1],
                a2[:, n * 512:(n + 1) * 512],
                start=(i == 0 and n == 0), stop=(i == ntiles - 1 and n == 3)),
                inc=msem)
    ctx.wait("vector", msem, msem.n)
    ctx.op("vector", lambda e: e.tensor_reduce(rowsum[:, 0:1], pab[0:1, 0:512], AX.X, ALU.add), inc=vsem)
    thr_t = thr
    nel = float(ntiles * 128 * 2048)
    ctx.wait("vector", vsem, vsem.n)
    ctx.op("vector", lambda e: e.tensor_scalar(rowsum[:, 1:2], rowsum[:, 0:1], 0.5 / nel, 0.5e-8, ALU.mult, ALU.max), inc=vsem)
    ctx.wait("tensor", vsem, vsem.n)
    ctx.op("tensor", lambda e: e.matmul(pab[:, 512:513], onesf[0:1, :], rowsum[:, 1:2], start=True, stop=True), inc=msem)
    ctx.wait("vector", msem, msem.n)
    ctx.op("vector", lambda e: e.tensor_copy(thr[:], pab[:, 512:513]), inc=th)
    ctx.wait("scalar", th, th.n)
    ctx.op("scalar", lambda e: e.mul(nthr[:], thr[:], -1.0), inc=th)
    th_n = th.n
    ctx.wait("gpsimd", th, th_n)
    ctx.wait("vector", th, th_n)

    rd0 = rd.n
    pch0 = pch.n
    for i in range(ntiles):
        s_ = i % NB
        b = bufs[s_]
        ctx.wait("sync", rd, rd0 + max(0, (i - NB + 1)) * 2)
        ctx.wait("sync", pch, pch0 + max(0, (i - NB + 1)))
        ctx.dma("sync", b[:], wt_[i], inc=lds[s_])
        t = tern[i % 2]
        l = le[i % 2]
        ctx.wait("vector", lds[s_], 16 * ((ntiles + i) // NB + 1))
        ctx.wait("gpsimd", lds[s_], 16 * ((ntiles + i) // NB + 1))
        if i >= 2:
            ctx.wait("vector", sts[i % 2], 16 * ((i - 2) // 2 + 1))
            ctx.wait("gpsimd", rd, rd0 + (i - 2) * 2 + 2)
        ctx.op("vector", lambda e, b=b: e.tensor_single_scalar(ge[:], b[:], thr[:], ALU.is_ge), inc=rd)
        ctx.op("gpsimd", lambda e, b=b, l=l: e.tensor_single_scalar(l[:], b[:], nthr[:], ALU.is_le), inc=pch)
        ctx.wait("vector", pch, pch0 + i + 1)
        ctx.op("vector", lambda e, t=t, l=l: e.tensor_tensor(t[:], ge[:], l[:], ALU.subtract), inc=rd)
        ctx.wait("scalar", rd, rd.n)
        ctx.dma("scalar", wo[i], t[:], inc=sts[i % 2])
    return sts


def prepare_cvals(ctx: Ctx, values):
    tile = ctx.sbuf("cvals", [128, max(1, len(values))], F32)
    sem = ctx.sem("cvals")
    for i, v in enumerate(values):
        ctx.op("gpsimd", lambda e, i=i, v=v: e.memset(tile[:, i:i + 1], float(v)),
               inc=sem)
    for eng in ("vector", "scalar", "tensor", "gpsimd"):
        ctx.wait(eng, sem, len(values))
    ctx.cvals = {float(v): i for i, v in enumerate(values)}
    ctx.cvtile = tile


def cv(ctx, v, p=128):
    return ctx.cvtile[0:p, ctx.cvals[float(v)]:ctx.cvals[float(v)] + 1]


def load_consts(ctx: Ctx, dram_consts):
    if not hasattr(ctx, "consts") or ctx.consts is None:
        ctx.consts = {}
    else:
        ctx.consts = dict(ctx.consts)
    csem = ctx.sem()
    gsem = ctx.sem()
    n = g = 0
    for name, spec in dram_consts.items():
        ap, shape, dt = spec[0], spec[1], spec[2]
        cast = len(spec) > 3 and spec[3] == "cast"
        t = ctx.sbuf(f"c_{name}", list(shape), dt)
        if cast:
            ctx.dma("gpsimd", t[:], ap, inc=gsem)
            g += 16
        else:
            ctx.dma("sync", t[:], ap, inc=csem)
            n += 16
        ctx.consts[name] = t
    for eng in ("vector", "scalar", "tensor", "gpsimd", "sync"):
        if n:
            ctx.wait(eng, csem, n)
        if g:
            ctx.wait(eng, gsem, g)
    return csem



def wt(ctx, eng, rec):
    if rec is None:
        return
    if isinstance(rec, list):
        for r in rec:
            wt(ctx, eng, r)
        return
    ctx.wait(eng, rec[0], rec[1])


def emit_p1(ctx, x, wq, wba, x_fm, q_fm, k_fm, k_tm, v_tm, z_fm,
            beta_o, cumg_o, qc_scr):
    identb = ctx.consts["ident"]
    wba_s = ctx.consts["wba"]

    xt = x.rearrange("(n p) d -> n p d", p=128)
    xn_all = ctx.sbuf("p1xn", [128, 16 * 2048], BF16)
    es_a = ExitStack()
    xbufs = [ctx.sbuf(f"p1x{i}", [128, 2048], F32, scope=es_a) for i in range(2)]
    xlds = [ctx.sem(f"p1xld{i}") for i in range(2)]
    xstage = [ctx.sbuf(f"p1xs{i}", [128, 2048], BF16, scope=es_a) for i in range(2)]
    xnbf = ctx.sbuf("p1xnbf", [128, 2048], BF16, scope=es_a)
    xsq = ctx.sbuf("p1xsq", [128, 2048], F32, scope=es_a)
    xbf = ctx.sbuf("p1xbf", [128, 2048], BF16, scope=es_a)
    stats = ctx.sbuf("p1st", [128, 8], F32, scope=es_a)
    ptr = [es_a.enter_context(ctx.nc.psum_tensor(f"p1pt{i}", [128, 1024], BF16))
           for i in range(4)]
    vch = ctx.sem("p1vch")
    ach = ctx.sem("p1ach")
    tch = ctx.sem("p1tch")
    xnr = ctx.sem("p1xnr")
    xst = [ctx.sem(f"p1xst{i}") for i in range(2)]
    rel_x = [None, None]
    rel_pt = [None] * 4
    rel_xbf = None
    rel_xnbf = None
    rel_stage = [None, None]
    for tt in range(16):
        sl = tt % 2
        b = xbufs[sl]
        wt(ctx, "sync", rel_x[sl])
        ctx.dma("sync", b[:], xt[tt], inc=xlds[sl])
        ctx.wait("vector", xlds[sl], 16 * (tt // 2 + 1))
        ctx.wait("scalar", xlds[sl], 16 * (tt // 2 + 1))
        wt(ctx, "scalar", rel_xbf)
        ctx.op("scalar", lambda e, b=b: e.activation(xbf[:], b[:], AF.Identity), inc=ach)
        xbf_ready = ("scalar", ach.n)
        ctx.op("vector", lambda e, b=b: e.tensor_reduce(stats[:, 0:1], b[:], AX.X, ALU.add), inc=vch)
        ctx.op("vector", lambda e, b=b: e.tensor_tensor(xsq[:], b[:], b[:], ALU.mult), inc=vch)
        ctx.wait("vector", vch, vch.n)
        ctx.op("vector", lambda e: e.tensor_reduce(stats[:, 1:2], xsq[:], AX.X, ALU.add), inc=vch)
        ctx.wait("vector", vch, vch.n)
        ctx.op("vector", lambda e: e.tensor_scalar(stats[:, 2:3], stats[:, 0:1], 1.0 / 2048, None, ALU.mult), inc=vch)
        ctx.op("vector", lambda e: e.tensor_scalar(stats[:, 3:4], stats[:, 1:2], 1.0 / 2048, None, ALU.mult), inc=vch)
        ctx.wait("vector", vch, vch.n)
        ctx.op("vector", lambda e: e.tensor_tensor(stats[:, 4:5], stats[:, 2:3], stats[:, 2:3], ALU.mult), inc=vch)
        ctx.wait("vector", vch, vch.n)
        ctx.op("vector", lambda e: e.tensor_tensor(stats[:, 5:6], stats[:, 3:4], stats[:, 4:5], ALU.subtract), inc=vch)
        ctx.wait("scalar", vch, vch.n)
        ctx.op("scalar", lambda e: e.activation(stats[:, 6:7], stats[:, 5:6], AF.Sqrt, bias=cv(ctx, 1e-5)), inc=ach)
        ctx.wait("vector", ach, ach.n)
        ctx.op("vector", lambda e: e.reciprocal(stats[:, 7:8], stats[:, 6:7]), inc=vch)
        ctx.wait("vector", vch, vch.n)
        wt(ctx, "vector", rel_xnbf)
        ctx.op("vector", lambda e, b=b: e.tensor_scalar(
            xnbf[:], b[:], stats[:, 2:3], stats[:, 7:8], ALU.subtract, ALU.mult), inc=vch)
        rel_x[sl] = [(vch, vch.n), (ach, xbf_ready[1])]
        xn_ready = vch.n
        ctx.wait("tensor", ach, xbf_ready[1])
        wt(ctx, "vector", rel_stage[sl])
        for g in range(2):
            pt = ptr[g]
            wt(ctx, "tensor", rel_pt[g])
            for c8 in range(8):
                c = g * 8 + c8
                ctx.op("tensor", lambda e, pt=pt, c8=c8, c=c: e.transpose(
                    pt[:, c8 * 128:(c8 + 1) * 128], xbf[:, c * 128:(c + 1) * 128],
                    identb[:]), inc=(tch if c8 == 7 else None))
            ctx.wait("vector", tch, tch.n)
            ctx.op("vector", lambda e, pt=pt, g=g, sl=sl: e.tensor_copy(
                xstage[sl][:, g * 1024:(g + 1) * 1024], pt[:]), inc=vch)
            rel_pt[g] = (vch, vch.n)
        rel_xbf = (tch, tch.n)
        ctx.wait("tensor", vch, xn_ready)
        for g in range(2):
            pt = ptr[2 + g]
            wt(ctx, "tensor", rel_pt[2 + g])
            for c8 in range(8):
                c = g * 8 + c8
                ctx.op("tensor", lambda e, pt=pt, c8=c8, c=c: e.transpose(
                    pt[:, c8 * 128:(c8 + 1) * 128], xnbf[:, c * 128:(c + 1) * 128],
                    identb[:]), inc=(tch if c8 == 7 else None))
            ctx.wait("vector", tch, tch.n)
            def cpxn(e, pt=pt, g=g, tt=tt):
                dst = xn_all[:].rearrange("p (c t) -> p c t", c=16)[
                    :, g * 8:(g + 1) * 8, tt * 128:(tt + 1) * 128]
                return e.tensor_copy(dst, pt[:].rearrange("p (c t) -> p c t", c=8))
            ctx.op("vector", cpxn, inc=vch)
            ctx.op("vector", lambda e: None, inc=None)
            ctx.q["vector"].pop()
            rel_pt[2 + g] = (vch, vch.n)
            xnr.n += 0
        rel_xnbf = (tch, tch.n)
        ctx.wait("scalar", vch, vch.n)
        def st_x(e, sl=sl, tt=tt):
            dst = x_fm.rearrange("(c p) t -> p c t", p=128)[:, :, tt * 128:(tt + 1) * 128]
            return e.dma_start(out=dst, in_=xstage[sl][:].rearrange("p (c t) -> p c t", c=16))
        ctx.op("scalar", st_x, inc=xst[sl], k=16)
        rel_stage[sl] = (xst[sl], 16 * (tt // 2 + 1))
    xn_done_rec = (vch, vch.n)

    es_a.close()
    p1a_done = [xn_done_rec, (ach, ach.n), (tch, tch.n),
                (xst[0], xst[0].n), (xst[1], xst[1].n)]
    mainA = ctx.psum("p1mA", [128, 1024], F32)
    mainB = ctx.psum("p1mB", [128, 1024], F32)
    es_stats = ExitStack()
    pstat = es_stats.enter_context(ctx.nc.psum_tensor("p1ps", [32, 2048], F32))
    pmm = ctx.sem("p1pmm")
    alog = ctx.consts["alog"]
    dtb = ctx.consts["dtb"]
    es_ba = ExitStack()
    wt(ctx, "tensor", p1a_done)
    for k in range(16):
        for n in range(4):
            pb = (mainA if n < 2 else mainB)[0:64, (n % 2) * 512:(n % 2) * 512 + 512]
            ctx.op("tensor", lambda e, pb=pb, k=k, n=n: e.matmul(
                pb, wba_s[:, k * 64:(k + 1) * 64],
                xn_all[:, k * 2048 + n * 512: k * 2048 + (n + 1) * 512],
                start=(k == 0), stop=(k == 15)),
                inc=(pmm if (k == 15 and n == 3) else None))
    ba_f = ctx.sbuf("p1baf", [64, 2048], F32, scope=es_ba)
    ctx.wait("scalar", pmm, pmm.n)
    for n in range(4):
        pb = (mainA if n < 2 else mainB)[0:64, (n % 2) * 512:(n % 2) * 512 + 512]
        ctx.op("scalar", lambda e, pb=pb, n=n: e.copy(ba_f[:, n * 512:(n + 1) * 512], pb),
               inc=ach)
    rel_mainA = ("x", ach, ach.n)[1:]
    rel_mainB = (ach, ach.n)
    bsb = ctx.sbuf("p1beta", [32, 2048], F32, scope=es_ba)
    ctx.wait("scalar", ach, ach.n)
    ctx.op("scalar", lambda e: e.activation(bsb[:], ba_f[0:32, :], AF.Sigmoid), inc=ach)
    beta_rec = (ach, ach.n)
    a_sb = ctx.sbuf("p1a", [32, 2048], F32, scope=es_ba)
    t_sb = ctx.sbuf("p1t", [32, 2048], F32, scope=es_ba)
    s_sb = a_sb
    gsb = ba_f[0:32, :]
    cgsb = ba_f[32:64, :]
    ones32 = ctx.sbuf("p1o32", [32, 2048], F32, scope=es_ba)
    nea = ctx.sbuf("p1nea", [32, 1], F32, scope=es_ba)
    ones_rdy = ctx.sem("p1or")
    wt(ctx, "gpsimd", p1a_done)
    ctx.op("gpsimd", lambda e: e.memset(ones32[:], 1.0), inc=ones_rdy)
    ctx.op("scalar", lambda e: e.activation(nea[:], alog[:], AF.Exp), inc=ach)
    ctx.wait("vector", ach, ach.n)
    ctx.op("vector", lambda e: e.tensor_scalar(nea[:], nea[:], -1.0, None, ALU.mult), inc=vch)
    nea_rec = (vch, vch.n)
    ctx.op("vector", lambda e: e.tensor_copy(a_sb[:], ba_f[32:64, :]), inc=vch)
    ctx.wait("vector", vch, vch.n)
    ctx.op("vector", lambda e: e.tensor_scalar(s_sb[:], a_sb[:], dtb[:], None, ALU.add), inc=vch)
    ctx.wait("scalar", vch, vch.n)
    ctx.op("scalar", lambda e: e.activation(t_sb[:], s_sb[:], AF.Abs), inc=ach)
    ctx.wait("scalar", ach, ach.n)
    ctx.op("scalar", lambda e: e.activation(t_sb[:], t_sb[:], AF.Exp, scale=cv(ctx, -1.0, 32)), inc=ach)
    ctx.wait("scalar", ach, ach.n)
    ctx.op("scalar", lambda e: e.activation(t_sb[:], t_sb[:], AF.Ln, bias=cv(ctx, 1.0, 32)), inc=ach)
    wt(ctx, "vector", beta_rec)
    ctx.op("vector", lambda e: e.tensor_scalar(gsb[:], s_sb[:], 0.0, None, ALU.max), inc=vch)
    ctx.wait("vector", ach, ach.n)
    ctx.wait("vector", vch, vch.n)
    ctx.op("vector", lambda e: e.tensor_tensor(gsb[:], gsb[:], t_sb[:], ALU.add), inc=vch)
    ctx.wait("vector", vch, vch.n)
    wt(ctx, "vector", nea_rec)
    ctx.op("vector", lambda e: e.tensor_scalar(gsb[:], gsb[:], nea[:], None, ALU.mult), inc=vch)
    ctx.wait("vector", vch, vch.n)
    ctx.wait("vector", ones_rdy, 1)
    ctx.op("vector", lambda e: e.tensor_tensor_scan(
        cgsb[:], ones32[:], gsb[:], 0.0, ALU.mult, ALU.add), inc=vch)
    bgst = ctx.sem("p1bg")
    wt(ctx, "sync", beta_rec)
    ctx.wait("sync", vch, vch.n)
    ctx.dma("sync", beta_o, bsb[:], inc=bgst)
    ctx.dma("sync", cumg_o, cgsb[:], inc=bgst)
    if getattr(ctx, "dbg_g", None) is not None:
        ctx.dma("sync", ctx.dbg_g, gsb, inc=bgst)
        ctx.dma("sync", ctx.dbg_ones, ones32[:], inc=bgst)
    return dict(vch=vch, ach=ach, tch=tch, pmm=pmm,
                rel_mainA=rel_mainA, rel_mainB=rel_mainB,
                mainA=mainA, mainB=mainB, pstat=pstat, es_stats=es_stats,
                xn_all=xn_all, xn_done_rec=xn_done_rec, bgst=bgst, es_ba=es_ba)


def emit_p1b(ctx, h, wq, q_fm, k_fm, k_tm, v_tm, z_fm, qc_scr):
    identb = ctx.consts["ident"]
    onehotb = ctx.consts["onehotb"]
    colones = ctx.consts["colones"]
    cw = ctx.consts["cw"]
    cb = ctx.consts["cb"]
    vch, ach, tch, pmm = h["vch"], h["ach"], h["tch"], h["pmm"]
    mainA, mainB, pstat = h["mainA"], h["mainB"], h["pstat"]
    xn_all = h["xn_all"]

    h["es_ba"].close()
    wqt = wq.rearrange("(m k p) f -> m p k f", m=96, k=16)
    wbuf = [ctx.sbuf(f"p1w{i}", [128, 2048], BF16) for i in range(2)]
    wlds = [ctx.sem(f"p1wld{i}") for i in range(2)]
    qcf = ctx.sbuf("p1qcf", [128, 2048], BF16)
    accb = ctx.sbuf("p1acc", [128, 2048], BF16)
    qcb = [ctx.sbuf(f"p1qcb{i}", [128, 2048], BF16) for i in range(2)]
    sqb = [ctx.sbuf(f"p1sqb{i}", [128, 2048], BF16) for i in range(2)]
    vstage = ctx.sbuf("p1vst", [128, 2048], BF16)
    qcst = [ctx.sem(f"p1qst{i}") for i in range(2)]
    vtst = ctx.sem("p1vts")
    statdone = ctx.sem("p1sd")
    vt_evac = ctx.sem("p1vte")
    rstd_f = ctx.sbuf("p1rstd", [32, 2048], F32)
    rstd_b = ctx.sbuf("p1rstdb", [32, 2048], BF16)

    state = dict(rel_qc=[None, None], rel_sq=[None, None], rel_qcf=None,
                 rel_acc=None, rel_vtp=None, rel_vstage=None,
                 rel_w=[None, None], rel_mainA=h["rel_mainA"],
                 rel_mainB=h["rel_mainB"], rstd_rec=None, rstdb_rec=None)
    pe_post = []
    vtp_holder = {}

    def emit_rstd():
        ctx.wait("scalar", pmm, pmm.n)
        ctx.op("scalar", lambda e: e.activation(rstd_f[:], pstat[:], AF.Sqrt, bias=cv(ctx, 1e-6, 32)),
               inc=statdone)
        ctx.wait("vector", statdone, 1)
        ctx.op("vector", lambda e: e.reciprocal(rstd_f[:], rstd_f[:]), inc=vch)
        ctx.wait("vector", vch, vch.n)
        ctx.op("vector", lambda e: e.tensor_copy(rstd_b[:], rstd_f[:]), inc=vch)
        state["rstdb_rec"] = (vch, vch.n)

    for m in range(96):
        sl = m % 2
        wb = wbuf[sl]
        if m == 0:
            ctx.wait("gpsimd", h["bgst"], h["bgst"].n)
        wt(ctx, "gpsimd", state["rel_w"][sl])
        ctx.op("gpsimd", lambda e, wb=wb, m=m: e.dma_start(
            out=wb[:].rearrange("p (k f) -> p k f", k=16), in_=wqt[m]),
            inc=wlds[sl], k=16)
        ctx.wait("tensor", wlds[sl], 16 * (m // 2 + 1))
        if m == 0:
            wt(ctx, "tensor", h["xn_done_rec"])
        for half in range(2):
            pmain = mainA if half == 0 else mainB
            wt(ctx, "tensor", state["rel_mainA" if half == 0 else "rel_mainB"])
            for k in range(16):
                for n2 in range(2):
                    n = half * 2 + n2
                    ctx.op("tensor", lambda e, pmain=pmain, wb=wb, k=k, n2=n2, n=n: e.matmul(
                        pmain[:, n2 * 512:(n2 + 1) * 512],
                        wb[:, k * 128:(k + 1) * 128],
                        xn_all[:, k * 2048 + n * 512: k * 2048 + (n + 1) * 512],
                        start=(k == 0), stop=(k == 15)),
                        inc=(pmm if (k == 15 and n2 == 1) else None))
        state["rel_w"][sl] = (pmm, pmm.n)
        mm_half = [pmm.n - 1, pmm.n]
        for fn in pe_post:
            fn()
        pe_post = []
        if m == 32:
            emit_rstd()
            h["es_stats"].close()
            vtp_holder["vtp"] = ctx.psum("p1vtp", [128, 2048], BF16)

        if m < 64:
            qc = qcb[sl]
            ctx.wait("scalar", pmm, mm_half[0])
            wt(ctx, "scalar", state["rel_qcf"])
            ctx.op("scalar", lambda e: e.copy(qcf[:, 0:1024], mainA[:]), inc=ach)
            state["rel_mainA"] = (ach, ach.n)
            ctx.wait("scalar", pmm, mm_half[1])
            ctx.op("scalar", lambda e: e.copy(qcf[:, 1024:2048], mainB[:]), inc=ach)
            state["rel_mainB"] = (ach, ach.n)
            qcf_rdy = ach.n
            ctx.wait("vector", ach, qcf_rdy)
            wt(ctx, "vector", state["rel_acc"])
            cwm = [cw[:, m * 4 + j: m * 4 + j + 1] for j in (3, 2, 1, 0)]
            ctx.op("vector", lambda e, c=cwm[0]: e.tensor_scalar(accb[:], qcf[:], c, None, ALU.mult), inc=vch)
            ctx.op("vector", lambda e, c=cwm[1]: e.scalar_tensor_tensor(
                accb[:, 1:2048], qcf[:, 0:2047], c, accb[:, 1:2048], ALU.mult, ALU.add), inc=vch)
            ctx.op("vector", lambda e, c=cwm[2]: e.scalar_tensor_tensor(
                accb[:, 2:2048], qcf[:, 0:2046], c, accb[:, 2:2048], ALU.mult, ALU.add), inc=vch)
            ctx.op("vector", lambda e, c=cwm[3]: e.scalar_tensor_tensor(
                accb[:, 3:2048], qcf[:, 0:2045], c, accb[:, 3:2048], ALU.mult, ALU.add), inc=vch)
            state["rel_qcf"] = (vch, vch.n)
            ctx.wait("scalar", vch, vch.n)
            wt(ctx, "scalar", state["rel_qc"][sl])
            ctx.op("scalar", lambda e, qc=qc, m=m: e.activation(
                qc[:], accb[:], AF.Silu, bias=cb[:, m:m + 1]), inc=ach)
            state["rel_acc"] = (ach, ach.n)
            silu_rdy = ach.n
            if m < 32:
                sq = sqb[sl]
                ctx.wait("sync", ach, silu_rdy)
                ctx.dma("sync", qc_scr.rearrange("(m p) t -> m p t", p=128)[m], qc[:],
                        inc=qcst[sl])
                wt(ctx, "scalar", state["rel_sq"][sl])
                ctx.op("scalar", lambda e, qc=qc, sq=sq: e.activation(sq[:], qc[:], AF.Square),
                       inc=ach)
                sq_rdy = ach.n
                state["rel_qc"][sl] = [(qcst[sl], 16 * ((m // 2) + 1)), (ach, sq_rdy)]

                def stats_mms(m=m, sq=sq, sl=sl, sq_rdy=sq_rdy):
                    ctx.wait("tensor", ach, sq_rdy)
                    for n in range(4):
                        ctx.op("tensor", lambda e, m=m, sq=sq, n=n: e.matmul(
                            pstat[:, n * 512:(n + 1) * 512],
                            colones[:, m * 32:(m + 1) * 32],
                            sq[:, n * 512:(n + 1) * 512],
                            start=(m == 0), stop=(m == 31)),
                            inc=(pmm if n == 3 else None))
                    state["rel_sq"][sl] = (pmm, pmm.n)
                pe_post.append(stats_mms)
            else:
                vh = m - 32

                def v_post(m=m, vh=vh, silu_rdy=silu_rdy, qc=qc, sl=sl):
                    vtp = vtp_holder["vtp"]
                    ctx.wait("tensor", ach, silu_rdy)
                    ctx.wait("tensor", statdone, 1)
                    wt(ctx, "tensor", state["rel_vtp"])
                    for tc in range(16):
                        ctx.op("tensor", lambda e, tc=tc, qc=qc, vtp=vtp: e.transpose(
                            vtp[:, tc * 128:(tc + 1) * 128],
                            qc[:, tc * 128:(tc + 1) * 128], identb[:]),
                            inc=(tch if tc == 15 else None))
                    state["rel_qc"][sl] = (tch, tch.n)
                    ctx.wait("vector", tch, tch.n)
                    wt(ctx, "vector", state["rel_vstage"])
                    ctx.op("vector", lambda e, vtp=vtp: e.tensor_copy(vstage[:], vtp[:]),
                           inc=vt_evac)
                    state["rel_vtp"] = (vt_evac, vt_evac.n)
                    ctx.wait("sync", vt_evac, vt_evac.n)

                    def st_v(e, vh=vh):
                        dst = v_tm.rearrange("(tc v t) d -> v t tc d", tc=16, v=32, t=128)[vh]
                        return e.dma_start(
                            out=dst, in_=vstage[:].rearrange("p (tc d) -> p tc d", tc=16))
                    ctx.op("sync", st_v, inc=vtst, k=16)
                    state["rel_vstage"] = (vtst, vtst.n)
                pe_post.append(v_post)
        else:
            zh = m - 64
            qc = qcb[sl]
            ctx.wait("scalar", pmm, mm_half[0])
            wt(ctx, "scalar", state["rel_qc"][sl])
            ctx.op("scalar", lambda e, qc=qc: e.copy(qc[:, 0:1024], mainA[:]), inc=ach)
            state["rel_mainA"] = (ach, ach.n)
            ctx.wait("scalar", pmm, mm_half[1])
            ctx.op("scalar", lambda e, qc=qc: e.copy(qc[:, 1024:2048], mainB[:]), inc=ach)
            state["rel_mainB"] = (ach, ach.n)
            ctx.wait("sync", ach, ach.n)
            ctx.dma("sync", z_fm.rearrange("(z p) t -> z p t", p=128)[zh], qc[:],
                    inc=qcst[sl])
            state["rel_qc"][sl] = (qcst[sl], 16 * (16 + zh // 2 + 1))
    for fn in pe_post:
        fn()

    qa = [ctx.sbuf(f"p1qa{i}", [128, 2048], BF16) for i in range(2)]
    qh = [ctx.sbuf(f"p1qh{i}", [128, 2048], BF16) for i in range(2)]
    kst = ctx.sem("p1kst")
    qlds = [ctx.sem(f"p1qld{i}") for i in range(2)]
    qsts = [ctx.sem(f"p1qsts{i}") for i in range(2)]
    rel_qa = [None, None]
    rel_qh = [None, None]
    DKs = float(DK) ** -0.5
    vtp = vtp_holder["vtp"]
    for m in range(32):
        sl = m % 2
        a = qa[sl]
        qout = qh[sl]
        wt(ctx, "sync", rel_qa[sl])
        ctx.dma("sync", a[:], qc_scr.rearrange("(m p) t -> m p t", p=128)[m], inc=qlds[sl])
        ctx.wait("tensor", qlds[sl], 16 * (m // 2 + 1))
        wt(ctx, "tensor", state["rstdb_rec"])
        for n in range(4):
            pmain = mainA if n < 2 else mainB
            wt(ctx, "tensor", state["rel_mainA" if n < 2 else "rel_mainB"])
            ctx.op("tensor", lambda e, pmain=pmain, n=n, m=m: e.matmul(
                pmain[:, (n % 2) * 512:(n % 2) * 512 + 512],
                onehotb[:, m * 128:(m + 1) * 128],
                rstd_b[:, n * 512:(n + 1) * 512],
                start=True, stop=True), inc=(pmm if n == 3 else None))
        ctx.wait("vector", pmm, pmm.n)
        wt(ctx, "vector", rel_qh[sl])
        cmul = DKs if m < 16 else 1.0
        for n in range(4):
            pmain = mainA if n < 2 else mainB
            ctx.op("vector", lambda e, pmain=pmain, n=n, a=a, qout=qout, cmul=cmul:
                   e.scalar_tensor_tensor(
                       qout[:, n * 512:(n + 1) * 512], a[:, n * 512:(n + 1) * 512],
                       cmul, pmain[:, (n % 2) * 512:(n % 2) * 512 + 512],
                       ALU.mult, ALU.mult), inc=vch)
        rel_qa[sl] = (vch, vch.n)
        state["rel_mainA"] = (vch, vch.n)
        state["rel_mainB"] = (vch, vch.n)
        ctx.wait("sync", vch, vch.n)
        if m < 16:
            ctx.dma("sync", q_fm.rearrange("(h p) t -> h p t", p=128)[m], qout[:],
                    inc=qsts[sl])
            rel_qh[sl] = (qsts[sl], 16 * (m // 2 + 1))
        else:
            kh = m - 16
            ctx.dma("sync", k_fm.rearrange("(h p) t -> h p t", p=128)[kh], qout[:],
                    inc=qsts[sl])
            ctx.wait("tensor", vch, vch.n)
            wt(ctx, "tensor", state["rel_vtp"])
            for tc in range(16):
                ctx.op("tensor", lambda e, tc=tc, qout=qout: e.transpose(
                    vtp[:, tc * 128:(tc + 1) * 128], qout[:, tc * 128:(tc + 1) * 128],
                    identb[:]), inc=(tch if tc == 15 else None))
            rel_qh[sl] = [(qsts[sl], 16 * (m // 2 + 1)), (tch, tch.n)]
            ctx.wait("vector", tch, tch.n)
            wt(ctx, "vector", state["rel_vstage"])
            ctx.op("vector", lambda e: e.tensor_copy(vstage[:], vtp[:]), inc=vt_evac)
            state["rel_vtp"] = (vt_evac, vt_evac.n)
            ctx.wait("scalar", vt_evac, vt_evac.n)

            def st_k(e, kh=kh):
                dst = k_tm.rearrange("(tc h t) d -> h t tc d", tc=16, h=16, t=128)[kh]
                return e.dma_start(out=dst,
                                   in_=vstage[:].rearrange("p (tc d) -> p tc d", tc=16))
            ctx.op("scalar", st_k, inc=kst, k=16)
            state["rel_vstage"] = (kst, kst.n)
    return dict(qsts=qsts, kst=kst, vtst=vtst, qcst=qcst,
                done=[(qsts[0], qsts[0].n), (qsts[1], qsts[1].n),
                      (kst, kst.n), (vtst, vtst.n),
                      (qcst[0], qcst[0].n), (qcst[1], qcst[1].n),
                      (h["bgst"], h["bgst"].n)])



def emit_p2(ctx, q_fm, k_fm, k_tm, v_tm, z_fm, beta_i, cumg_i, o_norm, nch=16, nvh=32, stage=9):
    identb = ctx.consts["ident"]
    identf = ctx.consts["identf"]
    onesf = ctx.consts["onesf"]
    onehotf = ctx.consts["onehotf"]
    maskS = ctx.consts["maskS"]
    maskIT = ctx.consts["maskIT"]
    normw = ctx.consts["normw"]

    vch = ctx.sem("p2v")
    ach = ctx.sem("p2a")
    tch = ctx.sem("p2t")

    S_all = ctx.sbuf("p2S", [128, 32 * 128], F32)
    beta_sb = ctx.sbuf("p2beta", [32, 2048], F32)
    cumg_sb = ctx.sbuf("p2cumg", [32, 2048], F32)
    sinit = ctx.sem("p2si")
    ctx.op("gpsimd", lambda e: e.memset(S_all[:], 0.0), inc=sinit)
    ctx.wait("vector", sinit, 1)
    ost_init = ctx.sem("p2oi")
    ldrows = ctx.sem("p2ldr")
    ctx.dma("sync", beta_sb[:], beta_i, inc=ldrows)
    ctx.dma("sync", cumg_sb[:], cumg_i, inc=ldrows)

    qc_b = [ctx.sbuf(f"p2q{i}", [128, 2048], BF16) for i in range(2)]
    kc_b = [ctx.sbuf(f"p2k{i}", [128, 2048], BF16) for i in range(2)]
    kt_b = [ctx.sbuf(f"p2kt{i}", [128, 2048], BF16) for i in range(2)]
    vt_b = [ctx.sbuf(f"p2vt{i}", [128, 4096], BF16) for i in range(2)]
    zc_b = [ctx.sbuf(f"p2z{i}", [128, 4096], BF16) for i in range(2)]
    lds = [[ctx.sem(f"p2ld{i}_{j}") for j in range(5)] for i in range(2)]
    rel_in = [None, None]

    rows = ctx.sbuf("p2rows", [32, 5 * 128], F32)
    rowsT = ctx.sbuf("p2rowsT", [128, 5 * 32], F32)
    tD = ctx.sbuf("p2tD", [128, 256], F32)
    eD = ctx.sbuf("p2eD", [128, 256], F32)
    n_pp = [ctx.sbuf(f"p2n{i}", [128, 128], BF16) for i in range(2)]
    nt_pow = ctx.sbuf("p2ntp", [128, 7 * 128], BF16)
    aq_bf = ctx.sbuf("p2aq", [128, 128], BF16)
    x_bf = [ctx.sbuf(f"p2x{i}", [128, 256], BF16) for i in range(2)]
    sbf = ctx.sbuf("p2sbf", [128, 128], BF16)
    wfm = ctx.sbuf("p2wfm", [128, 128], BF16)
    dl_bf = ctx.sbuf("p2dl", [128, 128], BF16)
    dls_bf = ctx.sbuf("p2dls", [128, 128], BF16)
    t1 = ctx.sbuf("p2t1", [128, 128], F32)
    otm = ctx.sbuf("p2otm", [128, 128], BF16)
    zs = ctx.sbuf("p2zs", [128, 128], F32)
    t2 = ctx.sbuf("p2t2", [128, 128], F32)
    sq2 = ctx.sbuf("p2sq2", [128, 128], BF16)
    rrow = ctx.sbuf("p2rr", [1, 128], F32)
    ecc = ctx.sbuf("p2ec", [128, 1], F32)
    ostage = [ctx.sbuf(f"p2os{i}", [128, 4096], BF16) for i in range(2)]
    for i in range(2):
        ctx.op("gpsimd", lambda e, i=i: e.memset(ostage[i][:], 0.0), inc=ost_init)
    ctx.wait("vector", ost_init, 2)

    B0 = ctx.psum("p2B0", [128, 512], F32)
    B1 = ctx.psum("p2B1", [128, 512], F32)
    B2 = ctx.psum("p2B2", [128, 512], F32)
    B3 = ctx.psum("p2B3", [128, 512], F32)
    B4 = ctx.psum("p2B4", [128, 512], F32)
    B5 = ctx.psum("p2B5", [128, 512], F32)
    B6 = ctx.psum("p2B6", [128, 512], F32)
    B7 = ctx.psum("p2B7", [128, 1024], BF16)

    rel = {k: None for k in
           ["B0", "B1", "B2", "B3", "B4", "B5", "B6", "B7",
            "rows", "rowsT", "tD", "eD", "aq", "sbf", "wfm", "dl", "dls",
            "t1", "otm", "zs", "t2", "sq2", "rrow", "ecc", "ntpow"]}
    rel["npp"] = [None, None]
    rel["x"] = [None, None]
    rel["ost"] = [None, None]
    ost_sem = ctx.sem("p2ost")

    for tc in range(nch):
        sl = tc % 2
        wt(ctx, "sync", rel_in[sl])
        ctx.op("sync", lambda e, sl=sl, tc=tc: e.dma_start(
            out=qc_b[sl][:].rearrange("p (h t) -> p h t", h=16),
            in_=q_fm.rearrange("(h p) t -> p h t", p=128)[:, :, tc * 128:(tc + 1) * 128]),
            inc=lds[sl][0], k=16)
        ctx.op("sync", lambda e, sl=sl, tc=tc: e.dma_start(
            out=kc_b[sl][:].rearrange("p (h t) -> p h t", h=16),
            in_=k_fm.rearrange("(h p) t -> p h t", p=128)[:, :, tc * 128:(tc + 1) * 128]),
            inc=lds[sl][1], k=16)
        ctx.op("sync", lambda e, sl=sl, tc=tc: e.dma_start(
            out=kt_b[sl][:].rearrange("p (h d) -> p h d", h=16),
            in_=k_tm.rearrange("(tc h t) d -> tc t h d", tc=16, h=16)[tc]),
            inc=lds[sl][2], k=16)
        ctx.op("sync", lambda e, sl=sl, tc=tc: e.dma_start(
            out=vt_b[sl][:].rearrange("p (v d) -> p v d", v=32),
            in_=v_tm.rearrange("(tc v t) d -> tc t v d", tc=16, v=32)[tc]),
            inc=lds[sl][3], k=16)
        ctx.op("sync", lambda e, sl=sl, tc=tc: e.dma_start(
            out=zc_b[sl][:].rearrange("p (v t) -> p v t", v=32),
            in_=z_fm.rearrange("(v p) t -> p v t", p=128)[:, :, tc * 128:(tc + 1) * 128]),
            inc=lds[sl][4], k=16)
        nld = 16 * (tc // 2 + 1)
        qc, kc, kt, vt, zc = (b[sl] for b in (qc_b, kc_b, kt_b, vt_b, zc_b))
        rel_in[sl] = []

        cgl = rows[:, 0:128]
        gam = rows[:, 128:256]
        eg = rows[:, 256:384]
        bgn = rows[:, 384:512]
        bet = rows[:, 512:640]
        if tc == 0:
            ctx.wait("vector", ldrows, 32)
        wt(ctx, "vector", rel["rows"])
        if tc == 0:
            ctx.op("vector", lambda e: e.tensor_copy(cgl, cumg_sb[:, 0:128]), inc=vch)
        else:
            ctx.op("vector", lambda e, tc=tc: e.tensor_scalar(
                cgl, cumg_sb[:, tc * 128:(tc + 1) * 128],
                cumg_sb[:, tc * 128 - 1:tc * 128], None, ALU.subtract), inc=vch)
        ctx.op("vector", lambda e, tc=tc: e.tensor_copy(
            bet, beta_sb[:, tc * 128:(tc + 1) * 128]), inc=vch)
        cgl_rdy = vch.n
        ctx.wait("scalar", vch, cgl_rdy)
        ctx.op("scalar", lambda e: e.activation(gam, cgl, AF.Exp), inc=ach)
        ctx.op("scalar", lambda e: e.activation(
            eg, cgl, AF.Exp, scale=cv(ctx, -1.0, 32), bias=cgl[:, 127:128]), inc=ach)
        ctx.wait("vector", ach, ach.n)
        ctx.op("vector", lambda e: e.scalar_tensor_tensor(
            bgn, gam, -1.0, bet, ALU.mult, ALU.mult), inc=vch)
        rows_rdy_v = vch.n
        ctx.wait("tensor", vch, rows_rdy_v)
        wt(ctx, "tensor", rel["B0"])
        wt(ctx, "tensor", rel["rowsT"])
        for j in range(5):
            ctx.op("tensor", lambda e, j=j: e.transpose(
                B0[:, j * 32:(j + 1) * 32], rows[:, j * 128:(j + 1) * 128],
                identf[0:32, 0:32]), inc=(tch if j == 4 else None))
        rowsT_mm = tch.n
        ctx.wait("vector", tch, rowsT_mm)
        ctx.op("vector", lambda e: e.tensor_copy(rowsT[:], B0[:, 0:160]), inc=vch)
        rel["rows"] = (tch, rowsT_mm)
        rel["B0"] = (vch, vch.n)
        rowsT_rdy = vch.n
        cglT = rowsT[:, 0:32]
        gamT = rowsT[:, 32:64]
        egT = rowsT[:, 64:96]
        bgnT = rowsT[:, 96:128]
        betT = rowsT[:, 128:160]

        ost = ostage[sl]
        wt(ctx, "vector", rel["ost"][sl])

        for vh in range(nvh):
            kh = vh // 2
            pair = (vh // 2) % 2
            pb = B1 if pair == 0 else B2
            pbk = "B1" if pair == 0 else "B2"
            if vh % 2 == 0:
                ctx.wait("tensor", lds[sl][0], nld)
                ctx.wait("tensor", lds[sl][1], nld)
                wt(ctx, "tensor", rel[pbk])
                ctx.op("tensor", lambda e, pb=pb, kc=kc, kh=kh: e.matmul(
                    pb[:, 0:128], kc[:, kh * 128:(kh + 1) * 128],
                    kc[:, kh * 128:(kh + 1) * 128], start=True, stop=True))
                ctx.op("tensor", lambda e, pb=pb, kc=kc, qc=qc, kh=kh: e.matmul(
                    pb[:, 128:256], kc[:, kh * 128:(kh + 1) * 128],
                    qc[:, kh * 128:(kh + 1) * 128], start=True, stop=True), inc=tch)
            bcol = 256 + (vh % 2) * 128
            ctx.wait("tensor", vch, rowsT_rdy)
            wt(ctx, "tensor", rel[pbk])
            ctx.op("tensor", lambda e, pb=pb, vh=vh, bcol=bcol, cgl=cgl: e.matmul(
                pb[:, bcol:bcol + 128], onehotf[:, vh * 128:(vh + 1) * 128],
                cgl, start=True, stop=True), inc=tch)
            mm_pair = tch.n
            if stage <= 1:
                continue
            ctx.wait("vector", tch, mm_pair)
            wt(ctx, "vector", rel["tD"])
            ctx.op("vector", lambda e, pb=pb, bcol=bcol, vh=vh, cglT=cglT: e.scalar_tensor_tensor(
                tD[:, 0:128], pb[:, bcol:bcol + 128], cglT[:, vh:vh + 1], maskS[:],
                ALU.subtract, ALU.add), inc=vch)
            ctx.op("vector", lambda e, pb=pb, bcol=bcol, vh=vh, cglT=cglT: e.scalar_tensor_tensor(
                tD[:, 128:256], pb[:, bcol:bcol + 128], cglT[:, vh:vh + 1], maskIT[:],
                ALU.subtract, ALU.subtract), inc=vch)
            tD_rdy = vch.n
            ctx.wait("scalar", vch, tD_rdy)
            wt(ctx, "scalar", rel["eD"])
            ctx.op("scalar", lambda e: e.activation(eD[:, 0:128], tD[:, 0:128],
                                                    AF.Exp, scale=cv(ctx, -1.0)), inc=ach)
            ctx.op("scalar", lambda e: e.activation(eD[:, 128:256], tD[:, 128:256],
                                                    AF.Exp), inc=ach)
            rel["tD"] = (ach, ach.n)
            eD_rdy = ach.n
            if stage <= 2:
                continue
            ctx.wait("vector", ach, eD_rdy)
            wt(ctx, "vector", rel["npp"][0])
            ctx.op("vector", lambda e, pb=pb, vh=vh, betT=betT: e.scalar_tensor_tensor(
                n_pp[0][:], eD[:, 0:128], betT[:, vh:vh + 1], pb[:, 0:128],
                ALU.mult, ALU.mult), inc=vch)
            n1_rdy = vch.n
            wt(ctx, "vector", rel["aq"])
            ctx.op("vector", lambda e, pb=pb: e.tensor_tensor(
                aq_bf[:], eD[:, 128:256], pb[:, 128:256], ALU.mult), inc=vch)
            rel["eD"] = (vch, vch.n)
            rel[pbk] = (vch, vch.n)
            aq_rdy = vch.n
            if stage <= 3:
                continue
            ctx.wait("tensor", vch, n1_rdy)
            wt(ctx, "tensor", rel["B7"])
            wt(ctx, "tensor", rel["ntpow"])
            ctx.op("tensor", lambda e: e.transpose(B7[:, 0:128], n_pp[0][:], identb[:]),
                   inc=tch)
            nt1_mm = tch.n
            ctx.wait("scalar", tch, nt1_mm)
            wt(ctx, "scalar", rel["ntpow"])
            ctx.op("scalar", lambda e: e.copy(nt_pow[:, 0:128], B7[:, 0:128]), inc=ach)
            rel["B7"] = (ach, ach.n)
            nt1_rdy = ach.n
            if stage <= 4:
                continue
            wt(ctx, "scalar", rel["x"][0])
            ctx.wait("scalar", lds[sl][3], nld)
            ctx.op("scalar", lambda e, vt=vt, vh=vh, betT=betT: e.activation(
                x_bf[0][:, 0:128], vt[:, vh * 128:(vh + 1) * 128], AF.Copy,
                scale=betT[:, vh:vh + 1]), inc=ach)
            ctx.wait("scalar", lds[sl][2], nld)
            ctx.op("scalar", lambda e, kt=kt, kh=kh, vh=vh, bgnT=bgnT: e.activation(
                x_bf[0][:, 128:256], kt[:, kh * 128:(kh + 1) * 128], AF.Copy,
                scale=bgnT[:, vh:vh + 1]), inc=ach)
            x_rdy = ach.n
            if stage <= 5:
                continue

            cur = 0
            n_rdy = n1_rdy
            nt_rdy = nt1_rdy
            ntpow_reads = []
            for li in range(6):
                nxt = 1 - cur
                pq = B3 if li % 2 == 0 else B4
                pqk = "B3" if li % 2 == 0 else "B4"
                if isinstance(n_rdy, tuple):
                    ctx.wait("tensor", vch if n_rdy[0] == "v" else ach, n_rdy[1])
                else:
                    ctx.wait("tensor", vch, n_rdy)
                if isinstance(nt_rdy, tuple):
                    ctx.wait("tensor", vch if nt_rdy[0] == "v" else ach, nt_rdy[1])
                else:
                    ctx.wait("tensor", ach, nt_rdy)
                wt(ctx, "tensor", rel[pqk])
                ntsl = nt_pow[:, li * 128:(li + 1) * 128]
                ntdst = nt_pow[:, (li + 1) * 128:(li + 2) * 128]
                if li < 5:
                    ctx.op("tensor", lambda e, pq=pq, ntsl=ntsl, cur=cur: e.matmul(
                        pq[:, 0:128], ntsl, n_pp[cur][:], start=True, stop=True))
                ctx.op("tensor", lambda e, pq=pq, ntsl=ntsl, cur=cur: e.matmul(
                    pq[:, 128:256], n_pp[cur][:], ntsl, start=True, stop=True), inc=tch)
                sqmm = tch.n
                ntpow_reads.append((tch, sqmm))
                rel["npp"][cur] = (tch, sqmm)
                if li % 2 == 0:
                    ctx.wait("vector", tch, sqmm)
                    if li < 5:
                        wt(ctx, "vector", rel["npp"][nxt])
                        ctx.op("vector", lambda e, pq=pq, nxt=nxt: e.tensor_copy(
                            n_pp[nxt][:], pq[:, 0:128]), inc=vch)
                        n_rdy = vch.n
                    ctx.op("vector", lambda e, pq=pq, ntdst=ntdst: e.tensor_copy(
                        ntdst, pq[:, 128:256]), inc=vch)
                    nt_rdy = ("v", vch.n)
                    rel[pqk] = (vch, vch.n)
                else:
                    ctx.wait("scalar", tch, sqmm)
                    if li < 5:
                        wt(ctx, "scalar", rel["npp"][nxt])
                        ctx.op("scalar", lambda e, pq=pq, nxt=nxt: e.copy(
                            n_pp[nxt][:], pq[:, 0:128]), inc=ach)
                        n_rdy = ("a", ach.n)
                    ctx.op("scalar", lambda e, pq=pq, ntdst=ntdst: e.copy(
                        ntdst, pq[:, 128:256]), inc=ach)
                    nt_rdy = ("a", ach.n)
                    rel[pqk] = (ach, ach.n)
                cur = nxt
            nt_all_rdy = nt_rdy if not isinstance(nt_rdy, tuple) else None
            nt_all_rec = ((vch if nt_rdy[0] == "v" else ach, nt_rdy[1])
                          if isinstance(nt_rdy, tuple) else (ach, nt_rdy))
            if stage <= 6:
                continue

            xcur = 0
            x_cur_rdy = x_rdy
            ap_engine_flip = 0
            for ai, p in enumerate([64, 32, 16, 8, 4, 2, 1]):
                li = {1: 0, 2: 1, 4: 2, 8: 3, 16: 4, 32: 5, 64: 6}[p]
                px = B5 if ai % 2 == 0 else B6
                pxk = "B5" if ai % 2 == 0 else "B6"
                ntsl = nt_pow[:, li * 128:(li + 1) * 128]
                wt(ctx, "tensor", nt_all_rec)
                ctx.wait("tensor", ach, x_cur_rdy if ai == 0 else 0)
                ctx.wait("tensor", vch, x_cur_rdy if ai > 0 else 0)
                wt(ctx, "tensor", rel[pxk])
                ctx.op("tensor", lambda e, px=px, ntsl=ntsl, xcur=xcur: e.matmul(
                    px[:, 0:256], ntsl, x_bf[xcur][:], start=True, stop=True), inc=tch)
                apmm = tch.n
                ntpow_reads.append((tch, apmm))
                xnxt = 1 - xcur
                ctx.wait("vector", tch, apmm)
                wt(ctx, "vector", rel["x"][xnxt])
                op = ALU.add if p >= 2 else ALU.subtract
                ctx.op("vector", lambda e, px=px, xcur=xcur, xnxt=xnxt, op=op: e.tensor_tensor(
                    x_bf[xnxt][:], x_bf[xcur][:], px[:, 0:256], op), inc=vch)
                rel["x"][xcur] = (vch, vch.n)
                rel[pxk] = (vch, vch.n)
                x_cur_rdy = vch.n
                xcur = xnxt
            rel["ntpow"] = ntpow_reads[-1]
            xfin = xcur
            Ux = x_bf[xfin][:, 0:128]
            Wx = x_bf[xfin][:, 128:256]
            if stage <= 7:
                continue

            ctx.wait("vector", vch, x_cur_rdy)
            wt(ctx, "vector", rel["sbf"])
            ctx.op("vector", lambda e, vh=vh: e.tensor_copy(
                sbf[:], S_all[:, vh * 128:(vh + 1) * 128]), inc=vch)
            sbf_rdy = vch.n
            ctx.wait("tensor", vch, x_cur_rdy)
            wt(ctx, "tensor", rel["B7"])
            ctx.op("tensor", lambda e, Wx=Wx: e.transpose(B7[:, 128:256], Wx, identb[:]),
                   inc=tch)
            wmm = tch.n
            ctx.wait("scalar", tch, wmm)
            wt(ctx, "scalar", rel["wfm"])
            ctx.op("scalar", lambda e: e.copy(wfm[:], B7[:, 128:256]), inc=ach)
            rel["B7"] = (ach, ach.n)
            wfm_rdy = ach.n
            if tc == 0:
                rel["wfm"] = (ach, wfm_rdy)
            if tc > 0:
                ctx.wait("tensor", ach, wfm_rdy)
                ctx.wait("tensor", vch, sbf_rdy)
                wt(ctx, "tensor", rel["B5"])
                ctx.op("tensor", lambda e: e.matmul(
                    B5[:, 0:128], wfm[:], sbf[:], start=True, stop=True), inc=tch)
                dlmm = tch.n
                rel["wfm"] = (tch, dlmm)
                ctx.wait("vector", tch, dlmm)
                wt(ctx, "vector", rel["dl"])
                ctx.op("vector", lambda e, Ux=Ux: e.tensor_tensor(
                    dl_bf[:], Ux, B5[:, 0:128], ALU.add), inc=vch)
                rel["B5"] = (vch, vch.n)
            else:
                wt(ctx, "vector", rel["dl"])
                ctx.op("vector", lambda e, Ux=Ux: e.tensor_copy(dl_bf[:], Ux), inc=vch)
            rel["x"][xfin] = (vch, vch.n)
            dl_rdy = vch.n
            ctx.wait("scalar", vch, dl_rdy)
            wt(ctx, "scalar", rel["dls"])
            ctx.op("scalar", lambda e, vh=vh, egT=egT: e.activation(
                dls_bf[:], dl_bf[:], AF.Copy, scale=egT[:, vh:vh + 1]), inc=ach)
            dls_rdy = ach.n
            wt(ctx, "tensor", rel["B6"])
            if tc > 0:
                ctx.op("tensor", lambda e, qc=qc, kh=kh: e.matmul(
                    B6[:, 0:128], qc[:, kh * 128:(kh + 1) * 128], sbf[:],
                    start=True, stop=True))
            ctx.wait("tensor", vch, dl_rdy)
            ctx.op("tensor", lambda e, qc=qc: e.matmul(
                B6[:, 128:256], aq_bf[:], dl_bf[:], start=True, stop=True), inc=tch)
            rel["aq"] = (tch, tch.n)
            omm = tch.n
            wt(ctx, "tensor", rel["ecc"])
            ctx.op("tensor", lambda e, vh=vh, gam=gam: e.matmul(
                B0[:, 384:385], onehotf[:, vh * 128:(vh + 1) * 128],
                gam[:, 127:128], start=True, stop=True), inc=tch)
            ecmm = tch.n
            ctx.wait("vector", tch, ecmm)
            ctx.op("vector", lambda e: e.tensor_copy(ecc[:], B0[:, 384:385]), inc=vch)
            ec_rdy = vch.n
            rel["B0"] = (vch, ec_rdy)
            ctx.wait("tensor", ach, dls_rdy)
            wt(ctx, "tensor", rel["B5"])
            ctx.op("tensor", lambda e, kt=kt, kh=kh: e.matmul(
                B5[:, 128:256], kt[:, kh * 128:(kh + 1) * 128], dls_bf[:],
                start=True, stop=True), inc=tch)
            rel["dls"] = (tch, tch.n)
            psmm = tch.n
            ctx.wait("vector", tch, psmm)
            ctx.wait("vector", vch, ec_rdy)
            ctx.op("vector", lambda e, vh=vh: e.scalar_tensor_tensor(
                S_all[:, vh * 128:(vh + 1) * 128], S_all[:, vh * 128:(vh + 1) * 128],
                ecc[:], B5[:, 128:256], ALU.mult, ALU.add), inc=vch)
            rel["B5"] = (vch, vch.n)
            rel["ecc"] = (vch, vch.n)
            rel["sbf"] = (tch, omm) if tc > 0 else None
            ctx.wait("vector", tch, omm)
            wt(ctx, "vector", rel["t1"])
            wt(ctx, "vector", rel["otm"])
            if tc > 0:
                ctx.op("vector", lambda e, vh=vh, gamT=gamT: e.tensor_scalar(
                    t1[:], B6[:, 0:128], gamT[:, vh:vh + 1], None, ALU.mult), inc=vch)
                ctx.wait("vector", vch, vch.n)
                ctx.op("vector", lambda e: e.tensor_tensor(
                    otm[:], t1[:], B6[:, 128:256], ALU.add), inc=vch)
            else:
                ctx.op("vector", lambda e: e.tensor_copy(otm[:], B6[:, 128:256]), inc=vch)
            rel["B6"] = (vch, vch.n)
            rel["t1"] = (vch, vch.n)
            otm_rdy = vch.n
            rel["dl"] = (tch, omm)
            if stage <= 8:
                continue
            ctx.wait("tensor", vch, otm_rdy)
            wt(ctx, "tensor", rel["B7"])
            ctx.op("tensor", lambda e: e.transpose(B7[:, 256:384], otm[:], identb[:]),
                   inc=tch)
            otmm = tch.n
            rel["otm"] = (tch, otmm)
            ctx.wait("scalar", lds[sl][4], nld)
            wt(ctx, "scalar", rel["zs"])
            ctx.op("scalar", lambda e, zc=zc, vh=vh: e.activation(
                zs[:], zc[:, vh * 128:(vh + 1) * 128], AF.Silu), inc=ach)
            zs_rdy = ach.n
            ctx.wait("vector", tch, otmm)
            ctx.wait("vector", ach, zs_rdy)
            wt(ctx, "vector", rel["t2"])
            ctx.op("vector", lambda e: e.tensor_tensor(t2[:], B7[:, 256:384], zs[:],
                                                       ALU.mult), inc=vch)
            rel["B7"] = (vch, vch.n)
            rel["zs"] = (vch, vch.n)
            t2_rdy = vch.n
            ctx.wait("scalar", vch, t2_rdy)
            wt(ctx, "scalar", rel["sq2"])
            ctx.op("scalar", lambda e: e.activation(sq2[:], t2[:], AF.Square), inc=ach)
            sq2_rdy = ach.n
            ctx.wait("tensor", ach, sq2_rdy)
            wt(ctx, "tensor", rel["rrow"])
            wt(ctx, "tensor", rel["B5"])
            ctx.op("tensor", lambda e: e.matmul(
                B5[0:1, 384:512], ctx.consts["onesb"][:, 0:1], sq2[:],
                start=True, stop=True), inc=tch)
            prmm = tch.n
            rel["sq2"] = (tch, prmm)
            ctx.wait("scalar", tch, prmm)
            ctx.op("scalar", lambda e: e.activation(
                rrow[:], B5[0:1, 384:512], AF.Sqrt,
                scale=cv(ctx, 1.0 / 128, 1), bias=cv(ctx, 1e-6, 1)), inc=ach)
            rel["B5"] = (ach, ach.n)
            ctx.wait("vector", ach, ach.n)
            ctx.op("vector", lambda e: e.reciprocal(rrow[:], rrow[:]), inc=vch)
            rr_rdy = vch.n
            ctx.wait("tensor", vch, rr_rdy)
            wt(ctx, "tensor", rel["B6"])
            ctx.op("tensor", lambda e: e.matmul(
                B6[:, 256:384], onesf[0:1, :], rrow[:], start=True, stop=True),
                inc=tch)
            rbmm = tch.n
            rel["rrow"] = (tch, rbmm)
            ctx.wait("vector", tch, rbmm)
            ctx.op("vector", lambda e, vh=vh, ost=ost: e.scalar_tensor_tensor(
                ost[:, vh * 128:(vh + 1) * 128], t2[:], normw[:],
                B6[:, 256:384], ALU.mult, ALU.mult), inc=vch)
            rel["B6"] = (vch, vch.n)
            rel["t2"] = (vch, vch.n)
        ctx.wait("sync", vch, vch.n)
        def st_o(e, sl=sl, tc=tc):
            dst = o_norm.rearrange("(v p) t -> p v t", p=128)[:, :, tc * 128:(tc + 1) * 128]
            return e.dma_start(out=dst, in_=ostage[sl][:].rearrange("p (v t) -> p v t", v=32))
        ctx.op("sync", st_o, inc=ost_sem, k=16)
        rel["ost"][sl] = (ost_sem, ost_sem.n)
        chunk_done = [(vch, vch.n), (ach, ach.n), (tch, tch.n)]
        rel_in[sl] = chunk_done + [(ost_sem, ost_sem.n)]
        rel["rows"] = chunk_done
        rel["rowsT"] = chunk_done
    return dict(ost=ost_sem, done=[(ost_sem, ost_sem.n)])



def _row_chain(ctx, vch, ach, rows, src1, src2, n):
    ctx.op("vector", lambda e: e.tensor_scalar(
        rows[:, 2 * n:3 * n], src1, 1.0 / 2048, None, ALU.mult), inc=vch)
    ctx.op("vector", lambda e: e.tensor_scalar(
        rows[:, 3 * n:4 * n], src2, 1.0 / 2048, None, ALU.mult), inc=vch)
    ctx.wait("vector", vch, vch.n)
    ctx.op("vector", lambda e: e.tensor_tensor(
        rows[:, 0:n], rows[:, 2 * n:3 * n], rows[:, 2 * n:3 * n], ALU.mult), inc=vch)
    ctx.wait("vector", vch, vch.n)
    ctx.op("vector", lambda e: e.tensor_tensor(
        rows[:, 0:n], rows[:, 3 * n:4 * n], rows[:, 0:n], ALU.subtract), inc=vch)
    ctx.wait("scalar", vch, vch.n)
    ctx.op("scalar", lambda e: e.activation(
        rows[:, 0:n], rows[:, 0:n], AF.Sqrt, bias=cv(ctx, 1e-5, 1)), inc=ach)
    ctx.wait("vector", ach, ach.n)
    ctx.op("vector", lambda e: e.reciprocal(rows[:, 0:n], rows[:, 0:n]), inc=vch)
    ctx.wait("vector", vch, vch.n)
    ctx.op("vector", lambda e: e.scalar_tensor_tensor(
        rows[:, n:2 * n], rows[:, 2 * n:3 * n], -1.0, rows[:, 0:n],
        ALU.mult, ALU.mult), inc=vch)
    return vch.n


def emit_p2b(ctx, o_norm, x_fm, w_o, x2_fm, xn2_fm):
    onesb = ctx.consts["onesb"]
    onesf = ctx.consts["onesf"]
    vch = ctx.sem("pbv")
    ach = ctx.sem("pba")
    tch = ctx.sem("pbt")

    ost = ctx.sbuf("pbo", [128, 32 * 512], BF16)
    attn = ctx.sbuf("pbattn", [128, 16 * 512], BF16)
    x2s = ctx.sbuf("pbx2", [128, 16 * 512], BF16)
    xfs = ctx.sbuf("pbxf", [128, 16 * 512], BF16)
    sqt = ctx.sbuf("pbsq", [128, 512], BF16)
    tmp = ctx.sbuf("pbtmp", [128, 512], F32)
    rows = ctx.sbuf("pbrows", [1, 4 * 512], F32)
    wob = [ctx.sbuf(f"pbw{i}", [128, 32 * 128], BF16) for i in range(2)]
    xn2s = ost[:, 0:16 * 512]
    olds = ctx.sem("pbol")
    xlds = ctx.sem("pbxl")
    wlds = [ctx.sem(f"pbwl{i}") for i in range(2)]
    sts = ctx.sem("pbst")
    pA = ctx.psum("pbA", [128, 512], F32)
    pB = ctx.psum("pbB", [128, 512], F32)
    pS1 = ctx.psum("pbS1", [1, 512], F32)
    pS2 = ctx.psum("pbS2", [1, 512], F32)
    pR1 = ctx.psum("pbR1", [128, 512], F32)
    pR2 = ctx.psum("pbR2", [128, 512], F32)
    rel = {k: None for k in ["A", "B", "S", "R", "sq", "strip", "rows", "tmp", "ost"]}
    rel_w = [None, None]
    nw = [0]
    wot = w_o.rearrange("(m k p) f -> m p k f", m=16, k=32)

    for s in range(4):
        wt(ctx, "sync", rel["strip"])
        ctx.op("sync", lambda e, s=s: e.dma_start(
            out=ost[:].rearrange("p (k t) -> p k t", k=32),
            in_=o_norm.rearrange("(k p) t -> p k t", p=128)[:, :, s * 512:(s + 1) * 512]),
            inc=olds, k=16)
        ctx.op("sync", lambda e, s=s: e.dma_start(
            out=xfs[:].rearrange("p (k t) -> p k t", k=16),
            in_=x_fm.rearrange("(k p) t -> p k t", p=128)[:, :, s * 512:(s + 1) * 512]),
            inc=xlds, k=16)
        for m in range(16):
            sl = m % 2
            wb = wob[sl]
            wt(ctx, "gpsimd", rel_w[sl])
            ctx.op("gpsimd", lambda e, wb=wb, m=m: e.dma_start(
                out=wb[:].rearrange("p (k f) -> p k f", k=32), in_=wot[m]),
                inc=wlds[sl], k=16)
            nw[0] += 1
            ctx.wait("tensor", wlds[sl], 16 * ((nw[0] - 1) // 2 + 1))
            ctx.wait("tensor", olds, 16 * (s + 1))
            pm = pA if sl == 0 else pB
            wt(ctx, "tensor", rel["A" if sl == 0 else "B"])
            for k in range(32):
                ctx.op("tensor", lambda e, pm=pm, wb=wb, k=k: e.matmul(
                    pm[:], wb[:, k * 128:(k + 1) * 128],
                    ost[:, k * 512:(k + 1) * 512],
                    start=(k == 0), stop=(k == 31)), inc=(tch if k == 31 else None))
            rel_w[sl] = (tch, tch.n)
            mmv = tch.n
            ctx.wait("scalar", tch, mmv)
            wt(ctx, "scalar", rel["sq"])
            ctx.op("scalar", lambda e, pm=pm, m=m: e.copy(
                attn[:, m * 512:(m + 1) * 512], pm[:]), inc=ach)
            ctx.op("scalar", lambda e, pm=pm: e.activation(
                sqt[:], pm[:], AF.Square), inc=ach)
            sqv = ach.n
            rel["A" if sl == 0 else "B"] = (ach, sqv)
            ctx.wait("tensor", ach, sqv)
            if m == 0:
                wt(ctx, "tensor", rel["S"])
            ctx.op("tensor", lambda e, m=m: e.matmul(
                pS1[:], onesb[:, 0:1], attn[:, m * 512:(m + 1) * 512],
                start=(m == 0), stop=(m == 15)))
            ctx.op("tensor", lambda e, m=m: e.matmul(
                pS2[:], onesb[:, 0:1], sqt[:],
                start=(m == 0), stop=(m == 15)), inc=(tch if m == 15 else None))
            rel["sq"] = (tch, tch.n)
        stmm = tch.n
        rel["ost"] = (tch, stmm)
        ctx.wait("vector", tch, stmm)
        wt(ctx, "vector", rel["rows"])
        rc1 = _row_chain(ctx, vch, ach, rows, pS1[:], pS2[:], 512)
        rel["S"] = (vch, rc1)
        wt(ctx, "tensor", rel["R"])
        ctx.wait("tensor", vch, rc1)
        ctx.op("tensor", lambda e: e.matmul(
            pR1[:], onesf[0:1, :], rows[:, 0:512], start=True, stop=True))
        ctx.op("tensor", lambda e: e.matmul(
            pR2[:], onesf[0:1, :], rows[:, 512:1024], start=True, stop=True), inc=tch)
        rmm = tch.n
        ctx.wait("vector", tch, rmm)
        ctx.wait("vector", xlds, 16 * (s + 1))
        for m in range(16):
            off = m * 512
            wt(ctx, "vector", rel["tmp"])
            ctx.op("vector", lambda e, off=off: e.tensor_tensor(
                tmp[:], attn[:, off:off + 512], pR1[:], ALU.mult), inc=vch)
            ctx.wait("vector", vch, vch.n)
            ctx.op("vector", lambda e: e.tensor_tensor(
                tmp[:], tmp[:], pR2[:], ALU.add), inc=vch)
            ctx.wait("vector", vch, vch.n)
            ctx.op("vector", lambda e, off=off: e.tensor_tensor(
                x2s[:, off:off + 512], tmp[:], xfs[:, off:off + 512], ALU.add), inc=vch)
            rel["tmp"] = (vch, vch.n)
            x2v = vch.n
            ctx.wait("scalar", vch, x2v)
            wt(ctx, "scalar", rel["sq"])
            ctx.op("scalar", lambda e, off=off: e.activation(
                sqt[:], x2s[:, off:off + 512], AF.Square), inc=ach)
            sqv = ach.n
            ctx.wait("tensor", ach, sqv)
            ctx.wait("tensor", vch, x2v)
            if m == 0:
                wt(ctx, "tensor", rel["S"])
            ctx.op("tensor", lambda e, off=off, m=m: e.matmul(
                pS1[:], onesb[:, 0:1], x2s[:, off:off + 512],
                start=(m == 0), stop=(m == 15)))
            ctx.op("tensor", lambda e, m=m: e.matmul(
                pS2[:], onesb[:, 0:1], sqt[:],
                start=(m == 0), stop=(m == 15)), inc=(tch if m == 15 else None))
            rel["sq"] = (tch, tch.n)
        rel["R"] = (vch, vch.n)
        st2 = tch.n
        ctx.wait("vector", tch, st2)
        rc2 = _row_chain(ctx, vch, ach, rows, pS1[:], pS2[:], 512)
        rel["S"] = (vch, rc2)
        rel["rows"] = (vch, rc2)
        wt(ctx, "tensor", rel["R"])
        ctx.wait("tensor", vch, rc2)
        ctx.op("tensor", lambda e: e.matmul(
            pR1[:], onesf[0:1, :], rows[:, 0:512], start=True, stop=True))
        ctx.op("tensor", lambda e: e.matmul(
            pR2[:], onesf[0:1, :], rows[:, 512:1024], start=True, stop=True), inc=tch)
        rmm2 = tch.n
        ctx.wait("vector", tch, rmm2)
        wt(ctx, "vector", rel["ost"])
        for m in range(16):
            off = m * 512
            wt(ctx, "vector", rel["tmp"])
            ctx.op("vector", lambda e, off=off: e.tensor_tensor(
                tmp[:], x2s[:, off:off + 512], pR1[:], ALU.mult), inc=vch)
            ctx.wait("vector", vch, vch.n)
            ctx.op("vector", lambda e, off=off: e.tensor_tensor(
                xn2s[:, off:off + 512], tmp[:], pR2[:], ALU.add), inc=vch)
            rel["tmp"] = (vch, vch.n)
        rel["R"] = (vch, vch.n)
        ctx.wait("sync", vch, vch.n)
        ctx.op("sync", lambda e, s=s: e.dma_start(
            out=x2_fm.rearrange("(k p) t -> p k t", p=128)[:, :, s * 512:(s + 1) * 512],
            in_=x2s[:].rearrange("p (k t) -> p k t", k=16)), inc=sts, k=16)
        ctx.op("sync", lambda e, s=s: e.dma_start(
            out=xn2_fm.rearrange("(k p) t -> p k t", p=128)[:, :, s * 512:(s + 1) * 512],
            in_=xn2s.rearrange("p (k t) -> p k t", k=16)), inc=sts, k=16)
        rel["strip"] = [(vch, vch.n), (ach, ach.n), (tch, tch.n), (sts, sts.n)]
    return dict(sts=sts, done=[(sts, sts.n)])



def emit_p3(ctx, xn2_fm, x2_fm, w1q, w2q, out, nstrips=4):
    onesb = ctx.consts["onesb"]
    onesf = ctx.consts["onesf"]
    identf = ctx.consts["identf"]
    b1 = ctx.consts["b1"]
    b2 = ctx.consts["b2"]
    vch = ctx.sem("p3v")
    ach = ctx.sem("p3a")
    tch = ctx.sem("p3t")

    xns = ctx.sbuf("p3xn", [128, 16 * 512], BF16)
    hst = ctx.sbuf("p3h", [128, 64 * 512], BF16)
    fstg = ctx.sbuf("p3f", [128, 16 * 512], BF16)
    outT = ctx.sbuf("p3ot", [128, 4 * 2048], F32)
    sqt = ctx.sbuf("p3sq", [128, 512], BF16)
    tmp = ctx.sbuf("p3tmp", [128, 512], F32)
    outm = ctx.sbuf("p3om", [128, 512], F32)
    rows = ctx.sbuf("p3rows", [1, 4 * 512], F32)
    w1b = [ctx.sbuf(f"p3w1{i}", [128, 16 * 128], BF16) for i in range(2)]
    w2b = [ctx.sbuf(f"p3w2{i}", [128, 64 * 128], BF16) for i in range(2)]
    x2st = xns
    xl = ctx.sem("p3xl")
    x2l = ctx.sem("p3x2l")
    w1l = [ctx.sem(f"p3w1l{i}") for i in range(2)]
    w2l = [ctx.sem(f"p3w2l{i}") for i in range(2)]
    sts = ctx.sem("p3st")
    pA = ctx.psum("p3A", [128, 512], F32)
    pB = ctx.psum("p3B", [128, 512], F32)
    pS1 = ctx.psum("p3S1", [1, 512], F32)
    pS2 = ctx.psum("p3S2", [1, 512], F32)
    pR1 = ctx.psum("p3R1", [128, 512], F32)
    pR2 = ctx.psum("p3R2", [128, 512], F32)
    pT = ctx.psum("p3T", [128, 512], F32)
    rel = {k: None for k in ["A", "B", "S", "R", "T", "sq", "strip", "rows",
                             "tmp", "h", "f", "outm", "outT", "xns"]}
    rel_w1 = [None, None]
    rel_w2 = [None, None]
    nw1 = [0]
    nw2 = [0]
    w1t = w1q.rearrange("(m p) f -> m p f", m=64)
    w2t = w2q.rearrange("(m p a) b -> m p (a b)", m=16, p=128, a=4)
    outp = out.rearrange("(b p) e -> b p e", p=128)

    for s in range(nstrips):
        wt(ctx, "sync", rel["strip"])
        wt(ctx, "sync", rel["xns"])
        ctx.op("sync", lambda e, s=s: e.dma_start(
            out=xns[:].rearrange("p (k t) -> p k t", k=16),
            in_=xn2_fm.rearrange("(k p) t -> p k t", p=128)[:, :, s * 512:(s + 1) * 512]),
            inc=xl, k=16)
        for m in range(64):
            sl = m % 2
            wb = w1b[sl]
            wt(ctx, "gpsimd", rel_w1[sl])
            ctx.op("gpsimd", lambda e, wb=wb, m=m: e.dma_start(
                out=wb[:], in_=w1t[m]), inc=w1l[sl], k=16)
            nw1[0] += 1
            ctx.wait("tensor", w1l[sl], 16 * ((nw1[0] - 1) // 2 + 1))
            ctx.wait("tensor", xl, 16 * (s + 1))
            pm = pA if sl == 0 else pB
            wt(ctx, "tensor", rel["A" if sl == 0 else "B"])
            for k in range(16):
                ctx.op("tensor", lambda e, pm=pm, wb=wb, k=k: e.matmul(
                    pm[:], wb[:, k * 128:(k + 1) * 128],
                    xns[:, k * 512:(k + 1) * 512],
                    start=(k == 0), stop=(k == 15)), inc=(tch if k == 15 else None))
            rel_w1[sl] = (tch, tch.n)
            mmv = tch.n
            ctx.wait("scalar", tch, mmv)
            wt(ctx, "scalar", rel["h"])
            ctx.op("scalar", lambda e, pm=pm, m=m: e.activation(
                hst[:, m * 512:(m + 1) * 512], pm[:], AF.Gelu,
                bias=b1[:, m:m + 1]), inc=ach)
            rel["A" if sl == 0 else "B"] = (ach, ach.n)
        h_rdy = ach.n
        ffn1_mms = tch.n
        rel["xns"] = (tch, ffn1_mms)
        wt(ctx, "sync", rel["xns"])
        ctx.op("sync", lambda e, s=s: e.dma_start(
            out=x2st[:].rearrange("p (k t) -> p k t", k=16),
            in_=x2_fm.rearrange("(k p) t -> p k t", p=128)[:, :, s * 512:(s + 1) * 512]),
            inc=x2l, k=16)
        for m in range(16):
            sl = m % 2
            wb = w2b[sl]
            wt(ctx, "gpsimd", rel_w2[sl])
            ctx.op("gpsimd", lambda e, wb=wb, m=m: e.dma_start(
                out=wb[:], in_=w2t[m]), inc=w2l[sl], k=16)
            nw2[0] += 1
            ctx.wait("tensor", w2l[sl], 16 * ((nw2[0] - 1) // 2 + 1))
            ctx.wait("tensor", ach, h_rdy)
            pm = pA if sl == 0 else pB
            wt(ctx, "tensor", rel["A" if sl == 0 else "B"])
            for k in range(64):
                ctx.op("tensor", lambda e, pm=pm, wb=wb, k=k: e.matmul(
                    pm[:], wb[:, k * 128:(k + 1) * 128],
                    hst[:, k * 512:(k + 1) * 512],
                    start=(k == 0), stop=(k == 63)), inc=(tch if k == 63 else None))
            rel_w2[sl] = (tch, tch.n)
            mmv = tch.n
            ctx.wait("vector", tch, mmv)
            wt(ctx, "vector", rel["f"])
            ctx.op("vector", lambda e, pm=pm, m=m: e.tensor_scalar(
                fstg[:, m * 512:(m + 1) * 512], pm[:], b2[:, m:m + 1], None,
                ALU.add), inc=vch)
            fv = vch.n
            ctx.wait("scalar", vch, fv)
            wt(ctx, "scalar", rel["sq"])
            ctx.op("scalar", lambda e, m=m: e.activation(
                sqt[:], fstg[:, m * 512:(m + 1) * 512], AF.Square), inc=ach)
            sqv = ach.n
            rel["A" if sl == 0 else "B"] = [(vch, fv), (ach, sqv)]
            ctx.wait("tensor", ach, sqv)
            ctx.wait("tensor", vch, fv)
            if m == 0:
                wt(ctx, "tensor", rel["S"])
            ctx.op("tensor", lambda e, m=m: e.matmul(
                pS1[:], onesb[:, 0:1], fstg[:, m * 512:(m + 1) * 512],
                start=(m == 0), stop=(m == 15)))
            ctx.op("tensor", lambda e, m=m: e.matmul(
                pS2[:], onesb[:, 0:1], sqt[:],
                start=(m == 0), stop=(m == 15)), inc=(tch if m == 15 else None))
            rel["sq"] = (tch, tch.n)
        stmm = tch.n
        ctx.wait("vector", tch, stmm)
        wt(ctx, "vector", rel["rows"])
        rc = _row_chain(ctx, vch, ach, rows, pS1[:], pS2[:], 512)
        rel["S"] = (vch, rc)
        rel["rows"] = (vch, rc)
        wt(ctx, "tensor", rel["R"])
        ctx.wait("tensor", vch, rc)
        ctx.op("tensor", lambda e: e.matmul(
            pR1[:], onesf[0:1, :], rows[:, 0:512], start=True, stop=True))
        ctx.op("tensor", lambda e: e.matmul(
            pR2[:], onesf[0:1, :], rows[:, 512:1024], start=True, stop=True), inc=tch)
        rmm = tch.n
        ctx.wait("vector", tch, rmm)
        ctx.wait("vector", x2l, 16 * (s + 1))
        for m in range(16):
            off = m * 512
            wt(ctx, "vector", rel["tmp"])
            ctx.op("vector", lambda e, off=off: e.tensor_tensor(
                tmp[:], fstg[:, off:off + 512], pR1[:], ALU.mult), inc=vch)
            ctx.wait("vector", vch, vch.n)
            ctx.op("vector", lambda e: e.tensor_tensor(
                tmp[:], tmp[:], pR2[:], ALU.add), inc=vch)
            ctx.wait("vector", vch, vch.n)
            wt(ctx, "vector", rel["outm"])
            ctx.op("vector", lambda e, off=off: e.tensor_tensor(
                outm[:], tmp[:], x2st[:, off:off + 512], ALU.add), inc=vch)
            rel["tmp"] = (vch, vch.n)
            outm_rdy = vch.n
            ctx.wait("tensor", vch, outm_rdy)
            wt(ctx, "tensor", rel["T"])
            for tt4 in range(4):
                ctx.op("tensor", lambda e, tt4=tt4: e.transpose(
                    pT[:, tt4 * 128:(tt4 + 1) * 128],
                    outm[:, tt4 * 128:(tt4 + 1) * 128], identf[:]),
                    inc=(tch if tt4 == 3 else None))
            tmm = tch.n
            rel["outm"] = (tch, tmm)
            ctx.wait("vector", tch, tmm)
            wt(ctx, "vector", rel["outT"])
            def cpo(e, m=m):
                dst = outT[:].rearrange("p (b e) -> p b e", b=4)[:, :, m * 128:(m + 1) * 128]
                return e.tensor_copy(dst, pT[:].rearrange("p (b e) -> p b e", b=4))
            ctx.op("vector", cpo, inc=vch)
            rel["T"] = (vch, vch.n)
        rel["R"] = (vch, vch.n)
        rel["f"] = (vch, vch.n)
        ctx.wait("sync", vch, vch.n)
        for tt4 in range(4):
            ctx.op("sync", lambda e, s=s, tt4=tt4: e.dma_start(
                out=outp[s * 4 + tt4],
                in_=outT[:, tt4 * 2048:(tt4 + 1) * 2048]), inc=sts, k=16)
        rel["outT"] = (sts, sts.n)
        rel["h"] = (tch, tch.n)
        rel["strip"] = [(vch, vch.n), (ach, ach.n), (tch, tch.n), (sts, sts.n)]
        if s == 0 and getattr(ctx, "dbg_h", None) is not None:
            ctx.wait("sync", ach, ach.n)
            ctx.wait("sync", vch, vch.n)
            ctx.dma("sync", ctx.dbg_h, hst[:], inc=sts)
            ctx.dma("sync", ctx.dbg_f, fstg[:], inc=sts)
            rel["strip"].append((sts, sts.n))
            rel["h"] = [rel["h"], (sts, sts.n)]
            rel["f"] = [rel["f"], (sts, sts.n)]
    return dict(sts=sts, done=[(sts, sts.n)])



def emit_p2v2(ctx, q_fm, k_fm, k_tm, v_tm, z_fm, beta_i, cumg_i, o_norm, nch=16):
    identb = ctx.consts["ident"]
    identf = ctx.consts["identf"]
    onesf = ctx.consts["onesf"]
    onesb = ctx.consts["onesb"]
    onehotf = ctx.consts["onehotf"]
    maskS = ctx.consts["maskS"]
    maskIT = ctx.consts["maskIT"]
    normw = ctx.consts["normw"]

    vch = ctx.sem()
    ach = ctx.sem()
    tch = ctx.sem()

    S_all = ctx.sbuf("p2S", [128, 32 * 128], F32)
    beta_sb = ctx.sbuf("p2beta", [32, 2048], F32)
    cumg_sb = ctx.sbuf("p2cumg", [32, 2048], F32)
    sinit = ctx.sem()
    ctx.op("gpsimd", lambda e: e.memset(S_all[:], 0.0), inc=sinit)
    ctx.wait("vector", sinit, 1)
    ldrows = ctx.sem()
    ctx.dma("sync", beta_sb[:], beta_i, inc=ldrows)
    ctx.dma("sync", cumg_sb[:], cumg_i, inc=ldrows)

    qc_b = [ctx.sbuf(f"q2q{i}", [128, 2048], BF16) for i in range(2)]
    kc_b = [ctx.sbuf(f"q2k{i}", [128, 2048], BF16) for i in range(2)]
    kt_b = [ctx.sbuf(f"q2kt{i}", [128, 2048], BF16) for i in range(2)]
    vt_b = [ctx.sbuf(f"q2vt{i}", [128, 4096], BF16) for i in range(2)]
    zc_b = [ctx.sbuf(f"q2z{i}", [128, 4096], BF16) for i in range(2)]
    lds = [[ctx.sem() for j in range(5)] for i in range(2)]
    rel_in = [None, None]

    rows = ctx.sbuf("q2rows", [32, 5 * 128], F32)
    rowsT = ctx.sbuf("q2rowsT", [128, 5 * 32], F32)
    G = 8
    tD = [ctx.sbuf(f"q2tD{j}", [128, 256], F32) for j in range(4)]
    eD = [ctx.sbuf(f"q2eD{j}", [128, 256], F32) for j in range(4)]
    n_cur = [ctx.sbuf(f"q2nc{j}", [128, 128], BF16) for j in range(G)]
    n_nxt = [ctx.sbuf(f"q2nn{j}", [128, 128], BF16) for j in range(G)]
    nt_pow = [ctx.sbuf(f"q2np{j}", [128, 7 * 128], BF16) for j in range(G)]
    aq_bf = [ctx.sbuf(f"q2aq{j}", [128, 128], BF16) for j in range(16)]
    x_bf = [[ctx.sbuf(f"q2x{i}_{j}", [128, 256], BF16) for j in range(G)] for i in range(2)]
    sbf = [ctx.sbuf(f"q2sb{j}", [128, 128], BF16) for j in range(G)]
    wfm = [ctx.sbuf(f"q2wf{j}", [128, 128], BF16) for j in range(G)]
    dl_bf = [ctx.sbuf(f"q2dl{j}", [128, 128], BF16) for j in range(G)]
    dls_bf = [ctx.sbuf(f"q2ds{j}", [128, 128], BF16) for j in range(G)]
    t1 = [ctx.sbuf(f"q2t1{j}", [128, 128], F32) for j in range(2)]
    otm = [ctx.sbuf(f"q2ot{j}", [128, 128], BF16) for j in range(G)]
    zs = [ctx.sbuf(f"q2zs{j}", [128, 128], F32) for j in range(2)]
    t2 = [ctx.sbuf(f"q2t2{j}", [128, 128], F32) for j in range(G)]
    sq2 = [ctx.sbuf(f"q2sq{j}", [128, 128], BF16) for j in range(2)]
    rrow = [ctx.sbuf(f"q2rr{j}", [1, 128], F32) for j in range(2)]
    ecc = ctx.sbuf("q2ec", [128, 8], F32)
    ostage = [ctx.sbuf(f"q2os{i}", [128, 4096], BF16) for i in range(2)]
    ost_init = ctx.sem()
    for i in range(2):
        ctx.op("gpsimd", lambda e, i=i: e.memset(ostage[i][:], 0.0), inc=ost_init)
    ctx.wait("vector", ost_init, 2)

    B0 = ctx.psum("q2B0", [128, 512], F32)
    B1 = ctx.psum("q2B1", [128, 512], F32)
    B2 = ctx.psum("q2B2", [128, 512], F32)
    B3 = ctx.psum("q2B3", [128, 512], F32)
    B4 = ctx.psum("q2B4", [128, 512], F32)
    B5 = ctx.psum("q2B5", [128, 512], F32)
    B6 = ctx.psum("q2B6", [128, 512], F32)
    B7 = ctx.psum("q2B7", [128, 1024], BF16)

    rel = {k: None for k in ["B0", "B1", "B2", "B3", "B4", "B5", "B6",
                             "rows", "rowsT", "ecc"]}
    relB7 = [None] * 8
    rel_tD = [None] * 4
    rel_eD = [None] * 4
    rel_nc = [None] * G
    rel_nn = [None] * G
    rel_np = [None] * G
    rel_aq = [None] * 16
    rel_x = [[None] * G, [None] * G]
    rel_sbf = [None] * G
    rel_wfm = [None] * G
    rel_dl = [None] * G
    rel_dls = [None] * G
    rel_t1 = [None] * 2
    rel_otm = [None] * G
    rel_zs = [None] * 2
    rel_t2 = [None] * G
    rel_sq2 = [None] * 2
    rel_rr = [None] * 2
    rel_ost = [None, None]
    ost_sem = ctx.sem()

    for tc in range(nch):
        sl = tc % 2
        wt(ctx, "sync", rel_in[sl])
        ctx.op("sync", lambda e, sl=sl, tc=tc: e.dma_start(
            out=qc_b[sl][:].rearrange("p (h t) -> p h t", h=16),
            in_=q_fm.rearrange("(h p) t -> p h t", p=128)[:, :, tc * 128:(tc + 1) * 128]),
            inc=lds[sl][0], k=16)
        ctx.op("sync", lambda e, sl=sl, tc=tc: e.dma_start(
            out=kc_b[sl][:].rearrange("p (h t) -> p h t", h=16),
            in_=k_fm.rearrange("(h p) t -> p h t", p=128)[:, :, tc * 128:(tc + 1) * 128]),
            inc=lds[sl][1], k=16)
        ctx.op("sync", lambda e, sl=sl, tc=tc: e.dma_start(
            out=kt_b[sl][:].rearrange("p (h d) -> p h d", h=16),
            in_=k_tm.rearrange("(tc h t) d -> tc t h d", tc=16, h=16)[tc]),
            inc=lds[sl][2], k=16)
        ctx.op("sync", lambda e, sl=sl, tc=tc: e.dma_start(
            out=vt_b[sl][:].rearrange("p (v d) -> p v d", v=32),
            in_=v_tm.rearrange("(tc v t) d -> tc t v d", tc=16, v=32)[tc]),
            inc=lds[sl][3], k=16)
        ctx.op("sync", lambda e, sl=sl, tc=tc: e.dma_start(
            out=zc_b[sl][:].rearrange("p (v t) -> p v t", v=32),
            in_=z_fm.rearrange("(v p) t -> p v t", p=128)[:, :, tc * 128:(tc + 1) * 128]),
            inc=lds[sl][4], k=16)
        nld = 16 * (tc // 2 + 1)
        qc, kc, kt, vt, zc = (b[sl] for b in (qc_b, kc_b, kt_b, vt_b, zc_b))

        cgl = rows[:, 0:128]
        gam = rows[:, 128:256]
        eg = rows[:, 256:384]
        bgn = rows[:, 384:512]
        bet = rows[:, 512:640]
        if tc == 0:
            ctx.wait("vector", ldrows, 32)
        wt(ctx, "vector", rel["rows"])
        if tc == 0:
            ctx.op("vector", lambda e: e.tensor_copy(cgl, cumg_sb[:, 0:128]), inc=vch)
        else:
            ctx.op("vector", lambda e, tc=tc: e.tensor_scalar(
                cgl, cumg_sb[:, tc * 128:(tc + 1) * 128],
                cumg_sb[:, tc * 128 - 1:tc * 128], None, ALU.subtract), inc=vch)
        ctx.op("vector", lambda e, tc=tc: e.tensor_copy(
            bet, beta_sb[:, tc * 128:(tc + 1) * 128]), inc=vch)
        cgl_rdy = vch.n
        ctx.wait("scalar", vch, cgl_rdy)
        ctx.op("scalar", lambda e: e.activation(gam, cgl, AF.Exp), inc=ach)
        ctx.op("scalar", lambda e: e.activation(
            eg, cgl, AF.Exp, scale=cv(ctx, -1.0, 32), bias=cgl[:, 127:128]), inc=ach)
        ctx.wait("vector", ach, ach.n)
        ctx.op("vector", lambda e: e.scalar_tensor_tensor(
            bgn, gam, -1.0, bet, ALU.mult, ALU.mult), inc=vch)
        rows_rdy_v = vch.n
        ctx.wait("tensor", vch, rows_rdy_v)
        wt(ctx, "tensor", rel["B0"])
        wt(ctx, "tensor", rel["rowsT"])
        for j in range(5):
            ctx.op("tensor", lambda e, j=j: e.transpose(
                B0[:, j * 32:(j + 1) * 32], rows[:, j * 128:(j + 1) * 128],
                identf[0:32, 0:32]), inc=(tch if j == 4 else None))
        rowsT_mm = tch.n
        ctx.wait("vector", tch, rowsT_mm)
        ctx.op("vector", lambda e: e.tensor_copy(rowsT[:], B0[:, 0:160]), inc=vch)
        rel["B0"] = (vch, vch.n)
        rowsT_rdy = vch.n
        cglT = rowsT[:, 0:32]
        gamT = rowsT[:, 32:64]
        egT = rowsT[:, 64:96]
        bgnT = rowsT[:, 96:128]
        betT = rowsT[:, 128:160]
        ost = ostage[sl]
        wt(ctx, "vector", rel_ost[sl])
        dpost = []

        for g in range(4):
            x_rdy = [0] * G
            n1_rdy = [0] * G
            nt1_rdy = [0] * G
            aq_rdy = [0] * G
            for j in range(G):
                vh = g * 8 + j
                kh = vh // 2
                pair = (vh // 2) % 2
                pb = B1 if pair == 0 else B2
                pbk = "B1" if pair == 0 else "B2"
                if vh % 2 == 0:
                    ctx.wait("tensor", lds[sl][0], nld)
                    ctx.wait("tensor", lds[sl][1], nld)
                    wt(ctx, "tensor", rel[pbk])
                    ctx.op("tensor", lambda e, pb=pb, kc=kc, kh=kh: e.matmul(
                        pb[:, 0:128], kc[:, kh * 128:(kh + 1) * 128],
                        kc[:, kh * 128:(kh + 1) * 128], start=True, stop=True))
                    ctx.op("tensor", lambda e, pb=pb, kc=kc, qc=qc, kh=kh: e.matmul(
                        pb[:, 128:256], kc[:, kh * 128:(kh + 1) * 128],
                        qc[:, kh * 128:(kh + 1) * 128], start=True, stop=True), inc=tch)
                bcol = 256 + (vh % 2) * 128
                ctx.wait("tensor", vch, rowsT_rdy)
                ctx.op("tensor", lambda e, pb=pb, vh=vh, bcol=bcol, cgl=cgl: e.matmul(
                    pb[:, bcol:bcol + 128], onehotf[:, vh * 128:(vh + 1) * 128],
                    cgl, start=True, stop=True), inc=tch)
                mm_pair = tch.n
                j4 = j % 4
                ctx.wait("vector", tch, mm_pair)
                wt(ctx, "vector", rel_tD[j4])
                ctx.op("vector", lambda e, pb=pb, bcol=bcol, vh=vh, cglT=cglT, j4=j4:
                       e.scalar_tensor_tensor(
                           tD[j4][:, 0:128], pb[:, bcol:bcol + 128], cglT[:, vh:vh + 1],
                           maskS[:], ALU.subtract, ALU.add), inc=vch)
                ctx.op("vector", lambda e, pb=pb, bcol=bcol, vh=vh, cglT=cglT, j4=j4:
                       e.scalar_tensor_tensor(
                           tD[j4][:, 128:256], pb[:, bcol:bcol + 128], cglT[:, vh:vh + 1],
                           maskIT[:], ALU.subtract, ALU.subtract), inc=vch)
                tD_rdy = vch.n
                ctx.wait("scalar", vch, tD_rdy)
                wt(ctx, "scalar", rel_eD[j4])
                ctx.op("scalar", lambda e, j4=j4: e.activation(
                    eD[j4][:, 0:128], tD[j4][:, 0:128], AF.Exp, scale=cv(ctx, -1.0)), inc=ach)
                ctx.op("scalar", lambda e, j4=j4: e.activation(
                    eD[j4][:, 128:256], tD[j4][:, 128:256], AF.Exp), inc=ach)
                rel_tD[j4] = (ach, ach.n)
                eD_rdy = ach.n
                ctx.wait("vector", ach, eD_rdy)
                wt(ctx, "vector", rel_nc[j])
                ctx.op("vector", lambda e, pb=pb, vh=vh, betT=betT, j=j, j4=j4:
                       e.scalar_tensor_tensor(
                           n_cur[j][:], eD[j4][:, 0:128], betT[:, vh:vh + 1],
                           pb[:, 0:128], ALU.mult, ALU.mult), inc=vch)
                n1_rdy[j] = vch.n
                j16 = vh % 16
                wt(ctx, "vector", rel_aq[j16])
                ctx.op("vector", lambda e, pb=pb, j16=j16, j4=j4: e.tensor_tensor(
                    aq_bf[j16][:], eD[j4][:, 128:256], pb[:, 128:256], ALU.mult), inc=vch)
                rel_eD[j4] = (vch, vch.n)
                rel[pbk] = (vch, vch.n)
                aq_rdy[j] = vch.n
                b7s = j % 2
                ctx.wait("tensor", vch, n1_rdy[j])
                wt(ctx, "tensor", relB7[b7s])
                wt(ctx, "tensor", rel_np[j])
                ctx.op("tensor", lambda e, j=j, b7s=b7s: e.transpose(
                    B7[:, b7s * 128:(b7s + 1) * 128], n_cur[j][:], identb[:]), inc=tch)
                nt1_mm = tch.n
                ctx.wait("scalar", tch, nt1_mm)
                wt(ctx, "scalar", rel_np[j])
                ctx.op("scalar", lambda e, j=j, b7s=b7s: e.copy(
                    nt_pow[j][:, 0:128], B7[:, b7s * 128:(b7s + 1) * 128]), inc=ach)
                relB7[b7s] = (ach, ach.n)
                nt1_rdy[j] = ach.n
                wt(ctx, "scalar", rel_x[0][j])
                ctx.wait("scalar", lds[sl][3], nld)
                ctx.op("scalar", lambda e, vt=vt, vh=vh, betT=betT, j=j: e.activation(
                    x_bf[0][j][:, 0:128], vt[:, vh * 128:(vh + 1) * 128], AF.Copy,
                    scale=betT[:, vh:vh + 1]), inc=ach)
                ctx.wait("scalar", lds[sl][2], nld)
                ctx.op("scalar", lambda e, kt=kt, kh=kh, vh=vh, bgnT=bgnT, j=j: e.activation(
                    x_bf[0][j][:, 128:256], kt[:, kh * 128:(kh + 1) * 128], AF.Copy,
                    scale=bgnT[:, vh:vh + 1]), inc=ach)
                x_rdy[j] = ach.n

            n_rdy = list(n1_rdy)
            nt_rdy = [("a", v) for v in nt1_rdy]
            cur_is_c = [True] * G
            for li in range(6):
                for j in range(G):
                    pq = B3 if j % 2 == 0 else B4
                    pqk = "B3" if j % 2 == 0 else "B4"
                    ncur = n_cur[j] if cur_is_c[j] else n_nxt[j]
                    nnxt = n_nxt[j] if cur_is_c[j] else n_cur[j]
                    ctx.wait("tensor", vch, n_rdy[j]) if not isinstance(n_rdy[j], tuple) else wt(ctx, "tensor", (vch if n_rdy[j][0] == "v" else ach, n_rdy[j][1]))
                    wt(ctx, "tensor", (vch if nt_rdy[j][0] == "v" else ach, nt_rdy[j][1]))
                    wt(ctx, "tensor", rel[pqk])
                    ntsl = nt_pow[j][:, li * 128:(li + 1) * 128]
                    ntdst = nt_pow[j][:, (li + 1) * 128:(li + 2) * 128]
                    if li < 5:
                        ctx.op("tensor", lambda e, pq=pq, ntsl=ntsl, ncur=ncur: e.matmul(
                            pq[:, 0:128], ntsl, ncur[:], start=True, stop=True))
                    ctx.op("tensor", lambda e, pq=pq, ntsl=ntsl, ncur=ncur: e.matmul(
                        pq[:, 128:256], ncur[:], ntsl, start=True, stop=True), inc=tch)
                    sqmm = tch.n
                    (rel_nc if cur_is_c[j] else rel_nn)[j] = (tch, sqmm)
                    if li % 2 == 0:
                        ctx.wait("vector", tch, sqmm)
                        if li < 5:
                            wt(ctx, "vector", (rel_nn if cur_is_c[j] else rel_nc)[j])
                            ctx.op("vector", lambda e, pq=pq, nnxt=nnxt: e.tensor_copy(
                                nnxt[:], pq[:, 0:128]), inc=vch)
                            n_rdy[j] = vch.n
                        ctx.op("vector", lambda e, pq=pq, ntdst=ntdst: e.tensor_copy(
                            ntdst, pq[:, 128:256]), inc=vch)
                        nt_rdy[j] = ("v", vch.n)
                        rel[pqk] = (vch, vch.n)
                    else:
                        ctx.wait("scalar", tch, sqmm)
                        if li < 5:
                            wt(ctx, "scalar", (rel_nn if cur_is_c[j] else rel_nc)[j])
                            ctx.op("scalar", lambda e, pq=pq, nnxt=nnxt: e.copy(
                                nnxt[:], pq[:, 0:128]), inc=ach)
                            n_rdy[j] = ("a", ach.n)
                        ctx.op("scalar", lambda e, pq=pq, ntdst=ntdst: e.copy(
                            ntdst, pq[:, 128:256]), inc=ach)
                        nt_rdy[j] = ("a", ach.n)
                        rel[pqk] = (ach, ach.n)
                    cur_is_c[j] = not cur_is_c[j]
            nt_all = [(vch if t[0] == "v" else ach, t[1]) for t in nt_rdy]

            if dpost:
                dpost.pop(0)()

            xcur = 0
            x_now = [("a", v) for v in x_rdy]
            for ai, p in enumerate([64, 32, 16, 8, 4, 2, 1]):
                li = {1: 0, 2: 1, 4: 2, 8: 3, 16: 4, 32: 5, 64: 6}[p]
                xnxt = 1 - xcur
                for j in range(G):
                    px = B5 if j % 2 == 0 else B6
                    pxk = "B5" if j % 2 == 0 else "B6"
                    ntsl = nt_pow[j][:, li * 128:(li + 1) * 128]
                    wt(ctx, "tensor", nt_all[j])
                    wt(ctx, "tensor", (vch if x_now[j][0] == "v" else ach, x_now[j][1]))
                    wt(ctx, "tensor", rel[pxk])
                    ctx.op("tensor", lambda e, px=px, ntsl=ntsl, j=j, xcur=xcur: e.matmul(
                        px[:, 0:256], ntsl, x_bf[xcur][j][:], start=True, stop=True),
                        inc=tch)
                    apmm = tch.n
                    rel_np[j] = (tch, apmm)
                    rel_x[xcur][j] = (tch, apmm)
                    ctx.wait("vector", tch, apmm)
                    wt(ctx, "vector", rel_x[xnxt][j])
                    op = ALU.add if p >= 2 else ALU.subtract
                    ctx.op("vector", lambda e, px=px, j=j, xcur=xcur, xnxt=xnxt, op=op:
                           e.tensor_tensor(x_bf[xnxt][j][:], x_bf[xcur][j][:],
                                           px[:, 0:256], op), inc=vch)
                    rel[pxk] = (vch, vch.n)
                    x_now[j] = ("v", vch.n)
                xcur = xnxt
            xfin = xcur
            x_fin_rdy = [t[1] for t in x_now]

            def subD(g=g, sl=sl, tc=tc, qc=qc, kt=kt, zc=zc, ost=ost,
                     xfin=xfin, x_fin_rdy=tuple(x_fin_rdy),
                     cgl=cgl, gam=gam, gamT=gamT, egT=egT, nld=nld):
                for j in range(G):
                    vh = g * 8 + j
                    kh = vh // 2
                    ctx.wait("vector", vch, x_fin_rdy[j])
                    wt(ctx, "vector", rel_sbf[j])
                    ctx.op("vector", lambda e, vh=vh, j=j: e.tensor_copy(
                        sbf[j][:], S_all[:, vh * 128:(vh + 1) * 128]), inc=vch)
                    sbf_rdy = vch.n
                    b7s = 2 + j % 2
                    Wx = x_bf[xfin][j][:, 128:256]
                    ctx.wait("tensor", vch, x_fin_rdy[j])
                    wt(ctx, "tensor", relB7[b7s])
                    ctx.op("tensor", lambda e, Wx=Wx, b7s=b7s: e.transpose(
                        B7[:, b7s * 128:(b7s + 1) * 128], Wx, identb[:]), inc=tch)
                    wmm = tch.n
                    ctx.wait("scalar", tch, wmm)
                    wt(ctx, "scalar", rel_wfm[j])
                    ctx.op("scalar", lambda e, j=j, b7s=b7s: e.copy(
                        wfm[j][:], B7[:, b7s * 128:(b7s + 1) * 128]), inc=ach)
                    relB7[b7s] = (ach, ach.n)
                    wfm_rdy = ach.n
                    Ux = x_bf[xfin][j][:, 0:128]
                    pd = B5 if j % 2 == 0 else B6
                    pdk = "B5" if j % 2 == 0 else "B6"
                    if tc > 0:
                        ctx.wait("tensor", ach, wfm_rdy)
                        ctx.wait("tensor", vch, sbf_rdy)
                        wt(ctx, "tensor", rel[pdk])
                        ctx.op("tensor", lambda e, j=j, pd=pd: e.matmul(
                            pd[:, 0:128], wfm[j][:], sbf[j][:], start=True, stop=True),
                            inc=tch)
                        dlmm = tch.n
                        rel_wfm[j] = (tch, dlmm)
                        ctx.wait("vector", tch, dlmm)
                        wt(ctx, "vector", rel_dl[j])
                        ctx.op("vector", lambda e, Ux=Ux, j=j, pd=pd: e.tensor_tensor(
                            dl_bf[j][:], Ux, pd[:, 0:128], ALU.add), inc=vch)
                        rel[pdk] = (vch, vch.n)
                    else:
                        rel_wfm[j] = (ach, wfm_rdy)
                        ctx.wait("vector", vch, x_fin_rdy[j])
                        wt(ctx, "vector", rel_dl[j])
                        ctx.op("vector", lambda e, Ux=Ux, j=j: e.tensor_copy(
                            dl_bf[j][:], Ux), inc=vch)
                    rel_x[xfin][j] = (vch, vch.n)
                    dl_rdy = vch.n
                    ctx.wait("scalar", vch, dl_rdy)
                    wt(ctx, "scalar", rel_dls[j])
                    ctx.op("scalar", lambda e, vh=vh, egT=egT, j=j: e.activation(
                        dls_bf[j][:], dl_bf[j][:], AF.Copy, scale=egT[:, vh:vh + 1]),
                        inc=ach)
                    dls_rdy = ach.n
                    po = B5 if j % 2 == 0 else B6
                    pok = "B5" if j % 2 == 0 else "B6"
                    ps_ = B6 if j % 2 == 0 else B5
                    psk = "B6" if j % 2 == 0 else "B5"
                    wt(ctx, "tensor", rel[pok])
                    if tc > 0:
                        ctx.op("tensor", lambda e, qc=qc, kh=kh, j=j, po=po: e.matmul(
                            po[:, 0:128], qc[:, kh * 128:(kh + 1) * 128], sbf[j][:],
                            start=True, stop=True))
                    ctx.wait("tensor", vch, dl_rdy)
                    j16 = vh % 16
                    ctx.op("tensor", lambda e, j16=j16, j=j, po=po: e.matmul(
                        po[:, 128:256], aq_bf[j16][:], dl_bf[j][:], start=True, stop=True),
                        inc=tch)
                    omm = tch.n
                    rel_aq[j16] = (tch, omm)
                    rel_dl[j] = (tch, omm)
                    rel_sbf[j] = (tch, omm)
                    wt(ctx, "tensor", rel["ecc"])
                    ctx.op("tensor", lambda e, vh=vh, gam=gam, j=j: e.matmul(
                        B0[:, 384 + j:385 + j], onehotf[:, vh * 128:(vh + 1) * 128],
                        gam[:, 127:128], start=True, stop=True), inc=tch)
                    ecmm = tch.n
                    ctx.wait("vector", tch, ecmm)
                    ctx.op("vector", lambda e, j=j: e.tensor_copy(
                        ecc[:, j:j + 1], B0[:, 384 + j:385 + j]), inc=vch)
                    ec_rdy = vch.n
                    rel["B0"] = (vch, ec_rdy)
                    ctx.wait("tensor", ach, dls_rdy)
                    wt(ctx, "tensor", rel[psk])
                    ctx.op("tensor", lambda e, kt=kt, kh=kh, j=j, ps_=ps_: e.matmul(
                        ps_[:, 0:128], kt[:, kh * 128:(kh + 1) * 128], dls_bf[j][:],
                        start=True, stop=True), inc=tch)
                    rel_dls[j] = (tch, tch.n)
                    psmm = tch.n
                    ctx.wait("vector", tch, psmm)
                    ctx.wait("vector", vch, ec_rdy)
                    ctx.op("vector", lambda e, vh=vh, j=j, ps_=ps_: e.scalar_tensor_tensor(
                        S_all[:, vh * 128:(vh + 1) * 128],
                        S_all[:, vh * 128:(vh + 1) * 128],
                        ecc[:, j:j + 1], ps_[:, 0:128], ALU.mult, ALU.add), inc=vch)
                    rel[psk] = (vch, vch.n)
                    rel["ecc"] = (vch, vch.n)
                    t1s = j % 2
                    ctx.wait("vector", tch, omm)
                    wt(ctx, "vector", rel_t1[t1s])
                    wt(ctx, "vector", rel_otm[j])
                    if tc > 0:
                        ctx.op("vector", lambda e, vh=vh, gamT=gamT, t1s=t1s, po=po:
                               e.tensor_scalar(t1[t1s][:], po[:, 0:128],
                                               gamT[:, vh:vh + 1], None, ALU.mult), inc=vch)
                        ctx.wait("vector", vch, vch.n)
                        ctx.op("vector", lambda e, j=j, t1s=t1s, po=po: e.tensor_tensor(
                            otm[j][:], t1[t1s][:], po[:, 128:256], ALU.add), inc=vch)
                    else:
                        ctx.op("vector", lambda e, j=j, po=po: e.tensor_copy(
                            otm[j][:], po[:, 128:256]), inc=vch)
                    rel[pok] = (vch, vch.n)
                    rel_t1[t1s] = (vch, vch.n)
                    otm_rdy = vch.n
                    b7o = 4 + j % 2
                    ctx.wait("tensor", vch, otm_rdy)
                    wt(ctx, "tensor", relB7[b7o])
                    ctx.op("tensor", lambda e, j=j, b7o=b7o: e.transpose(
                        B7[:, b7o * 128:(b7o + 1) * 128], otm[j][:], identb[:]), inc=tch)
                    otmm = tch.n
                    rel_otm[j] = (tch, otmm)
                    zss = j % 2
                    ctx.wait("scalar", lds[sl][4], nld)
                    wt(ctx, "scalar", rel_zs[zss])
                    ctx.op("scalar", lambda e, zc=zc, vh=vh, zss=zss: e.activation(
                        zs[zss][:], zc[:, vh * 128:(vh + 1) * 128], AF.Silu), inc=ach)
                    zs_rdy = ach.n
                    ctx.wait("vector", tch, otmm)
                    ctx.wait("vector", ach, zs_rdy)
                    wt(ctx, "vector", rel_t2[j])
                    ctx.op("vector", lambda e, j=j, b7o=b7o, zss=zss: e.tensor_tensor(
                        t2[j][:], B7[:, b7o * 128:(b7o + 1) * 128], zs[zss][:],
                        ALU.mult), inc=vch)
                    relB7[b7o] = (vch, vch.n)
                    rel_zs[zss] = (vch, vch.n)
                    t2_rdy = vch.n
                    sq2s = j % 2
                    ctx.wait("scalar", vch, t2_rdy)
                    wt(ctx, "scalar", rel_sq2[sq2s])
                    ctx.op("scalar", lambda e, j=j, sq2s=sq2s: e.activation(
                        sq2[sq2s][:], t2[j][:], AF.Square), inc=ach)
                    sq2_rdy = ach.n
                    prk = "B5" if j % 2 == 0 else "B6"
                    prb = B5 if j % 2 == 0 else B6
                    ctx.wait("tensor", ach, sq2_rdy)
                    wt(ctx, "tensor", rel_rr[j % 2])
                    wt(ctx, "tensor", rel[prk])
                    ctx.op("tensor", lambda e, sq2s=sq2s, prb=prb: e.matmul(
                        prb[0:1, 384:512], onesb[:, 0:1], sq2[sq2s][:],
                        start=True, stop=True), inc=tch)
                    prmm = tch.n
                    rel_sq2[sq2s] = (tch, prmm)
                    ctx.wait("scalar", tch, prmm)
                    ctx.op("scalar", lambda e, j=j, prb=prb: e.activation(
                        rrow[j % 2][:], prb[0:1, 384:512], AF.Sqrt,
                        scale=cv(ctx, 1.0 / 128, 1), bias=cv(ctx, 1e-6, 1)), inc=ach)
                    rel[prk] = (ach, ach.n)
                    ctx.wait("vector", ach, ach.n)
                    ctx.op("vector", lambda e, j=j: e.reciprocal(
                        rrow[j % 2][:], rrow[j % 2][:]), inc=vch)
                    rr_rdy = vch.n
                    brk = "B6" if j % 2 == 0 else "B5"
                    brb = B6 if j % 2 == 0 else B5
                    ctx.wait("tensor", vch, rr_rdy)
                    wt(ctx, "tensor", rel[brk])
                    ctx.op("tensor", lambda e, j=j, brb=brb: e.matmul(
                        brb[:, 256:384], onesf[0:1, :], rrow[j % 2][:],
                        start=True, stop=True), inc=tch)
                    rbmm = tch.n
                    rel_rr[j % 2] = (tch, rbmm)
                    ctx.wait("vector", tch, rbmm)
                    ctx.op("vector", lambda e, vh=vh, j=j, ost=ost, brb=brb:
                           e.scalar_tensor_tensor(
                               ost[:, vh * 128:(vh + 1) * 128], t2[j][:], normw[:],
                               brb[:, 256:384], ALU.mult, ALU.mult), inc=vch)
                    rel[brk] = (vch, vch.n)
                    rel_t2[j] = (vch, vch.n)
            dpost.append(subD)
        for fn in dpost:
            fn()
        dpost = []
        ctx.wait("sync", vch, vch.n)

        def st_o(e, sl=sl, tc=tc):
            dst = o_norm.rearrange("(v p) t -> p v t", p=128)[:, :, tc * 128:(tc + 1) * 128]
            return e.dma_start(out=dst, in_=ostage[sl][:].rearrange("p (v t) -> p v t", v=32))
        ctx.op("sync", st_o, inc=ost_sem, k=16)
        rel_ost[sl] = (ost_sem, ost_sem.n)
        chunk_done = [(vch, vch.n), (ach, ach.n), (tch, tch.n)]
        rel_in[sl] = chunk_done + [(ost_sem, ost_sem.n)]
        rel["rows"] = chunk_done
        rel["rowsT"] = chunk_done
    return dict(ost=ost_sem, done=[(ost_sem, ost_sem.n)])


import numpy as _np


def _consts_decl(ctx, names):
    spec = {
        "ident": ((128, 128), BF16), "identf": ((128, 128), F32),
        "onesf": ((128, 128), F32), "onesb": ((128, 128), BF16),
        "onehotb": ((32, 4096), BF16), "onehotf": ((32, 4096), F32),
        "colones": ((128, 1024), BF16), "cw": ((128, 256), F32),
        "cb": ((128, 64), F32), "alog": ((32, 1), F32), "dtb": ((32, 1), F32),
        "maskS": ((128, 128), F32), "maskIT": ((128, 128), F32),
        "normw": ((128, 1), F32), "b1": ((128, 64), F32), "b2": ((128, 16), F32),
    }
    cns = {}
    for nm in names:
        sh, dt = spec[nm]
        cns[nm] = (ctx.dram("d_" + nm, list(sh), dt, kind="ExternalInput"), sh, dt)
    if "wba" in names:
        pass
    return cns


_CONST_SRC = {
    "ident": "ident", "identf": "identf", "onesf": "onesf", "onesb": "onesb",
    "onehotb": "onehotb", "onehotf": "onehot", "colones": "colones",
    "cw": "cw", "cb": "cb", "alog": "alog", "dtb": "dtb",
    "maskS": "mask_strict", "maskIT": "mask_inclT", "normw": "normw",
    "b1": "b1", "b2": "b2",
}


def _const_inputs(hp, names):
    return {"d_" + nm: _np.ascontiguousarray(hp[_CONST_SRC[nm]]) for nm in names}


def _build_p0():
    ctx = Ctx("p0")
    w1 = ctx.dram("w1in", [8192, 2048], F32, kind="ExternalInput")
    w2 = ctx.dram("w2in", [8192, 2048], F32, kind="ExternalInput")
    w1q = ctx.dram("w1q", [8192, 2048], BF16, kind="ExternalOutput")
    w2q = ctx.dram("w2q", [8192, 2048], BF16, kind="ExternalOutput")
    cns = _consts_decl(ctx, ["onesf"])
    load_consts(ctx, cns)
    s1 = emit_p0(ctx, w1, w1q, 64, "wa")
    s2 = emit_p0(ctx, w2, w2q, 64, "wb")
    ctx.wait("sync", s1[0], 16 * 32)
    ctx.wait("sync", s1[1], 16 * 32)
    ctx.wait("sync", s2[0], 16 * 32)
    ctx.wait("sync", s2[1], 16 * 32)
    return ctx.finalize()


def _build_p1():
    ctx = Ctx("p1")
    x = ctx.dram("x", [2048, 2048], F32, kind="ExternalInput")
    wq = ctx.dram("wq", [96 * 16 * 128, 128], F32, kind="ExternalInput")
    cns = _consts_decl(ctx, ["ident", "onesf", "onehotb", "colones", "cw", "cb",
                             "alog", "dtb"])
    cns["wba"] = (ctx.dram("d_wba", [128, 1024], F32, kind="ExternalInput"),
                  (128, 1024), BF16, "cast")
    x_fm = ctx.dram("x_fm", [16 * 128, 2048], BF16, kind="ExternalOutput")
    q_fm = ctx.dram("q_fm", [16 * 128, 2048], BF16, kind="ExternalOutput")
    k_fm = ctx.dram("k_fm", [16 * 128, 2048], BF16, kind="ExternalOutput")
    k_tm = ctx.dram("k_tm", [16 * 16 * 128, 128], BF16, kind="ExternalOutput")
    v_tm = ctx.dram("v_tm", [16 * 32 * 128, 128], BF16, kind="ExternalOutput")
    z_fm = ctx.dram("z_fm", [32 * 128, 2048], BF16, kind="ExternalOutput")
    beta = ctx.dram("beta", [32, 2048], F32, kind="ExternalOutput")
    cumg = ctx.dram("cumg", [32, 2048], F32, kind="ExternalOutput")
    qc_scr = ctx.dram("qc_scr", [32 * 128, 2048], BF16, kind="Internal")
    load_consts(ctx, cns)
    prepare_cvals(ctx, [1e-5, 1e-6, 1.0, -1.0, 0.5, 1.0 / 128])
    h = emit_p1(ctx, x, wq, None, x_fm, q_fm, k_fm, k_tm, v_tm, z_fm,
                beta, cumg, qc_scr)
    d = emit_p1b(ctx, h, wq, q_fm, k_fm, k_tm, v_tm, z_fm, qc_scr)
    wt(ctx, "sync", d["done"])
    return ctx.finalize()


def _build_p2():
    ctx = Ctx("p2")
    q_fm = ctx.dram("q_fm", [16 * 128, 2048], BF16, kind="ExternalInput")
    k_fm = ctx.dram("k_fm", [16 * 128, 2048], BF16, kind="ExternalInput")
    k_tm = ctx.dram("k_tm", [16 * 16 * 128, 128], BF16, kind="ExternalInput")
    v_tm = ctx.dram("v_tm", [16 * 32 * 128, 128], BF16, kind="ExternalInput")
    z_fm = ctx.dram("z_fm", [32 * 128, 2048], BF16, kind="ExternalInput")
    beta = ctx.dram("beta", [32, 2048], F32, kind="ExternalInput")
    cumg = ctx.dram("cumg", [32, 2048], F32, kind="ExternalInput")
    o_norm = ctx.dram("o_norm", [32 * 128, 2048], BF16, kind="ExternalOutput")
    cns = _consts_decl(ctx, ["ident", "identf", "onesf", "onesb", "onehotf",
                             "maskS", "maskIT", "normw"])
    load_consts(ctx, cns)
    prepare_cvals(ctx, [1e-5, 1e-6, 1.0, -1.0, 0.5, 1.0 / 128])
    d = emit_p2(ctx, q_fm, k_fm, k_tm, v_tm, z_fm, beta, cumg, o_norm)
    wt(ctx, "sync", d["done"])
    return ctx.finalize()


def _build_p2b():
    ctx = Ctx("p2b")
    o_norm = ctx.dram("o_norm", [32 * 128, 2048], BF16, kind="ExternalInput")
    x_fm = ctx.dram("x_fm", [16 * 128, 2048], BF16, kind="ExternalInput")
    w_o = ctx.dram("w_o", [16 * 32 * 128, 128], F32, kind="ExternalInput")
    x2_fm = ctx.dram("x2_fm", [16 * 128, 2048], BF16, kind="ExternalOutput")
    xn2_fm = ctx.dram("xn2_fm", [16 * 128, 2048], BF16, kind="ExternalOutput")
    cns = _consts_decl(ctx, ["onesb", "onesf"])
    load_consts(ctx, cns)
    prepare_cvals(ctx, [1e-5, 1e-6, 1.0, -1.0, 0.5, 1.0 / 128])
    d = emit_p2b(ctx, o_norm, x_fm, w_o, x2_fm, xn2_fm)
    wt(ctx, "sync", d["done"])
    return ctx.finalize()


def _build_p3():
    ctx = Ctx("p3")
    xn2_fm = ctx.dram("xn2_fm", [16 * 128, 2048], BF16, kind="ExternalInput")
    x2_fm = ctx.dram("x2_fm", [16 * 128, 2048], BF16, kind="ExternalInput")
    w1q = ctx.dram("w1q", [8192, 2048], BF16, kind="ExternalInput")
    w2q = ctx.dram("w2q", [8192, 2048], BF16, kind="ExternalInput")
    out = ctx.dram("out", [2048, 2048], F32, kind="ExternalOutput")
    cns = _consts_decl(ctx, ["onesb", "onesf", "identf", "b1", "b2"])
    load_consts(ctx, cns)
    prepare_cvals(ctx, [1e-5, 1e-6, 1.0, -1.0, 0.5, 1.0 / 128])
    d = emit_p3(ctx, xn2_fm, x2_fm, w1q, w2q, out)
    wt(ctx, "sync", d["done"])
    return ctx.finalize()


def kernel_multi(**inputs):
    from concourse.bass_utils import run_bass_kernel_spmd
    N = 8
    hp = host_prep(inputs)
    x = _np.ascontiguousarray(_np.asarray(inputs["L0_attn_x"], dtype=_np.float32))

    nc = _build_p0()
    im = {"w1in": _np.ascontiguousarray(hp["w1t"].reshape(8192, 2048)),
          "w2in": _np.ascontiguousarray(hp["w2t"].reshape(8192, 2048)),
          **_const_inputs(hp, ["onesf"])}
    r0 = run_bass_kernel_spmd(nc, [im] * N, core_ids=list(range(N))).results

    nc = _build_p1()
    base = {"wq": _np.ascontiguousarray(hp["wq"].reshape(96 * 16 * 128, 128)),
            "d_wba": hp["wba"],
            **_const_inputs(hp, ["ident", "onesf", "onehotb", "colones", "cw",
                                 "cb", "alog", "dtb"])}
    ims = [dict(base, x=_np.ascontiguousarray(x[i])) for i in range(N)]
    r1 = run_bass_kernel_spmd(nc, ims, core_ids=list(range(N))).results

    nc = _build_p2()
    cns2 = _const_inputs(hp, ["ident", "identf", "onesf", "onesb", "onehotf",
                              "maskS", "maskIT", "normw"])
    ims = [dict(cns2, q_fm=r1[i]["q_fm"], k_fm=r1[i]["k_fm"], k_tm=r1[i]["k_tm"],
                v_tm=r1[i]["v_tm"], z_fm=r1[i]["z_fm"], beta=r1[i]["beta"],
                cumg=r1[i]["cumg"]) for i in range(N)]
    r2 = run_bass_kernel_spmd(nc, ims, core_ids=list(range(N))).results

    nc = _build_p2b()
    cns3 = _const_inputs(hp, ["onesb", "onesf"])
    wof = _np.ascontiguousarray(hp["wo"].reshape(16 * 32 * 128, 128))
    ims = [dict(cns3, o_norm=r2[i]["o_norm"], x_fm=r1[i]["x_fm"], w_o=wof)
           for i in range(N)]
    r3 = run_bass_kernel_spmd(nc, ims, core_ids=list(range(N))).results

    nc = _build_p3()
    cns4 = _const_inputs(hp, ["onesb", "onesf", "identf", "b1", "b2"])
    ims = [dict(cns4, xn2_fm=r3[i]["xn2_fm"], x2_fm=r3[i]["x2_fm"],
                w1q=r0[i]["w1q"], w2q=r0[i]["w2q"]) for i in range(N)]
    r4 = run_bass_kernel_spmd(nc, ims, core_ids=list(range(N))).results

    out = _np.stack([_np.asarray(r4[i]["out"], dtype=_np.float32) for i in range(N)])
    return out



def _build_fused():
    ctx = Ctx("mk")
    x = ctx.dram("x", [2048, 2048], F32, kind="ExternalInput")
    wq = ctx.dram("wq", [96 * 16 * 128, 128], F32, kind="ExternalInput")
    w_o = ctx.dram("w_o", [16 * 32 * 128, 128], F32, kind="ExternalInput")
    w1 = ctx.dram("w1in", [8192, 2048], F32, kind="ExternalInput")
    w2 = ctx.dram("w2in", [8192, 2048], F32, kind="ExternalInput")
    out = ctx.dram("out", [2048, 2048], F32, kind="ExternalOutput")
    w1q = ctx.dram("w1q", [8192, 2048], BF16, kind="Internal")
    w2q = ctx.dram("w2q", [8192, 2048], BF16, kind="Internal")
    x_fm = ctx.dram("x_fm", [16 * 128, 2048], BF16, kind="Internal")
    q_fm = ctx.dram("q_fm", [16 * 128, 2048], BF16, kind="Internal")
    k_fm = ctx.dram("k_fm", [16 * 128, 2048], BF16, kind="Internal")
    k_tm = ctx.dram("k_tm", [16 * 16 * 128, 128], BF16, kind="Internal")
    v_tm = ctx.dram("v_tm", [16 * 32 * 128, 128], BF16, kind="Internal")
    z_fm = ctx.dram("z_fm", [32 * 128, 2048], BF16, kind="Internal")
    beta = ctx.dram("beta", [32, 2048], F32, kind="Internal")
    cumg = ctx.dram("cumg", [32, 2048], F32, kind="Internal")
    qc_scr = ctx.dram("qc_scr", [32 * 128, 2048], BF16, kind="Internal")
    o_norm = ctx.dram("o_norm", [32 * 128, 2048], BF16, kind="Internal")
    x2_fm = ctx.dram("x2_fm", [16 * 128, 2048], BF16, kind="Internal")
    xn2_fm = ctx.dram("xn2_fm", [16 * 128, 2048], BF16, kind="Internal")
    gcns = _consts_decl(ctx, ["ident", "identf", "onesf", "onesb", "b1", "b2"])
    p1cns = _consts_decl(ctx, ["onehotb", "colones", "cw", "cb", "alog", "dtb"])
    p1cns["wba"] = (ctx.dram("d_wba", [128, 1024], F32, kind="ExternalInput"),
                    (128, 1024), BF16, "cast")
    p2cns = _consts_decl(ctx, ["onehotf", "maskS", "maskIT", "normw"])
    load_consts(ctx, gcns)
    gconsts = dict(ctx.consts)
    prepare_cvals(ctx, [1e-5, 1e-6, 1.0, -1.0, 0.5, 1.0 / 128])
    ctx.begin_phase()
    emit_p0(ctx, w1, w1q, 64, "wa")
    emit_p0(ctx, w2, w2q, 64, "wb")
    ctx.end_phase()
    ctx.begin_phase()
    load_consts(ctx, p1cns)
    ctx.consts.update(gconsts)
    h = emit_p1(ctx, x, wq, None, x_fm, q_fm, k_fm, k_tm, v_tm, z_fm,
                beta, cumg, qc_scr)
    emit_p1b(ctx, h, wq, q_fm, k_fm, k_tm, v_tm, z_fm, qc_scr)
    ctx.end_phase()
    ctx.begin_phase()
    load_consts(ctx, p2cns)
    ctx.consts.update(gconsts)
    emit_p2(ctx, q_fm, k_fm, k_tm, v_tm, z_fm, beta, cumg, o_norm)
    ctx.end_phase()
    ctx.begin_phase()
    ctx.consts = dict(gconsts)
    emit_p2b(ctx, o_norm, x_fm, w_o, x2_fm, xn2_fm)
    ctx.end_phase()
    ctx.begin_phase()
    d = emit_p3(ctx, xn2_fm, x2_fm, w1q, w2q, out)
    ctx.end_phase()
    wt(ctx, "sync", d["done"])
    return ctx.finalize()


def kernel(**inputs):
    from concourse.bass_utils import run_bass_kernel_spmd
    N = 8
    hp = host_prep(inputs)
    x = _np.ascontiguousarray(_np.asarray(inputs["L0_attn_x"], dtype=_np.float32))
    nc = _build_fused()
    base = {"wq": _np.ascontiguousarray(hp["wq"].reshape(96 * 16 * 128, 128)),
            "w_o": _np.ascontiguousarray(hp["wo"].reshape(16 * 32 * 128, 128)),
            "w1in": _np.ascontiguousarray(hp["w1t"].reshape(8192, 2048)),
            "w2in": _np.ascontiguousarray(hp["w2t"].reshape(8192, 2048)),
            "d_wba": hp["wba"],
            **_const_inputs(hp, ["ident", "identf", "onesf", "onesb", "onehotb",
                                 "onehotf", "colones", "cw", "cb", "alog", "dtb",
                                 "maskS", "maskIT", "normw", "b1", "b2"])}
    ims = [dict(base, x=_np.ascontiguousarray(x[i])) for i in range(N)]
    r = run_bass_kernel_spmd(nc, ims, core_ids=list(range(N))).results
    return _np.stack([_np.asarray(r[i]["out"], dtype=_np.float32) for i in range(N)])
